# version 01
# baseline (speedup 1.0000x reference)
import sys
sys.path.insert(0, '/opt/trn_rl_repo')
import math
import numpy as np
import ml_dtypes
from contextlib import ExitStack

import concourse.bass as bass
import concourse.mybir as mybir
from concourse.bass_utils import run_bass_kernel_spmd

F32 = mybir.dt.float32
BF16 = mybir.dt.bfloat16
I32 = mybir.dt.int32
AF = mybir.ActivationFunctionType
ALU = mybir.AluOpType
AX = mybir.AxisListType

N_CORES = 8
S = 4096
D = 1024
NT = S // 128
NF = D // 128

BF = ml_dtypes.bfloat16



class _Op:
    __slots__ = ("eng", "fn", "is_dma", "idx", "need_inc", "inc_no", "deps", "tag",
                 "sem_key", "sem_val", "slot_prior")
    def __init__(self, eng, fn, is_dma, idx, tag):
        self.eng = eng; self.fn = fn; self.is_dma = is_dma
        self.idx = idx; self.need_inc = is_dma; self.inc_no = None
        self.deps = []
        self.tag = tag
        self.sem_key = None; self.sem_val = None; self.slot_prior = 0


def _ap_box(ap):
    if not hasattr(ap, "tensor"):
        ap = ap[:]
    t = ap.tensor
    name = t.name
    sp = str(ap.space)
    if sp == "MemorySpace.PSUM":
        return (name, 0, 128, 0, 1 << 30)
    es = mybir.dt.size(ap.dtype)
    shape = ap.shape
    apl = ap.ap
    if sp == "MemorySpace.SBUF":
        row = apl[0][0]
        p0 = (ap.offset // row) if row else 0
        p1 = p0 + shape[0]
        off = ap.offset - p0 * row
        span = 1
        for (st, cnt) in apl[1:]:
            span += abs(st) * (cnt - 1)
        return (name, p0, p1, off * es, (off + span) * es)
    else:
        off = ap.offset
        span = 1
        for (st, cnt) in apl:
            span += abs(st) * (cnt - 1)
        return (name, 0, 1, off * es, (off + span) * es)


def _overlap(a, b):
    return (a[0] == b[0] and a[1] < b[2] and b[1] < a[2]
            and a[3] < b[4] and b[3] < a[4])


def _covers(a, b):
    return a[0] == b[0] and a[1] <= b[1] and a[2] >= b[2] and a[3] <= b[3] and a[4] >= b[4]


class Prog:

    ENGINES = ("pe", "dve", "act", "pool", "sp")

    def __init__(self, name="prog", serial=False):
        self.nc = bass.Bass("TRN2", target_bir_lowering=False, name=name)
        self.serial = serial
        self.ops = {e: [] for e in self.ENGINES}
        self.order = []
        self.writers = {}
        self.readers = {}
        self.ctx = ExitStack()
        self._tensors = {}
        self._finished = False
        self._nin = {}
        self._ninputs = []
        self._constvals = {}
        self._const_t = None

    def sbuf(self, name, shape, dtype):
        t = self.ctx.enter_context(self.nc.sbuf_tensor(name, list(shape), dtype))
        self._tensors[name] = t
        return t

    def psum(self, name, shape, dtype=F32):
        t = self.ctx.enter_context(self.nc.psum_tensor(name, list(shape), dtype))
        self._tensors[name] = t
        return t

    def dram_in(self, name, shape, dtype=F32):
        ap = self.nc.dram_tensor(name, list(shape), dtype, kind="ExternalInput").ap()
        self._ninputs.append(name)
        return ap

    def dram_out(self, name, shape, dtype=F32):
        return self.nc.dram_tensor(name, list(shape), dtype, kind="ExternalOutput").ap()

    def dram_tmp(self, name, shape, dtype=F32):
        return self.nc.dram_tensor(name, list(shape), dtype, kind="Internal").ap()

    def constant(self, val):
        val = float(val)
        if self._const_t is None:
            self._const_t = self.sbuf("_consts_sb", [128, 32], F32)
            self._const_d = self.dram_in("_consts", (128, 32), F32)
            self.dma("sp", self._const_t[:], self._const_d, tag="ld_consts")
        if val not in self._constvals:
            j = len(self._constvals)
            assert j < 32
            self._constvals[val] = j
        j = self._constvals[val]
        return self._const_t[:, j:j + 1]

    def const_array(self):
        a = np.zeros((128, 32), np.float32)
        for v, j in self._constvals.items():
            a[:, j] = v
        return a

    def op(self, eng, fn, reads, writes, tag=""):
        assert not self._finished
        o = _Op(eng, fn, False, len(self.ops[eng]), tag)
        self._record(o, reads, writes)
        return o

    def dma(self, queue, out_ap, in_ap, tag="", **kw):
        assert queue in ("sp", "act", "pool")
        fn = lambda e: e.dma_start(out=out_ap, in_=in_ap, **kw)
        o = _Op(queue, fn, True, len(self.ops[queue]), tag)
        self._record(o, [in_ap], [out_ap])
        return o

    def _record(self, o, reads, writes):
        rboxes = [_ap_box(a) for a in reads]
        wboxes = [_ap_box(a) for a in writes]
        deps = []
        if self.serial and self.order:
            deps.append(self.order[-1])
        for b in rboxes:
            for (wb, wop) in self.writers.get(b[0], ()):
                if _overlap(b, wb):
                    deps.append(wop)
        for b in wboxes:
            for (wb, wop) in self.writers.get(b[0], ()):
                if _overlap(b, wb):
                    deps.append(wop)
            for (rb, rop) in self.readers.get(b[0], ()):
                if _overlap(b, rb):
                    deps.append(rop)
        seen = set()
        for d in deps:
            if id(d) in seen or d is o:
                continue
            seen.add(id(d))
            o.deps.append(d)
            d.need_inc = True
        for b in wboxes:
            wl = self.writers.setdefault(b[0], [])
            wl[:] = [(wb, wop) for (wb, wop) in wl if not _covers(b, wb)]
            wl.append((b, o))
            rl = self.readers.setdefault(b[0], [])
            rl[:] = [(rb, rop) for (rb, rop) in rl if not _covers(b, rb)]
        for b in rboxes:
            self.readers.setdefault(b[0], []).append((b, o))
        self.ops[o.eng].append(o)
        self.order.append(o)

    DMA_SLOTS = 8

    def finish(self):
        assert not self._finished
        self._finished = True
        nc = self.nc
        W = self.DMA_SLOTS
        for e in self.ENGINES:
            if self.ops[e]:
                self.ops[e][-1].need_inc = True
        inc_totals = {}
        dma_idx = {e: 0 for e in self.ENGINES}
        slot_uses = {}
        for e in self.ENGINES:
            cum = 0
            for o in self.ops[e]:
                if o.is_dma:
                    s = dma_idx[e] % W
                    dma_idx[e] += 1
                    key = (e, s)
                    prior = slot_uses.get(key, 0)
                    o.sem_key = key
                    o.sem_val = 16 * (prior + 1)
                    o.slot_prior = 16 * prior
                    slot_uses[key] = prior + 1
                else:
                    if o.need_inc:
                        cum += 1
                        o.sem_key = e
                        o.sem_val = cum
            inc_totals[e] = cum
        sems = {}
        for e in self.ENGINES:
            sems[e] = self.ctx.enter_context(nc.semaphore(f"sem_{e}"))
        for (q, s) in slot_uses:
            sems[(q, s)] = self.ctx.enter_context(nc.semaphore(f"dsem_{q}{s}"))
        block = self.ctx.enter_context(nc.Block())

        def emit(e, engobj):
            hw = {}
            def wait(key, val):
                if val > hw.get(key, 0):
                    engobj.wait_ge(sems[key], val)
                    hw[key] = val
            for o in self.ops[e]:
                for d in o.deps:
                    assert getattr(d, "sem_key", None) is not None, (e, o.tag, d.eng, d.tag)
                    wait(d.sem_key, d.sem_val)
                if o.is_dma and o.slot_prior > 0:
                    wait(o.sem_key, o.slot_prior)
                inst = o.fn(engobj)
                if o.is_dma:
                    inst.then_inc(sems[o.sem_key], 16)
                elif o.need_inc:
                    inst.then_inc(sems[e], 1)

        @block.tensor
        def _(e):
            emit("pe", e)

        @block.vector
        def _(e):
            emit("dve", e)

        @block.scalar
        def _(e):
            emit("act", e)

        @block.sync
        def _(e):
            emit("sp", e)

        @block.gpsimd
        def _(e):
            emit("pool", e)
            for p in self.ENGINES:
                if p != "pool" and inc_totals[p]:
                    e.wait_ge(sems[p], inc_totals[p])
            if inc_totals["pool"]:
                e.wait_ge(sems["pool"], inc_totals["pool"])
            for key, uses in slot_uses.items():
                e.wait_ge(sems[key], 16 * uses)

        self.ctx.close()
        return nc



def vop(P, fn, reads, writes, tag=""):
    return P.op("dve", fn, reads, writes, tag)

def sop(P, fn, reads, writes, tag=""):
    return P.op("act", fn, reads, writes, tag)

def gop(P, fn, reads, writes, tag=""):
    return P.op("pool", fn, reads, writes, tag)

def mm(P, out, lhsT, rhs, start=True, stop=True, tag=""):
    return P.op("pe", lambda e: e.matmul(out=out, lhsT=lhsT, rhs=rhs,
                                         start=start, stop=stop),
                [lhsT, rhs], [out], tag)

def tr(P, out, in_, ident, tag=""):
    return P.op("pe", lambda e: e.transpose(out=out, in_=in_, identity=ident),
                [in_, ident], [out], tag)



EPS_LN = 1e-5


def _rope_tables():
    half = 32
    inv_freq = 10000.0 ** (-(np.arange(half, dtype=np.float64) * 2.0 / 64.0))
    t = np.arange(S, dtype=np.float64)
    ang = t[:, None] * inv_freq[None, :]
    cos = np.cos(ang); sin = np.sin(ang)
    CC = np.empty((128, S), np.float32)
    SS = np.empty((128, S), np.float32)
    for p in range(128):
        i = (p % 64) // 2
        CC[p] = cos[:, i]
        SS[p] = (-sin[:, i]) if (p % 2 == 0) else sin[:, i]
    return CC.astype(BF), SS.astype(BF)


def _perm_swap16():
    M = np.zeros((128, 128), np.float32)
    for i in range(64):
        M[2 * i, 2 * i + 1] = 1.0
        M[2 * i + 1, 2 * i] = 1.0
    return M.astype(BF)


def _ident16():
    return np.eye(128, dtype=np.float32).astype(BF)


def _tri16():
    return np.tril(np.ones((128, 128), np.float32)).astype(BF)


def alloc_common(P, notm=2, psums=True):
    c = {}
    c["ident16"] = P.sbuf("c_ident16", [128, 128], BF16)
    c["stats"] = P.sbuf("c_stats", [128, 32], F32)
    c["xtm"] = [P.sbuf(f"c_xtm{i}", [128, 1024], F32) for i in range(2)]
    c["otm"] = [P.sbuf(f"c_otm{i}", [128, 1024], F32) for i in range(max(1, notm))]
    if notm == 1:
        c["otm"] = [c["otm"][0], c["otm"][0]]
    c["sq"] = c["otm"][0]
    c["lntile"] = [P.sbuf(f"c_lnt{i}", [128, 1024], BF16) for i in range(2)]
    if not psums:
        return c
    c["pA"] = P.psum("p_A", [128, 1024], F32)
    c["pB"] = P.psum("p_B", [128, 1024], F32)
    c["pC"] = P.psum("p_C", [128, 512], F32)
    c["pT0"] = P.psum("p_T0", [128, 1024], BF16)
    c["pT1"] = P.psum("p_T1", [128, 1024], BF16)
    c["pD"] = P.psum("p_D", [128, 512], F32)
    return c


def load_const(P, c, G, names):
    for nm, dst in names:
        P.dma("sp", dst[:], G(nm), tag=f"const_{nm}")
        if dst is c.get("ident16"):
            c["_ident_loaded"] = True


def ensure_ident(P, c, G, name):
    if not c.get("_ident_loaded"):
        P.dma("sp", c["ident16"][:], G(name), tag="const_ident")
        c["_ident_loaded"] = True


def emit_ln(P, c, x_hbm, big, tt_range=None, out_off=0):
    st = c["stats"]
    for tt in (tt_range or range(NT)):
        xa = c["xtm"][tt % 2]
        lnt = c["lntile"][tt % 2]
        P.dma("sp", xa[:], x_hbm[tt * 128:(tt + 1) * 128, :], tag=f"ln_ld{tt}")
        sop(P, lambda e, xa=xa: e.activation(out=c["sq"][:], in_=xa[:], func=AF.Square,
                                             accum_out=st[:, 1:2]),
            [xa[:]], [c["sq"][:], st[:, 1:2]], tag="ln_sq")
        vop(P, lambda e, xa=xa: e.reduce_sum(out=st[:, 0:1], in_=xa[:], axis=AX.X),
            [xa[:]], [st[:, 0:1]], tag="ln_sum")
        vop(P, lambda e: e.tensor_scalar(out=st[:, 2:3], in0=st[:, 0:1], scalar1=1.0 / D,
                                         scalar2=None, op0=ALU.mult),
            [st[:, 0:1]], [st[:, 2:3]])
        vop(P, lambda e: e.tensor_scalar(out=st[:, 3:4], in0=st[:, 1:2], scalar1=1.0 / D,
                                         scalar2=None, op0=ALU.mult),
            [st[:, 1:2]], [st[:, 3:4]])
        vop(P, lambda e: e.tensor_tensor(out=st[:, 4:5], in0=st[:, 2:3], in1=st[:, 2:3],
                                         op=ALU.mult),
            [st[:, 2:3]], [st[:, 4:5]])
        vop(P, lambda e: e.tensor_sub(out=st[:, 5:6], in0=st[:, 3:4], in1=st[:, 4:5]),
            [st[:, 3:4], st[:, 4:5]], [st[:, 5:6]])
        epsap = P.constant(EPS_LN)
        sop(P, lambda e, epsap=epsap: e.activation(out=st[:, 6:7], in_=st[:, 5:6], func=AF.Sqrt,
                                      bias=epsap),
            [st[:, 5:6], epsap], [st[:, 6:7]])
        vop(P, lambda e: e.reciprocal(out=st[:, 7:8], in_=st[:, 6:7]),
            [st[:, 6:7]], [st[:, 7:8]])
        vop(P, lambda e: e.tensor_tensor(out=st[:, 8:9], in0=st[:, 2:3], in1=st[:, 7:8],
                                         op=ALU.mult),
            [st[:, 2:3], st[:, 7:8]], [st[:, 8:9]])
        vop(P, lambda e: e.tensor_scalar(out=st[:, 9:10], in0=st[:, 8:9], scalar1=-1.0,
                                         scalar2=None, op0=ALU.mult),
            [st[:, 8:9]], [st[:, 9:10]])
        vop(P, lambda e, xa=xa, lnt=lnt: e.tensor_scalar(out=lnt[:], in0=xa[:],
                scalar1=st[:, 7:8], scalar2=st[:, 9:10], op0=ALU.mult, op1=ALU.add),
            [xa[:], st[:, 7:8], st[:, 9:10]], [lnt[:]], tag="ln_norm")
        for k in range(NF):
            pb = c["pT0"] if k % 2 == 0 else c["pT1"]
            slot = (k // 2) * 128
            tr(P, pb[:, slot:slot + 128], lnt[:, k * 128:(k + 1) * 128], c["ident16"][:],
               tag=f"ln_tr{tt}_{k}")
            dst = big[:, out_off + k * S + tt * 128: out_off + k * S + (tt + 1) * 128]
            vop(P, lambda e, pb=pb, slot=slot, dst=dst: e.tensor_copy(out=dst, in_=pb[:, slot:slot + 128]),
                [pb[:, slot:slot + 128]], [dst], tag="ln_ev")



def prep_l0_attn(inputs):
    CC, SS = _rope_tables()
    return {
        "l0a_wqT": np.ascontiguousarray(inputs["L0_attn_w_q"].T),
        "l0a_wkT": np.ascontiguousarray(inputs["L0_attn_w_k"].T),
        "l0a_wvT": np.ascontiguousarray(inputs["L0_attn_w_v"].T),
        "l0a_woT": np.ascontiguousarray(inputs["L0_attn_w_o"].T),
        "l0a_cc": CC, "l0a_ss": SS,
        "l0a_perm": _perm_swap16(), "l0a_ident": _ident16(), "l0a_tri": _tri16(),
        **prep_router("l0c_", inputs["L0_ffn_router_w"], inputs["L0_ffn_router_b"]),
    }


def dram_l0_attn(P):
    g = {}
    for nm in ("l0a_wqT", "l0a_wkT", "l0a_wvT", "l0a_woT"):
        g[nm] = P.dram_in(nm, (D, D), F32)
    g["l0a_cc"] = P.dram_in("l0a_cc", (128, S), BF16)
    g["l0a_ss"] = P.dram_in("l0a_ss", (128, S), BF16)
    g["l0a_perm"] = P.dram_in("l0a_perm", (128, 128), BF16)
    g["l0a_ident"] = P.dram_in("l0a_ident", (128, 128), BF16)
    g["l0a_tri"] = P.dram_in("l0a_tri", (128, 128), BF16)
    g["l0a_qTd"] = P.dram_tmp("l0a_qTd", (16, 64, S), BF16)
    g["l0a_kTd"] = P.dram_tmp("l0a_kTd", (16, 64, S), BF16)
    g.update(dram_router(P, "l0c_"))
    g["moe_counts"] = P.dram_out("moe_counts", (1, 8), F32)
    return g


def emit_l0_attn(P, xin, xout, G):
    c = alloc_common(P)
    big = P.sbuf("l0a_big", [128, NF * S], BF16)
    vsb = P.sbuf("l0a_vsb", [128, NT * 1024], BF16)
    w16 = P.sbuf("l0a_w16", [128, 16384], BF16)
    wA = w16[:, 0:8192].rearrange("p (ki o) -> p ki o", o=1024)
    wB = w16[:, 8192:16384].rearrange("p (ki o) -> p ki o", o=1024)
    cc16 = w16[:, 8192:12288]
    ss16 = w16[:, 12288:16384]
    kbuf = w16[0:64, 8192:12288]
    qbuf = w16[0:64, 12288:16384]
    perm16 = P.sbuf("l0a_perm16", [128, 128], BF16)
    tri16 = P.sbuf("l0a_tri16", [128, 128], BF16)
    p16 = [P.sbuf(f"l0a_p16_{i}", [128, 1024], BF16) for i in range(2)]
    pm16 = P.sbuf("l0a_pm16", [128, 128], BF16)
    pt16 = P.sbuf("l0a_pt16", [128, 512], BF16)
    att16 = P.sbuf("l0a_att16", [128, 64], BF16)
    qf16 = [P.sbuf("l0a_qf16_0", [128, 512], BF16)] * 2
    tb32 = [P.sbuf("l0a_tb_0", [128, 512], F32)] * 2
    ub32 = [P.sbuf("l0a_ub_0", [128, 512], F32)] * 2
    rp16 = [P.sbuf(f"l0a_rp_{i}", [128, 512], BF16) for i in range(2)]
    dsum = P.sbuf("l0a_dsum", [128, 8], F32)

    load_const(P, c, G, [("l0a_ident", c["ident16"]), ("l0a_perm", perm16),
                         ("l0a_tri", tri16)])

    emit_ln(P, c, xin, big)

    P.dma("pool", wB, G("l0a_wvT").rearrange("(ki p) o -> p ki o", p=128), tag="ld_wv")
    for tt in range(NT):
        for n in range(2):
            ps = c["pA"][:, n * 512:(n + 1) * 512]
            for ki in range(NF):
                mm(P, ps, big[:, ki * S + tt * 128:ki * S + (tt + 1) * 128],
                   wB[:, ki, n * 512:(n + 1) * 512],
                   start=(ki == 0), stop=(ki == NF - 1), tag=f"v_mm{tt}_{n}")
            dst = vsb[:, tt * 1024 + n * 512: tt * 1024 + (n + 1) * 512]
            vop(P, lambda e, ps=ps, dst=dst: e.tensor_copy(out=dst, in_=ps),
                [ps], [dst], tag=f"v_ev{tt}_{n}")

    P.dma("sp", w16[:, 8192:12288], G("l0a_cc"), tag="ld_cc")
    P.dma("sp", w16[:, 12288:16384], G("l0a_ss"), tag="ld_ss")
    for qk, (wname, dnam) in enumerate([("l0a_wqT", "l0a_qTd"), ("l0a_wkT", "l0a_kTd")]):
        P.dma("pool", wA, G(wname).rearrange("(ki p) o -> p ki o", p=128), tag=f"ld_w{qk}")
        dstd = G(dnam)
        for oc in range(NF):
            for tb in range(8):
                r = (oc * 8 + tb) % 2
                ps = c["pB"][:, r * 512:(r + 1) * 512]
                for ki in range(NF):
                    mm(P, ps, wA[:, ki, oc * 128:(oc + 1) * 128],
                       big[:, ki * S + tb * 512:ki * S + (tb + 1) * 512],
                       start=(ki == 0), stop=(ki == NF - 1), tag=f"qk_mm{qk}_{oc}_{tb}")
                qf = qf16[r][:]; tb_ = tb32[r][:]; ub_ = ub32[r][:]; rp = rp16[r]
                vop(P, lambda e, ps=ps, qf=qf: e.tensor_copy(out=qf, in_=ps),
                    [ps], [qf], tag="rope_cast")
                swp = c["pC"][:, 0:512]
                mm(P, swp, perm16[:], qf, tag="rope_swap")
                ccs = cc16[:, tb * 512:(tb + 1) * 512]
                sss = ss16[:, tb * 512:(tb + 1) * 512]
                vop(P, lambda e, ps=ps, ccs=ccs, tb_=tb_: e.tensor_tensor(out=tb_, in0=ps, in1=ccs, op=ALU.mult),
                    [ps, ccs], [tb_], tag="rope_cc")
                vop(P, lambda e, swp=swp, sss=sss, ub_=ub_: e.tensor_tensor(out=ub_, in0=swp, in1=sss, op=ALU.mult),
                    [swp, sss], [ub_], tag="rope_ss")
                vop(P, lambda e, tb_=tb_, ub_=ub_, rp=rp: e.tensor_add(out=rp[:], in0=tb_, in1=ub_),
                    [tb_, ub_], [rp[:]], tag="rope_add")
                P.dma("act", dstd[2 * oc, :, tb * 512:(tb + 1) * 512], rp[0:64, :], tag="rope_st0")
                P.dma("act", dstd[2 * oc + 1, :, tb * 512:(tb + 1) * 512], rp[64:128, :], tag="rope_st1")

    P.dma("pool", wA, G("l0a_woT").rearrange("(ki p) o -> p ki o", p=128), tag="ld_wo")
    for h in range(16):
        P.dma("sp", kbuf[:], G("l0a_kTd")[h], tag=f"fl_ldk{h}")
        P.dma("sp", qbuf[:], G("l0a_qTd")[h], tag=f"fl_ldq{h}")
        for qt in range(NT):
            npair_pv = qt // 8
            pd = qt // 8
            for pi in range(4):
                ps2 = c["pA"] if pi % 2 == 0 else c["pB"]
                lhs = qbuf[:, qt * 128:(qt + 1) * 128]
                mm(P, ps2[:, 0:512], lhs, kbuf[:, pi * 1024:pi * 1024 + 512],
                   tag=f"fl_s{h}_{qt}_{pi}a")
                mm(P, ps2[:, 512:1024], lhs, kbuf[:, pi * 1024 + 512:(pi + 1) * 1024],
                   tag=f"fl_s{h}_{qt}_{pi}b")
                pbuf = p16[pi % 2]
                sh30 = P.constant(-30.0)
                sop(P, lambda e, ps2=ps2, pbuf=pbuf, pi=pi, sh30=sh30: e.activation(
                        out=pbuf[:], in_=ps2[:], func=AF.Exp, scale=0.125, bias=sh30,
                        accum_out=dsum[:, pi:pi + 1]),
                    [ps2[:], sh30], [pbuf[:], dsum[:, pi:pi + 1]], tag=f"fl_exp{h}_{qt}_{pi}")
                kt0 = pi * 4
                nsub = min(4, qt + 1 - kt0)
                if nsub <= 0:
                    continue
                for j in range(nsub):
                    kt = kt0 + j
                    src = pbuf[:, j * 128:(j + 1) * 128]
                    if kt == qt:
                        vop(P, lambda e, src=src: e.tensor_tensor(out=pm16[:], in0=src,
                                                                  in1=tri16[:], op=ALU.mult),
                            [src, tri16[:]], [pm16[:]], tag="fl_mask")
                        src = pm16[:]
                    pb = c["pT1"]
                    tr(P, pb[:, j * 128:(j + 1) * 128], src, c["ident16"][:],
                       tag=f"fl_tr{h}_{qt}_{kt}")
                ev = pt16[:, 0:nsub * 128]
                vop(P, lambda e, ev=ev, nsub=nsub: e.tensor_copy(out=ev, in_=c["pT1"][:, 0:nsub * 128]),
                    [c["pT1"][:, 0:nsub * 128]], [ev], tag="fl_ptev")
                for j in range(nsub):
                    kt = kt0 + j
                    mm(P, c["pC"][:, 0:64], pt16[:, j * 128:(j + 1) * 128],
                       vsb[:, kt * 1024 + h * 64: kt * 1024 + h * 64 + 64],
                       start=(kt == 0), stop=(kt == qt), tag=f"fl_pv{h}_{qt}_{kt}")
            st = c["stats"]
            vop(P, lambda e: e.reduce_sum(out=st[:, 0:1], in_=dsum[:, 0:4], axis=AX.X),
                [dsum[:, 0:4]], [st[:, 0:1]], tag="fl_den")
            vop(P, lambda e: e.reciprocal(out=st[:, 1:2], in_=st[:, 0:1]),
                [st[:, 0:1]], [st[:, 1:2]], tag="fl_rden")
            vop(P, lambda e: e.tensor_scalar(out=att16[:], in0=c["pC"][:, 0:64],
                                             scalar1=st[:, 1:2], scalar2=None, op0=ALU.mult),
                [c["pC"][:, 0:64], st[:, 1:2]], [att16[:]], tag="fl_div")
            tr(P, c["pT0"][0:64, 0:128], att16[:], c["ident16"][:], tag="fl_trout")
            ko = h // 2
            po = (h % 2) * 64
            dst = big[po:po + 64, ko * S + qt * 128: ko * S + (qt + 1) * 128]
            vop(P, lambda e, dst=dst: e.tensor_copy(out=dst, in_=c["pT0"][0:64, 0:128]),
                [c["pT0"][0:64, 0:128]], [dst], tag="fl_attev")

    for tt in range(NT):
        xa = c["xtm"][tt % 2]
        ot = c["otm"][tt % 2]
        P.dma("sp", xa[:], xin[tt * 128:(tt + 1) * 128, :], tag=f"wo_ldx{tt}")
        for n in range(2):
            ps = c["pB"][:, n * 512:(n + 1) * 512]
            for ki in range(NF):
                mm(P, ps, big[:, ki * S + tt * 128:ki * S + (tt + 1) * 128],
                   wA[:, ki, n * 512:(n + 1) * 512],
                   start=(ki == 0), stop=(ki == NF - 1), tag=f"wo_mm{tt}_{n}")
            vop(P, lambda e, ps=ps, xa=xa, n=n, ot=ot: e.tensor_tensor(
                    out=ot[:, n * 512:(n + 1) * 512], in0=ps,
                    in1=xa[:, n * 512:(n + 1) * 512], op=ALU.add),
                [ps, xa[:, n * 512:(n + 1) * 512]], [ot[:, n * 512:(n + 1) * 512]],
                tag=f"wo_add{tt}_{n}")
        P.dma("act", xout[tt * 128:(tt + 1) * 128, :], ot[:], tag=f"wo_st{tt}")

    import os as _os
    if _os.environ.get("SKIP_TAIL") != "1":
        emit_router(P, c, xout, big, G, "l0c_", counts_ap=G("moe_counts"))




STAGE_FUNCS = {}

GROUPS = [["l0_attn"], ["l0_ffn"], ["l1_attn"], ["l1_ffn"],
          ["l2_attn"], ["l2_ffn"], ["l3_attn"], ["l3_ffn"]]

_PROG_CACHE = {}


def build_group(names, serial=False):
    key = (tuple(names), serial)
    if key in _PROG_CACHE:
        return _PROG_CACHE[key]
    P = Prog("g_" + "_".join(names), serial=serial)
    xin = P.dram_in("x_in", (S, D), F32)
    xcur = xin
    for i, nm in enumerate(names):
        sf = STAGE_FUNCS[nm]
        g = sf["dram"](P)
        last = (i == len(names) - 1)
        xo = P.dram_out("x_out", (S, D), F32) if last else P.dram_tmp(f"x_mid{i}", (S, D), F32)
        sf["emit"](P, xcur, xo, lambda k, g=g: g[k])
        xcur = xo
    consts = P.const_array() if P._constvals else None
    nc = P.finish()
    _PROG_CACHE[key] = (nc, consts)
    return _PROG_CACHE[key]


def run_groups(groups, xs, inputs, serial=False, trace=False, carry0=None):
    carry = dict(carry0 or {})
    nlaunch = 0
    for names in groups:
        nc, consts = build_group(names, serial=serial)
        preps = {}
        extras = []
        for nm in names:
            sf = STAGE_FUNCS[nm]
            preps.update(sf["prep"](inputs, carry))
            extras += sf["extra"]
        in_maps = []
        for b in range(N_CORES):
            m = {"x_in": xs[b]}
            if consts is not None:
                m["_consts"] = consts
            for k, v in preps.items():
                m[k] = v[b] if isinstance(v, list) else v
            in_maps.append(m)
        res = run_bass_kernel_spmd(nc, in_maps, core_ids=list(range(N_CORES)))
        nlaunch += 1
        xs = [np.asarray(res.results[b]["x_out"]) for b in range(N_CORES)]
        for ex in extras:
            carry[ex] = [np.asarray(res.results[b][ex]) for b in range(N_CORES)]
    return xs, carry


def kernel(**inputs):
    x = inputs["L0_attn_x"]
    xs = [np.ascontiguousarray(x[b]) for b in range(N_CORES)]
    xs, _ = run_groups(GROUPS, xs, inputs)
    return np.stack(xs, axis=0)



def _triu_incl16():
    return np.triu(np.ones((128, 128), np.float32)).astype(BF)


def dram_router(P, pfx):
    g = {}
    g[pfx + "rwT"] = P.dram_in(pfx + "rwT", (D, 8), F32)
    g[pfx + "rbbc"] = P.dram_in(pfx + "rbbc", (128, 8), F32)
    g[pfx + "triu"] = P.dram_in(pfx + "triu", (128, 128), BF16)
    g[pfx + "id16"] = P.dram_in(pfx + "id16", (128, 128), BF16)
    return g


def prep_router(pfx, rw, rb):
    return {
        pfx + "rwT": np.ascontiguousarray(rw.T),
        pfx + "rbbc": np.ascontiguousarray(np.broadcast_to(rb.reshape(1, 8), (128, 8))).astype(np.float32),
        pfx + "triu": _triu_incl16(),
        pfx + "id16": _ident16(),
    }


def emit_router(P, c, xin, big, G, pfx, counts_ap=None, full=None):
    import os as _os
    LVL = float(_os.environ.get("RT_LVL", "9"))
    st = c["stats"]
    ensure_ident(P, c, G, pfx + "id16")
    triu = P.sbuf(pfx + "triu_s", [128, 128], BF16)
    rw32 = P.sbuf(pfx + "rw32", [128, 64], F32)
    rwh = P.sbuf(pfx + "rwh", [128, 64], BF16)
    rwl = P.sbuf(pfx + "rwl", [128, 64], BF16)
    rbbc = P.sbuf(pfx + "rbbc_s", [128, 8], F32)
    ones_c = P.sbuf(pfx + "onc", [128, 1], BF16)
    ones_r = P.sbuf(pfx + "onr", [1, 128], BF16)
    lnt32 = c["sq"][:]
    xlo = P.sbuf(pfx + "xlo", [128, 1024], BF16)
    xloT = P.sbuf(pfx + "xloT", [128, 1024], BF16)
    lgt = P.sbuf(pfx + "lgt", [128, 8], F32)
    msk1 = P.sbuf(pfx + "msk1", [128, 8], F32)
    msk2 = P.sbuf(pfx + "msk2", [128, 8], F32)
    tmp8 = P.sbuf(pfx + "tmp8", [128, 8], F32)
    m16 = P.sbuf(pfx + "m16", [128, 8], BF16)
    carry = P.sbuf(pfx + "carry", [1, 16], F32)
    carh = P.sbuf(pfx + "carh", [1, 8], BF16)
    carl = P.sbuf(pfx + "carl", [1, 8], BF16)
    _init_carhl = True
    cnt = P.sbuf(pfx + "cnt", [1, 8], F32)
    P.dma("sp", triu[:], G(pfx + "triu"))
    P.dma("sp", rw32[:].rearrange("p (ki e) -> p ki e", e=8),
          G(pfx + "rwT").rearrange("(ki p) e -> p ki e", p=128))
    P.dma("sp", rbbc[:], G(pfx + "rbbc"))
    vop(P, lambda e: e.memset(ones_c[:], 1.0), [], [ones_c[:]])
    vop(P, lambda e: e.memset(ones_r[:], 1.0), [], [ones_r[:]])
    vop(P, lambda e: e.memset(carry[:], 0.0), [], [carry[:]])
    vop(P, lambda e: e.memset(carh[:], 0.0), [], [carh[:]])
    vop(P, lambda e: e.memset(carl[:], 0.0), [], [carl[:]])
    vop(P, lambda e: e.tensor_copy(out=rwh[:], in_=rw32[:]), [rw32[:]], [rwh[:]])
    vop(P, lambda e: e.tensor_sub(out=rwl[:], in0=rw32[:], in1=rwh[:]),
        [rw32[:], rwh[:]], [rwl[:]])
    rwhv = rwh[:].rearrange("p (ki e) -> p ki e", e=8)
    rwlv = rwl[:].rearrange("p (ki e) -> p ki e", e=8)

    if full is not None:
        cm = P.sbuf(pfx + "cm", [8, 8], F32)
        cmh = P.sbuf(pfx + "cmh", [8, 8], BF16)
        cml = P.sbuf(pfx + "cml", [8, 8], BF16)
        pm16 = P.sbuf(pfx + "pm16", [8, 1], BF16)
        thr = P.sbuf(pfx + "thr", [1, 16], F32)
        thrh = P.sbuf(pfx + "thrh", [1, 8], BF16)
        thrl = P.sbuf(pfx + "thrl", [1, 8], BF16)
        thbc = P.sbuf(pfx + "thbc", [128, 8], F32)
        gall = P.sbuf(pfx + "gall", [128, 8], F32)
        g1b = P.sbuf(pfx + "g1b", [128, 8], F32)
        gf16 = P.sbuf(pfx + "gf16", [128, 8], BF16)
        keep = P.sbuf(pfx + "keep", [128, 8], F32)
        gfT8 = full["gfT8_buf"] if "gfT8_buf" in full else P.sbuf(pfx + "gfT8", [8, S], BF16)[:]
        P.dma("sp", cm[:], full["cmat"])
        P.dma("pool", pm16[:], full["prefmask"])
        vop(P, lambda e: e.tensor_copy(out=cmh[:], in_=cm[:]), [cm[:]], [cmh[:]])
        vop(P, lambda e: e.tensor_sub(out=cml[:], in0=cm[:], in1=cmh[:]),
            [cm[:], cmh[:]], [cml[:]])
        mm(P, c["pB"][0:1, 0:8], pm16[:], cmh[:], start=True, stop=False, tag="rtr_oh")
        mm(P, c["pB"][0:1, 0:8], pm16[:], cml[:], start=False, stop=True, tag="rtr_ol")
        vop(P, lambda e: e.tensor_scalar(out=thr[:, 0:8], in0=c["pB"][0:1, 0:8],
                                         scalar1=-1.0, scalar2=2048.0,
                                         op0=ALU.mult, op1=ALU.add),
            [c["pB"][0:1, 0:8]], [thr[:, 0:8]], tag="rtr_thr")
        vop(P, lambda e: e.tensor_copy(out=thrh[:], in_=thr[:, 0:8]), [thr[:, 0:8]], [thrh[:]])
        vop(P, lambda e: e.tensor_sub(out=thrl[:], in0=thr[:, 0:8], in1=thrh[:]),
            [thr[:, 0:8], thrh[:]], [thrl[:]])
        mm(P, c["pB"][:, 0:8], ones_r[:], thrh[:], start=True, stop=False, tag="rtr_tbh")
        mm(P, c["pB"][:, 0:8], ones_r[:], thrl[:], start=False, stop=True, tag="rtr_tbl")
        vop(P, lambda e: e.tensor_copy(out=thbc[:], in_=c["pB"][:, 0:8]),
            [c["pB"][:, 0:8]], [thbc[:]], tag="rtr_tbev")

    for tt in range(NT):
        xa = c["xtm"][tt % 2]
        lnt = c["lntile"][tt % 2]
        P.dma("sp", xa[:], xin[tt * 128:(tt + 1) * 128, :], tag=f"rt_ld{tt}")
        sop(P, lambda e, xa=xa: e.activation(out=c["sq"][:], in_=xa[:], func=AF.Square,
                                             accum_out=st[:, 1:2]),
            [xa[:]], [c["sq"][:], st[:, 1:2]])
        vop(P, lambda e, xa=xa: e.reduce_sum(out=st[:, 0:1], in_=xa[:], axis=AX.X),
            [xa[:]], [st[:, 0:1]])
        vop(P, lambda e: e.tensor_scalar(out=st[:, 2:3], in0=st[:, 0:1], scalar1=1.0 / D,
                                         scalar2=None, op0=ALU.mult), [st[:, 0:1]], [st[:, 2:3]])
        vop(P, lambda e: e.tensor_scalar(out=st[:, 3:4], in0=st[:, 1:2], scalar1=1.0 / D,
                                         scalar2=None, op0=ALU.mult), [st[:, 1:2]], [st[:, 3:4]])
        vop(P, lambda e: e.tensor_tensor(out=st[:, 4:5], in0=st[:, 2:3], in1=st[:, 2:3],
                                         op=ALU.mult), [st[:, 2:3]], [st[:, 4:5]])
        vop(P, lambda e: e.tensor_sub(out=st[:, 5:6], in0=st[:, 3:4], in1=st[:, 4:5]),
            [st[:, 3:4], st[:, 4:5]], [st[:, 5:6]])
        epsap = P.constant(EPS_LN)
        sop(P, lambda e, epsap=epsap: e.activation(out=st[:, 6:7], in_=st[:, 5:6],
                                                   func=AF.Sqrt, bias=epsap),
            [st[:, 5:6], epsap], [st[:, 6:7]])
        vop(P, lambda e: e.reciprocal(out=st[:, 7:8], in_=st[:, 6:7]),
            [st[:, 6:7]], [st[:, 7:8]])
        vop(P, lambda e: e.tensor_tensor(out=st[:, 8:9], in0=st[:, 2:3], in1=st[:, 7:8],
                                         op=ALU.mult), [st[:, 2:3], st[:, 7:8]], [st[:, 8:9]])
        vop(P, lambda e: e.tensor_scalar(out=st[:, 9:10], in0=st[:, 8:9], scalar1=-1.0,
                                         scalar2=None, op0=ALU.mult), [st[:, 8:9]], [st[:, 9:10]])
        vop(P, lambda e, xa=xa: e.tensor_scalar(out=lnt32[:], in0=xa[:],
                scalar1=st[:, 7:8], scalar2=st[:, 9:10], op0=ALU.mult, op1=ALU.add),
            [xa[:], st[:, 7:8], st[:, 9:10]], [lnt32[:]], tag="rt_norm")
        vop(P, lambda e, lnt=lnt: e.tensor_copy(out=lnt[:], in_=lnt32[:]),
            [lnt32[:]], [lnt[:]], tag="rt_cast")
        vop(P, lambda e, lnt=lnt: e.tensor_sub(out=xlo[:], in0=lnt32[:], in1=lnt[:]),
            [lnt32[:], lnt[:]], [xlo[:]], tag="rt_lo")
        for k in range(NF):
            pb = c["pT0"] if k % 2 == 0 else c["pT1"]
            slot = (k // 2) * 128
            tr(P, pb[:, slot:slot + 128], lnt[:, k * 128:(k + 1) * 128], c["ident16"][:])
            dst = big[:, k * S + tt * 128:k * S + (tt + 1) * 128]
            vop(P, lambda e, pb=pb, slot=slot, dst=dst: e.tensor_copy(out=dst, in_=pb[:, slot:slot + 128]),
                [pb[:, slot:slot + 128]], [dst])
        for k in range(NF):
            pb = c["pT0"] if k % 2 == 0 else c["pT1"]
            slot = (k // 2) * 128
            tr(P, pb[:, slot:slot + 128], xlo[:, k * 128:(k + 1) * 128], c["ident16"][:],
               tag="rt_lotr")
            dst = xloT[:, k * 128:(k + 1) * 128]
            vop(P, lambda e, pb=pb, slot=slot, dst=dst: e.tensor_copy(out=dst, in_=pb[:, slot:slot + 128]),
                [pb[:, slot:slot + 128]], [dst])
        if LVL < 2:
            continue
        ps8 = c["pB"][:, 0:8]
        for k in range(NF):
            xhiT = big[:, k * S + tt * 128:k * S + (tt + 1) * 128]
            mm(P, ps8, xhiT, rwhv[:, k, :], start=(k == 0), stop=False, tag=f"rt_lgh{tt}_{k}")
            mm(P, ps8, xhiT, rwlv[:, k, :], start=False, stop=False, tag=f"rt_lgl{tt}_{k}")
            mm(P, ps8, xloT[:, k * 128:(k + 1) * 128], rwhv[:, k, :],
               start=False, stop=(k == NF - 1), tag=f"rt_lgx{tt}_{k}")
        vop(P, lambda e: e.tensor_tensor(out=lgt[:], in0=ps8, in1=rbbc[:], op=ALU.add),
            [ps8, rbbc[:]], [lgt[:]], tag="rt_lgev")
        if LVL < 3:
            continue
        vop(P, lambda e: e.reduce_max(out=st[:, 10:11], in_=lgt[:], axis=AX.X),
            [lgt[:]], [st[:, 10:11]])
        vop(P, lambda e: e.tensor_single_scalar(out=msk1[:], in_=lgt[:], scalar=st[:, 10:11],
                                                op=ALU.is_equal), [lgt[:], st[:, 10:11]], [msk1[:]])
        vop(P, lambda e: e.scalar_tensor_tensor(out=tmp8[:], in0=msk1[:], scalar=-1e30,
                                                in1=lgt[:], op0=ALU.mult, op1=ALU.add),
            [msk1[:], lgt[:]], [tmp8[:]])
        vop(P, lambda e: e.reduce_max(out=st[:, 11:12], in_=tmp8[:], axis=AX.X),
            [tmp8[:]], [st[:, 11:12]])
        vop(P, lambda e: e.tensor_single_scalar(out=msk2[:], in_=tmp8[:], scalar=st[:, 11:12],
                                                op=ALU.is_equal), [tmp8[:], st[:, 11:12]], [msk2[:]])
        vop(P, lambda e: e.tensor_add(out=m16[:], in0=msk1[:], in1=msk2[:]),
            [msk1[:], msk2[:]], [m16[:]], tag="rt_m")
        if LVL < 4:
            continue
        mm(P, c["pD"][0:1, 0:8], ones_c[:], m16[:],
           start=(tt == 0), stop=(tt == NT - 1), tag=f"rt_cnt{tt}")
        if full is not None:
            vop(P, lambda e: e.tensor_sub(out=st[:, 12:13], in0=st[:, 10:11], in1=st[:, 11:12]),
                [st[:, 10:11], st[:, 11:12]], [st[:, 12:13]])
            sop(P, lambda e: e.activation(out=st[:, 13:14], in_=st[:, 12:13], func=AF.Sigmoid),
                [st[:, 12:13]], [st[:, 13:14]])
            vop(P, lambda e: e.tensor_scalar(out=st[:, 14:15], in0=st[:, 13:14], scalar1=-1.0,
                                             scalar2=1.0, op0=ALU.mult, op1=ALU.add),
                [st[:, 13:14]], [st[:, 14:15]])
            vop(P, lambda e: e.tensor_single_scalar(out=g1b[:], in_=msk1[:], scalar=st[:, 13:14],
                                                    op=ALU.mult), [msk1[:], st[:, 13:14]], [g1b[:]])
            vop(P, lambda e: e.scalar_tensor_tensor(out=gall[:], in0=msk2[:], scalar=st[:, 14:15],
                                                    in1=g1b[:], op0=ALU.mult, op1=ALU.add),
                [msk2[:], st[:, 14:15], g1b[:]], [gall[:]])
            pcu = c["pA"][:, 0:8]
            mm(P, pcu, triu[:], m16[:], start=True, stop=False, tag=f"rt_cum{tt}")
            mm(P, pcu, ones_r[:], carh[:], start=False, stop=False, tag=f"rt_cah{tt}")
            mm(P, pcu, ones_r[:], carl[:], start=False, stop=True, tag=f"rt_cal{tt}")
            vop(P, lambda e: e.tensor_tensor(out=keep[:], in0=pcu, in1=thbc[:], op=ALU.is_le),
                [pcu, thbc[:]], [keep[:]], tag="rt_keep")
            vop(P, lambda e: e.tensor_mul(out=g1b[:], in0=gall[:], in1=keep[:]),
                [gall[:], keep[:]], [g1b[:]], tag="rt_gf")
            vop(P, lambda e: e.tensor_copy(out=gf16[:], in_=g1b[:]), [g1b[:]], [gf16[:]])
            tr(P, c["pT0"][0:8, 0:128], gf16[:], c["ident16"][:], tag="rt_gtr")
            gslice = gfT8[:, tt * 128:(tt + 1) * 128]
            vop(P, lambda e, gslice=gslice: e.tensor_copy(out=gslice, in_=c["pT0"][0:8, 0:128]),
                [c["pT0"][0:8, 0:128]], [gslice])
            pcs = c["pA"][0:1, 8:16]
            mm(P, pcs, ones_c[:], m16[:], tag=f"rt_cs{tt}")
            vop(P, lambda e: e.tensor_tensor(out=carry[:, 0:8], in0=carry[:, 0:8], in1=pcs,
                                             op=ALU.add),
                [carry[:, 0:8], pcs], [carry[:, 0:8]], tag="rt_carup")
            vop(P, lambda e: e.tensor_copy(out=carh[:], in_=carry[:, 0:8]),
                [carry[:, 0:8]], [carh[:]])
            vop(P, lambda e: e.tensor_sub(out=carl[:], in0=carry[:, 0:8], in1=carh[:]),
                [carry[:, 0:8], carh[:]], [carl[:]])
    if counts_ap is not None and LVL >= 4:
        vop(P, lambda e: e.tensor_copy(out=cnt[:], in_=c["pD"][0:1, 0:8]),
            [c["pD"][0:1, 0:8]], [cnt[:]], tag="rt_cntev")
        P.dma("act", counts_ap, cnt[:], tag="rt_cntst")
    if full is not None:
        P.dma("act", full["ghbm"], gfT8, tag="rt_gfst")



def prep_l0_ffn(inputs, carry):
    d = prep_router("l0f_", inputs["L0_ffn_router_w"], inputs["L0_ffn_router_b"])
    d["l0f_W"] = np.ascontiguousarray(inputs["L0_ffn_W"])
    d["l0f_V"] = np.ascontiguousarray(inputs["L0_ffn_V"])
    d["l0f_Wout"] = np.ascontiguousarray(inputs["L0_ffn_W_out"])
    counts = carry["moe_counts"]
    cmat = np.concatenate([c.reshape(1, 8) for c in counts], axis=0).astype(np.float32)
    d["l0f_cmat"] = cmat
    d["l0f_pref"] = [np.asarray([[1.0]] * b + [[0.0]] * (8 - b), np.float32).reshape(8, 1)
                     for b in range(8)]
    return d


def dram_l0_ffn(P):
    g = dram_router(P, "l0f_")
    g["l0f_W"] = P.dram_in("l0f_W", (8, D, 2048), F32)
    g["l0f_V"] = P.dram_in("l0f_V", (8, D, 2048), F32)
    g["l0f_Wout"] = P.dram_in("l0f_Wout", (8, 2048, D), F32)
    g["l0f_cmat"] = P.dram_in("l0f_cmat", (8, 8), F32)
    g["l0f_pref"] = P.dram_in("l0f_pref", (8, 1), F32)
    g["l0f_ghbm"] = P.dram_tmp("l0f_ghbm", (8, S), BF16)
    g["l0f_yacc"] = P.dram_tmp("l0f_yacc", (S, D), F32)
    return g


def emit_l0_ffn(P, xin, xout, G):
    c = alloc_common(P, notm=1)
    big = P.sbuf("l0f_big", [128, NF * S], BF16)
    wW = P.sbuf("l0f_wW", [128, 16384], BF16)
    wV = P.sbuf("l0f_wV", [128, 16384], BF16)
    wO = P.sbuf("l0f_wO", [128, 16384], BF16)
    emit_router(P, c, xin, big, G, "l0f_",
                full=dict(cmat=G("l0f_cmat"), prefmask=G("l0f_pref"), ghbm=G("l0f_ghbm"),
                          gfT8_buf=wO[0:8, 0:S]))
    wWv = wW[:].rearrange("p (ki o) -> p ki o", o=2048)
    wVv = wV[:].rearrange("p (ki o) -> p ki o", o=2048)
    wOv = wO[:].rearrange("p (mi o) -> p mi o", o=1024)
    hgb = P.sbuf("l0f_hgb", [128, 16 * 512], BF16)
    hgv = hgb[:].rearrange("p (mi t) -> p mi t", t=512)
    grow16 = P.sbuf("l0f_grow16", [1, 512], BF16)
    gbc16 = P.sbuf("l0f_gbc", [128, 512], BF16)
    ag16 = P.sbuf("l0f_ag", [128, 512], BF16)
    sb16 = P.sbuf("l0f_sb", [128, 512], BF16)
    yst = [P.sbuf("l0f_yst0", [128, 512], F32)] * 2
    ones_r16 = P.sbuf("l0f_onr16", [1, 128], BF16)
    vop(P, lambda e: e.memset(ones_r16[:], 1.0), [], [ones_r16[:]])

    yacc = G("l0f_yacc")
    for ex in range(8):
        P.dma("pool", wWv, G("l0f_W")[ex].rearrange("(ki p) o -> p ki o", p=128),
              tag=f"ex{ex}_ldW")
        P.dma("pool", wVv, G("l0f_V")[ex].rearrange("(ki p) o -> p ki o", p=128),
              tag=f"ex{ex}_ldV")
        P.dma("pool", wOv, G("l0f_Wout")[ex].rearrange("(mi p) o -> p mi o", p=128),
              tag=f"ex{ex}_ldO")

        for tb in range(8):
            P.dma("sp", grow16[:], G("l0f_ghbm")[ex:ex + 1, tb * 512:(tb + 1) * 512],
                  tag=f"ex{ex}_ldg{tb}")
            mm(P, c["pD"][:, 0:512], ones_r16[:], grow16[:],
               tag=f"ex{ex}_gbc{tb}")
            vop(P, lambda e: e.tensor_copy(out=gbc16[:], in_=c["pD"][:, 0:512]),
                [c["pD"][:, 0:512]], [gbc16[:]], tag=f"ex{ex}_gbcev{tb}")
            for m in range(16):
                psa = c["pA"][:, (m % 2) * 512:(m % 2) * 512 + 512]
                psb_ = c["pB"][:, (m % 2) * 512:(m % 2) * 512 + 512]
                for ki in range(NF):
                    mm(P, psa, wWv[:, ki, m * 128:(m + 1) * 128],
                       big[:, ki * S + tb * 512:ki * S + (tb + 1) * 512],
                       start=(ki == 0), stop=(ki == NF - 1), tag=f"ex{ex}_a{tb}_{m}")
                for ki in range(NF):
                    mm(P, psb_, wVv[:, ki, m * 128:(m + 1) * 128],
                       big[:, ki * S + tb * 512:ki * S + (tb + 1) * 512],
                       start=(ki == 0), stop=(ki == NF - 1), tag=f"ex{ex}_b{tb}_{m}")
                vop(P, lambda e, psa=psa: e.tensor_tensor(out=ag16[:], in0=psa, in1=gbc16[:],
                                                          op=ALU.mult),
                    [psa, gbc16[:]], [ag16[:]], tag=f"ex{ex}_ag{tb}_{m}")
                sop(P, lambda e, psb_=psb_: e.activation(out=sb16[:], in_=psb_, func=AF.Silu),
                    [psb_], [sb16[:]], tag=f"ex{ex}_si{tb}_{m}")
                dst = hgv[:, m, :]
                vop(P, lambda e, dst=dst: e.tensor_tensor(out=dst, in0=ag16[:], in1=sb16[:],
                                                          op=ALU.mult),
                    [ag16[:], sb16[:]], [dst], tag=f"ex{ex}_hg{tb}_{m}")
            for ts in range(4):
                for n in range(2):
                    ps = c["pC"][:, 0:512] if (ts * 2 + n) % 2 == 0 else c["pD"][:, 0:512]
                    for m in range(16):
                        mm(P, ps, hgv[:, m, ts * 128:(ts + 1) * 128],
                           wOv[:, m, n * 512:(n + 1) * 512],
                           start=(m == 0), stop=(m == 15), tag=f"ex{ex}_o{tb}_{ts}_{n}")
                    stg = yst[(ts * 2 + n) % 2]
                    vop(P, lambda e, ps=ps, stg=stg: e.tensor_copy(out=stg[:], in_=ps),
                        [ps], [stg[:]], tag=f"ex{ex}_oev{tb}_{ts}_{n}")
                    dst = yacc[tb * 512 + ts * 128: tb * 512 + (ts + 1) * 128,
                               n * 512:(n + 1) * 512]
                    if ex == 0:
                        P.dma("pool", dst, stg[:], tag=f"ex{ex}_yst{tb}_{ts}_{n}")
                    else:
                        P.dma("pool", dst, stg[:], accum_op=ALU.add,
                              tag=f"ex{ex}_yacc{tb}_{ts}_{n}")
    for tt in range(NT):
        xa = c["xtm"][tt % 2]
        ya = c["otm"][tt % 2]
        P.dma("sp", xa[:], xin[tt * 128:(tt + 1) * 128, :], tag=f"fin_ldx{tt}")
        P.dma("sp", ya[:], yacc[tt * 128:(tt + 1) * 128, :], tag=f"fin_ldy{tt}")
        vop(P, lambda e, xa=xa, ya=ya: e.tensor_add(out=ya[:], in0=ya[:], in1=xa[:]),
            [xa[:], ya[:]], [ya[:]], tag=f"fin_add{tt}")
        P.dma("act", xout[tt * 128:(tt + 1) * 128, :], ya[:], tag=f"fin_st{tt}")


STAGE_FUNCS["l0_attn"] = dict(prep=lambda inputs, carry: prep_l0_attn(inputs),
                              dram=dram_l0_attn, emit=emit_l0_attn,
                              extra=["moe_counts"])
STAGE_FUNCS["l0_ffn"] = dict(prep=prep_l0_ffn, dram=dram_l0_ffn, emit=emit_l0_ffn,
                             extra=[])



def _strip_mask_1e10():
    u = np.arange(1024)[None, :]
    i = np.arange(128)[:, None]
    return np.where(u <= i + 512, 0.0, -1e10).astype(np.float32)


def prep_l1_attn(inputs):
    b = inputs["L1_attn_b_attn"]
    return {
        "l1a_wqkv": np.ascontiguousarray(inputs["L1_attn_w_attn"]),
        "l1a_wproj": np.ascontiguousarray(inputs["L1_attn_w_proj"]),
        "l1a_bqk": np.ascontiguousarray(
            np.stack([b[0:1024].reshape(8, 128).T, b[1024:2048].reshape(8, 128).T], axis=0)),
        "l1a_bvbc": np.ascontiguousarray(np.broadcast_to(b[2048:3072][None, :], (128, 1024))),
        "l1a_bpbc": np.ascontiguousarray(np.broadcast_to(
            inputs["L1_attn_b_proj"][None, :], (128, 1024))),
        "l1a_strip": _strip_mask_1e10().astype(BF),
        "l1a_id16": _ident16(),
        "l1a_gbc": np.ascontiguousarray(np.broadcast_to(
            inputs["L1_ffn_ln_gamma"][None, :], (128, 1024))),
        "l1a_bbc": np.ascontiguousarray(np.broadcast_to(
            inputs["L1_ffn_ln_beta"][None, :], (128, 1024))),
    }


def dram_l1_attn(P):
    g = {}
    g["l1a_wqkv"] = P.dram_in("l1a_wqkv", (D, 3 * D), F32)
    g["l1a_wproj"] = P.dram_in("l1a_wproj", (D, D), F32)
    g["l1a_bqk"] = P.dram_in("l1a_bqk", (2, 128, 8), F32)
    g["l1a_bvbc"] = P.dram_in("l1a_bvbc", (128, 1024), F32)
    g["l1a_bpbc"] = P.dram_in("l1a_bpbc", (128, 1024), F32)
    g["l1a_strip"] = P.dram_in("l1a_strip", (128, 1024), BF16)
    g["l1a_id16"] = P.dram_in("l1a_id16", (128, 128), BF16)
    g["l1a_gbc"] = P.dram_in("l1a_gbc", (128, 1024), F32)
    g["l1a_bbc"] = P.dram_in("l1a_bbc", (128, 1024), F32)
    g["xn_max"] = P.dram_out("xn_max", (1, 1), F32)
    g["l1a_qTd"] = P.dram_tmp("l1a_qTd", (D, S), BF16)
    g["l1a_kTd"] = P.dram_tmp("l1a_kTd", (D, S), BF16)
    g["l1a_pv"] = P.dram_tmp("l1a_pv", (S, D), F32)
    return g


def emit_l1_attn(P, xin, xout, G):
    c = alloc_common(P)
    ensure_ident(P, c, G, "l1a_id16")
    big = P.sbuf("l1a_big", [128, NF * S], BF16)
    vsb = P.sbuf("l1a_vsb", [128, NT * 1024], BF16)
    w16 = P.sbuf("l1a_w16", [128, 16384], BF16)
    wv_ = w16[:, 0:8192].rearrange("p (ki o) -> p ki o", o=1024)
    bqk = P.sbuf("l1a_bqk_s", [128, 16], F32)
    bvbc = P.sbuf("l1a_bvbc_s", [128, 1024], F32)
    bpbc = P.sbuf("l1a_bpbc_s", [128, 1024], F32)
    strip = P.sbuf("l1a_strip_s", [128, 1024], BF16)
    kblk = P.sbuf("l1a_kblk", [128, 8 * 512], BF16)
    kbv = kblk[:].rearrange("p (ki k) -> p ki k", k=512)
    p16 = P.sbuf("l1a_p16", [128, 512], BF16)
    pm32 = P.sbuf("l1a_pm32", [128, 512], F32)
    pt16 = P.sbuf("l1a_pt16", [128, 512], BF16)
    dall = P.sbuf("l1a_dall", [128, 32], F32)
    att16 = P.sbuf("l1a_att16", [128, 1024], BF16)
    atT16 = P.sbuf("l1a_atT16", [128, 1024], BF16)
    yst = None
    P.dma("sp", bqk[:, 0:8], G("l1a_bqk")[0])
    P.dma("sp", bqk[:, 8:16], G("l1a_bqk")[1])
    P.dma("sp", bvbc[:], G("l1a_bvbc"))
    P.dma("sp", strip[:], G("l1a_strip"))

    emit_ln(P, c, xin, big)

    P.dma("pool", w16[:, 0:8192].rearrange("p (ki o) -> p ki o", o=1024),
          G("l1a_wqkv")[:, 2048:3072].rearrange("(ki p) o -> p ki o", p=128), tag="ld_wv")
    for tt in range(NT):
        for n in range(2):
            ps = c["pA"][:, n * 512:(n + 1) * 512]
            for ki in range(NF):
                mm(P, ps, big[:, ki * S + tt * 128:ki * S + (tt + 1) * 128],
                   wv_[:, ki, n * 512:(n + 1) * 512],
                   start=(ki == 0), stop=(ki == NF - 1), tag=f"v_mm{tt}_{n}")
            dst = vsb[:, tt * 1024 + n * 512: tt * 1024 + (n + 1) * 512]
            vop(P, lambda e, ps=ps, dst=dst, n=n: e.tensor_tensor(
                    out=dst, in0=ps, in1=bvbc[:, n * 512:(n + 1) * 512], op=ALU.add),
                [ps, bvbc[:, n * 512:(n + 1) * 512]], [dst], tag=f"v_ev{tt}_{n}")
    for qk, dnam in enumerate(["l1a_qTd", "l1a_kTd"]):
        P.dma("pool", w16[:, 0:8192].rearrange("p (ki o) -> p ki o", o=1024),
              G("l1a_wqkv")[:, qk * 1024:(qk + 1) * 1024].rearrange("(ki p) o -> p ki o", p=128),
              tag=f"ld_w{qk}")
        wA = w16[:, 0:8192].rearrange("p (ki o) -> p ki o", o=1024)
        dstd = G(dnam)
        for oc in range(NF):
            for tb in range(8):
                r = (oc * 8 + tb) % 2
                ps = c["pB"][:, r * 512:(r + 1) * 512]
                for ki in range(NF):
                    mm(P, ps, wA[:, ki, oc * 128:(oc + 1) * 128],
                       big[:, ki * S + tb * 512:ki * S + (tb + 1) * 512],
                       start=(ki == 0), stop=(ki == NF - 1), tag=f"qk{qk}_mm{oc}_{tb}")
                rp = p16 if r == 0 else pt16
                vop(P, lambda e, ps=ps, rp=rp, qk=qk, oc=oc: e.tensor_scalar(
                        out=rp[:], in0=ps, scalar1=bqk[:, qk * 8 + oc:qk * 8 + oc + 1],
                        scalar2=None, op0=ALU.add),
                    [ps, bqk[:, qk * 8 + oc:qk * 8 + oc + 1]], [rp[:]], tag=f"qk{qk}_ev{oc}_{tb}")
                P.dma("act", dstd[oc * 128:(oc + 1) * 128, tb * 512:(tb + 1) * 512], rp[:],
                      tag=f"qk{qk}_st{oc}_{tb}")
    for ki in range(NF):
        P.dma("sp", big[:, ki * S:(ki + 1) * S], G("l1a_qTd")[ki * 128:(ki + 1) * 128, :],
              tag=f"ldq{ki}")

    pvd = G("l1a_pv")
    vop(P, lambda e: e.memset(dall[:], 0.0), [], [dall[:]])
    sh20 = P.constant(-20.0)
    for kc in range(8):
        P.dma("sp", kbv, G("l1a_kTd")[:, kc * 512:(kc + 1) * 512].rearrange(
            "(ki p) k -> p ki k", p=128), tag=f"fl_ldk{kc}")
        for qt in range(4 * kc, NT):
            j = qt - 4 * kc
            pss = c["pC"][:, 0:512] if qt % 2 == 0 else c["pD"][:, 0:512]
            for ki in range(NF):
                mm(P, pss, big[:, ki * S + qt * 128:ki * S + (qt + 1) * 128],
                   kbv[:, ki, :], start=(ki == 0), stop=(ki == NF - 1),
                   tag=f"fl_s{kc}_{qt}_{ki}")
            if j < 4:
                off = 512 - 128 * j
                vop(P, lambda e, pss=pss, off=off: e.tensor_tensor(
                        out=pm32[:], in0=pss, in1=strip[:, off:off + 512], op=ALU.add),
                    [pss, strip[:, off:off + 512]], [pm32[:]], tag=f"fl_msk{kc}_{qt}")
                esrc = pm32[:]
            else:
                esrc = pss
            sop(P, lambda e, esrc=esrc: e.activation(
                    out=p16[:], in_=esrc, func=AF.Exp, scale=1.0 / 32.0, bias=sh20,
                    accum_out=c["stats"][:, 0:1]),
                [esrc, sh20], [p16[:], c["stats"][:, 0:1]], tag=f"fl_exp{kc}_{qt}")
            vop(P, lambda e, qt=qt: e.tensor_tensor(out=dall[:, qt:qt + 1],
                    in0=dall[:, qt:qt + 1], in1=c["stats"][:, 0:1], op=ALU.add),
                [dall[:, qt:qt + 1], c["stats"][:, 0:1]], [dall[:, qt:qt + 1]],
                tag=f"fl_dacc{kc}_{qt}")
            nsub = min(4, qt + 1 - 4 * kc)
            for jj in range(nsub):
                tr(P, c["pT1"][:, jj * 128:(jj + 1) * 128], p16[:, jj * 128:(jj + 1) * 128],
                   c["ident16"][:], tag=f"fl_tr{kc}_{qt}_{jj}")
            vop(P, lambda e, nsub=nsub: e.tensor_copy(out=pt16[:, 0:nsub * 128],
                                                      in_=c["pT1"][:, 0:nsub * 128]),
                [c["pT1"][:, 0:nsub * 128]], [pt16[:, 0:nsub * 128]], tag=f"fl_ptev{kc}_{qt}")
            for n in range(2):
                ps = c["pA"][:, n * 512:(n + 1) * 512]
                for jj in range(nsub):
                    kt = kc * 4 + jj
                    mm(P, ps, pt16[:, jj * 128:(jj + 1) * 128],
                       vsb[:, kt * 1024 + n * 512: kt * 1024 + (n + 1) * 512],
                       start=(jj == 0), stop=(jj == nsub - 1), tag=f"fl_pv{kc}_{qt}_{jj}_{n}")
                stg = pm32[:]
                vop(P, lambda e, ps=ps, stg=stg: e.tensor_copy(out=stg, in_=ps),
                    [ps], [stg], tag=f"fl_pvev{kc}_{qt}_{n}")
                dst = pvd[qt * 128:(qt + 1) * 128, n * 512:(n + 1) * 512]
                if kc == 0:
                    P.dma("pool", dst, stg, tag=f"fl_pvst{kc}_{qt}_{n}")
                else:
                    P.dma("pool", dst, stg, accum_op=ALU.add, tag=f"fl_pvacc{kc}_{qt}_{n}")

    P.dma("pool", w16[:, 0:8192].rearrange("p (ki o) -> p ki o", o=1024),
          G("l1a_wproj").rearrange("(ki p) o -> p ki o", p=128), tag="ld_wp")
    P.dma("sp", bpbc[:], G("l1a_bpbc"), tag="ld_bp")
    wp = w16[:, 0:8192].rearrange("p (ki o) -> p ki o", o=1024)
    for tt in range(NT):
        xa = c["xtm"][tt % 2]
        ot = c["otm"][tt % 2]
        P.dma("sp", xa[:], pvd[tt * 128:(tt + 1) * 128, :], tag=f"pr_ldpv{tt}")
        vop(P, lambda e, tt=tt: e.reciprocal(out=c["stats"][:, 1:2], in_=dall[:, tt:tt + 1]),
            [dall[:, tt:tt + 1]], [c["stats"][:, 1:2]], tag=f"pr_rd{tt}")
        vop(P, lambda e, xa=xa: e.tensor_scalar(out=att16[:], in0=xa[:],
                scalar1=c["stats"][:, 1:2], scalar2=None, op0=ALU.mult),
            [xa[:], c["stats"][:, 1:2]], [att16[:]], tag=f"pr_div{tt}")
        for k in range(NF):
            pb = c["pT0"] if k % 2 == 0 else c["pT1"]
            slot = (k // 2) * 128
            tr(P, pb[:, slot:slot + 128], att16[:, k * 128:(k + 1) * 128], c["ident16"][:],
               tag=f"pr_tr{tt}_{k}")
            dst = atT16[:, k * 128:(k + 1) * 128]
            vop(P, lambda e, pb=pb, slot=slot, dst=dst: e.tensor_copy(out=dst, in_=pb[:, slot:slot + 128]),
                [pb[:, slot:slot + 128]], [dst], tag=f"pr_trev{tt}_{k}")
        P.dma("sp", xa[:], xin[tt * 128:(tt + 1) * 128, :], tag=f"pr_ldx{tt}")
        for n in range(2):
            ps = c["pB"][:, n * 512:(n + 1) * 512]
            for k in range(NF):
                mm(P, ps, atT16[:, k * 128:(k + 1) * 128], wp[:, k, n * 512:(n + 1) * 512],
                   start=(k == 0), stop=(k == NF - 1), tag=f"pr_mm{tt}_{n}")
            vop(P, lambda e, ps=ps, n=n: e.tensor_tensor(out=pm32[:], in0=ps,
                    in1=bpbc[:, n * 512:(n + 1) * 512], op=ALU.add),
                [ps, bpbc[:, n * 512:(n + 1) * 512]], [pm32[:]], tag=f"pr_b{tt}_{n}")
            vop(P, lambda e, xa=xa, n=n, ot=ot: e.tensor_tensor(
                    out=ot[:, n * 512:(n + 1) * 512], in0=pm32[:],
                    in1=xa[:, n * 512:(n + 1) * 512], op=ALU.add),
                [pm32[:], xa[:, n * 512:(n + 1) * 512]], [ot[:, n * 512:(n + 1) * 512]],
                tag=f"pr_add{tt}_{n}")
        P.dma("act", xout[tt * 128:(tt + 1) * 128, :], ot[:], tag=f"pr_st{tt}")

    st = c["stats"]
    xmx = st[:, 16:17]
    vop(P, lambda e: e.memset(xmx, 0.0), [], [xmx])
    P.dma("sp", bvbc[:], G("l1a_gbc"), tag="tl_ldg")
    P.dma("sp", bpbc[:], G("l1a_bbc"), tag="tl_ldb")
    for tt in range(NT):
        xa = c["xtm"][tt % 2]
        P.dma("sp", xa[:], xout[tt * 128:(tt + 1) * 128, :], tag=f"tl_ld{tt}")
        emit_ln_affine_tile(P, c, xa, bvbc, bpbc, c["sq"][:], tag=f"tl_ln{tt}_")
        vop(P, lambda e: e.tensor_reduce(out=st[:, 17:18], in_=c["sq"][:], axis=AX.X,
                                         op=ALU.max, apply_absolute_value=True),
            [c["sq"][:]], [st[:, 17:18]], tag=f"tl_mx{tt}")
        vop(P, lambda e: e.tensor_tensor(out=xmx, in0=xmx, in1=st[:, 17:18], op=ALU.max),
            [xmx, st[:, 17:18]], [xmx], tag=f"tl_acc{tt}")
    mh = P.sbuf("l1a_mh", [128, 1], BF16)
    ml = P.sbuf("l1a_ml", [128, 1], BF16)
    rh = P.sbuf("l1a_rh", [1, 128], F32)
    rl = P.sbuf("l1a_rl", [1, 128], F32)
    mx11 = P.sbuf("l1a_mx11", [1, 1], F32)
    vop(P, lambda e: e.tensor_copy(out=mh[:], in_=xmx), [xmx], [mh[:]])
    vop(P, lambda e: e.tensor_sub(out=ml[:], in0=xmx, in1=mh[:]), [xmx, mh[:]], [ml[:]])
    tr(P, c["pT0"][0:1, 0:128], mh[:], c["ident16"][:], tag="tl_trh")
    vop(P, lambda e: e.tensor_copy(out=rh[:], in_=c["pT0"][0:1, 0:128]),
        [c["pT0"][0:1, 0:128]], [rh[:]])
    tr(P, c["pT1"][0:1, 0:128], ml[:], c["ident16"][:], tag="tl_trl")
    vop(P, lambda e: e.tensor_copy(out=rl[:], in_=c["pT1"][0:1, 0:128]),
        [c["pT1"][0:1, 0:128]], [rl[:]])
    vop(P, lambda e: e.tensor_add(out=rh[:], in0=rh[:], in1=rl[:]), [rh[:], rl[:]], [rh[:]])
    vop(P, lambda e: e.reduce_max(out=mx11[:], in_=rh[:], axis=AX.X), [rh[:]], [mx11[:]])
    P.dma("act", G("xn_max"), mx11[:], tag="tl_st")


STAGE_FUNCS["l1_attn"] = dict(prep=lambda inputs, carry: prep_l1_attn(inputs),
                              dram=dram_l1_attn, emit=emit_l1_attn, extra=["xn_max"])



LN_EPS_L1 = 1e-3
QB = 128.0
EPS_Q = 1e-5


def emit_ln_affine_tile(P, c, xa, gbc, bbc, out32, tag=""):
    st = c["stats"]
    sop(P, lambda e: e.activation(out=c["sq"][:], in_=xa[:], func=AF.Square,
                                  accum_out=st[:, 1:2]), [xa[:]], [c["sq"][:], st[:, 1:2]],
        tag=tag + "sq")
    vop(P, lambda e: e.reduce_sum(out=st[:, 0:1], in_=xa[:], axis=AX.X),
        [xa[:]], [st[:, 0:1]], tag=tag + "sum")
    vop(P, lambda e: e.tensor_scalar(out=st[:, 2:3], in0=st[:, 0:1], scalar1=1.0 / D,
                                     scalar2=None, op0=ALU.mult), [st[:, 0:1]], [st[:, 2:3]])
    vop(P, lambda e: e.tensor_scalar(out=st[:, 3:4], in0=st[:, 1:2], scalar1=1.0 / D,
                                     scalar2=None, op0=ALU.mult), [st[:, 1:2]], [st[:, 3:4]])
    vop(P, lambda e: e.tensor_tensor(out=st[:, 4:5], in0=st[:, 2:3], in1=st[:, 2:3],
                                     op=ALU.mult), [st[:, 2:3]], [st[:, 4:5]])
    vop(P, lambda e: e.tensor_sub(out=st[:, 5:6], in0=st[:, 3:4], in1=st[:, 4:5]),
        [st[:, 3:4], st[:, 4:5]], [st[:, 5:6]])
    epsap = P.constant(LN_EPS_L1)
    sop(P, lambda e, epsap=epsap: e.activation(out=st[:, 6:7], in_=st[:, 5:6],
                                               func=AF.Sqrt, bias=epsap),
        [st[:, 5:6], epsap], [st[:, 6:7]])
    vop(P, lambda e: e.reciprocal(out=st[:, 7:8], in_=st[:, 6:7]), [st[:, 6:7]], [st[:, 7:8]])
    vop(P, lambda e: e.tensor_tensor(out=st[:, 8:9], in0=st[:, 2:3], in1=st[:, 7:8],
                                     op=ALU.mult), [st[:, 2:3], st[:, 7:8]], [st[:, 8:9]])
    vop(P, lambda e: e.tensor_scalar(out=st[:, 9:10], in0=st[:, 8:9], scalar1=-1.0,
                                     scalar2=None, op0=ALU.mult), [st[:, 8:9]], [st[:, 9:10]])
    vop(P, lambda e: e.tensor_scalar(out=out32, in0=xa[:],
            scalar1=st[:, 7:8], scalar2=st[:, 9:10], op0=ALU.mult, op1=ALU.add),
        [xa[:], st[:, 7:8], st[:, 9:10]], [out32], tag=tag + "n")
    vop(P, lambda e: e.tensor_mul(out=out32, in0=out32, in1=gbc[:]),
        [out32, gbc[:]], [out32], tag=tag + "g")
    vop(P, lambda e: e.tensor_add(out=out32, in0=out32, in1=bbc[:]),
        [out32, bbc[:]], [out32], tag=tag + "b")


def prep_l1_ffn(inputs, carry):
    xm = carry["xn_max"]
    return {
        "l1f_w": np.ascontiguousarray(inputs["L1_ffn_weight"]),
        "l1f_gbc": np.ascontiguousarray(np.broadcast_to(
            inputs["L1_ffn_ln_gamma"][None, :], (128, 1024))),
        "l1f_bbc": np.ascontiguousarray(np.broadcast_to(
            inputs["L1_ffn_ln_beta"][None, :], (128, 1024))),
        "l1f_xm8": np.concatenate([np.asarray(x).reshape(1, 1) for x in xm],
                                  axis=1).astype(np.float32),
        "l1f_id16": _ident16(),
    }


def dram_l1_ffn(P):
    g = {}
    g["l1f_w"] = P.dram_in("l1f_w", (D, D), F32)
    g["l1f_gbc"] = P.dram_in("l1f_gbc", (128, 1024), F32)
    g["l1f_bbc"] = P.dram_in("l1f_bbc", (128, 1024), F32)
    g["l1f_xm8"] = P.dram_in("l1f_xm8", (1, 8), F32)
    g["l1f_id16"] = P.dram_in("l1f_id16", (128, 128), BF16)
    return g


def _bcast_scalar(P, c, src11, dst, onesr16, hi, lo, tag):
    vop(P, lambda e: e.tensor_copy(out=hi, in_=src11), [src11], [hi], tag=tag + "h")
    vop(P, lambda e: e.tensor_sub(out=lo, in0=src11, in1=hi), [src11, hi], [lo], tag=tag + "l")
    mm(P, c["pD"][:, 0:1], onesr16, hi, start=True, stop=False, tag=tag + "mh")
    mm(P, c["pD"][:, 0:1], onesr16, lo, start=False, stop=True, tag=tag + "ml")
    vop(P, lambda e: e.tensor_copy(out=dst, in_=c["pD"][:, 0:1]),
        [c["pD"][:, 0:1]], [dst], tag=tag + "ev")


def emit_l1_ffn(P, xin, xout, G):
    c = alloc_common(P)
    ensure_ident(P, c, G, "l1f_id16")
    wf = P.sbuf("l1f_wf", [128, 8192], F32)
    wq16 = P.sbuf("l1f_wq", [128, 8192], BF16)
    gbc = P.sbuf("l1f_gbc_s", [128, 1024], F32)
    bbc = P.sbuf("l1f_bbc_s", [128, 1024], F32)
    xm8 = P.sbuf("l1f_xm8_s", [1, 8], F32)
    xn32 = P.sbuf("l1f_xn", [128, 1024], F32)
    xq16 = P.sbuf("l1f_xq", [128, 1024], BF16)
    xqT = P.sbuf("l1f_xqT", [128, 1024], BF16)
    wi = P.sbuf("l1f_wi", [128, 1024], I32)
    wsum = P.sbuf("l1f_wsum", [128, 4], F32)
    onesr16 = P.sbuf("l1f_onr", [1, 128], BF16)
    sc11 = P.sbuf("l1f_sc", [1, 8], F32)
    hi11 = P.sbuf("l1f_hi", [1, 1], BF16)
    lo11 = P.sbuf("l1f_lo", [1, 1], BF16)
    hi128 = P.sbuf("l1f_h128", [128, 4], F32)
    mh16 = P.sbuf("l1f_mh", [128, 1], BF16)
    ml16 = P.sbuf("l1f_ml", [128, 1], BF16)
    P.dma("sp", gbc[:], G("l1f_gbc"))
    P.dma("sp", bbc[:], G("l1f_bbc"))
    P.dma("sp", xm8[:], G("l1f_xm8"))
    P.dma("sp", wf[:].rearrange("p (ki o) -> p ki o", o=1024),
          G("l1f_w").rearrange("(ki p) o -> p ki o", p=128), tag="ld_w")
    vop(P, lambda e: e.memset(onesr16[:], 1.0), [], [onesr16[:]])
    wfv = wf[:].rearrange("p (ki o) -> p ki o", o=1024)
    wqv = wq16[:].rearrange("p (ki o) -> p ki o", o=1024)

    for ki in range(NF):
        if ki == 0:
            vop(P, lambda e: e.tensor_reduce(out=wsum[:, 0:1], in_=wfv[:, 0, :], axis=AX.X,
                                             op=ALU.add, apply_absolute_value=True),
                [wfv[:, 0, :]], [wsum[:, 0:1]], tag="bsum0")
        else:
            vop(P, lambda e, ki=ki: e.tensor_reduce(out=wsum[:, 1:2], in_=wfv[:, ki, :],
                                                    axis=AX.X, op=ALU.add,
                                                    apply_absolute_value=True),
                [wfv[:, ki, :]], [wsum[:, 1:2]], tag=f"bsum{ki}")
            vop(P, lambda e: e.tensor_add(out=wsum[:, 0:1], in0=wsum[:, 0:1], in1=wsum[:, 1:2]),
                [wsum[:, 0:1], wsum[:, 1:2]], [wsum[:, 0:1]], tag=f"bacc{ki}")
    vop(P, lambda e: e.tensor_copy(out=mh16[:], in_=wsum[:, 0:1]), [wsum[:, 0:1]], [mh16[:]])
    vop(P, lambda e: e.tensor_sub(out=ml16[:], in0=wsum[:, 0:1], in1=mh16[:]),
        [wsum[:, 0:1], mh16[:]], [ml16[:]])
    onc16 = P.sbuf("l1f_onc", [128, 1], BF16)
    vop(P, lambda e: e.memset(onc16[:], 1.0), [], [onc16[:]])
    mm(P, c["pD"][0:1, 0:1], onc16[:], mh16[:], start=True, stop=False, tag="bsum_h")
    mm(P, c["pD"][0:1, 0:1], onc16[:], ml16[:], start=False, stop=True, tag="bsum_l")
    vop(P, lambda e: e.tensor_scalar(out=sc11[:, 0:1], in0=c["pD"][0:1, 0:1],
                                     scalar1=1.0 / (D * D), scalar2=None, op0=ALU.mult),
        [c["pD"][0:1, 0:1]], [sc11[:, 0:1]], tag="beta")
    vop(P, lambda e: e.tensor_scalar(out=sc11[:, 1:2], in0=sc11[:, 0:1], scalar1=EPS_Q,
                                     scalar2=None, op0=ALU.add), [sc11[:, 0:1]], [sc11[:, 1:2]])
    vop(P, lambda e: e.reciprocal(out=sc11[:, 2:3], in_=sc11[:, 1:2]),
        [sc11[:, 1:2]], [sc11[:, 2:3]])
    _bcast_scalar(P, c, sc11[:, 2:3], hi128[:, 0:1], onesr16[:], hi11[:], lo11[:], "rb")
    for ki in range(NF):
        vop(P, lambda e, ki=ki: e.tensor_scalar(out=c["sq"][:], in0=wfv[:, ki, :],
                scalar1=hi128[:, 0:1], scalar2=None, op0=ALU.mult),
            [wfv[:, ki, :], hi128[:, 0:1]], [c["sq"][:]], tag=f"wr{ki}")
        vop(P, lambda e: e.tensor_copy(out=wi[:], in_=c["sq"][:]), [c["sq"][:]], [wi[:]],
            tag=f"wi{ki}")
        vop(P, lambda e: e.tensor_copy(out=c["sq"][:], in_=wi[:]), [wi[:]], [c["sq"][:]],
            tag=f"wf{ki}")
        dst = wqv[:, ki, :]
        vop(P, lambda e, dst=dst: e.tensor_scalar(out=dst, in0=c["sq"][:],
                scalar1=-1.0, scalar2=1.0, op0=ALU.max, op1=ALU.min),
            [c["sq"][:]], [dst], tag=f"wc{ki}")
    vop(P, lambda e: e.reduce_max(out=sc11[:, 3:4], in_=xm8[:], axis=AX.X),
        [xm8[:]], [sc11[:, 3:4]], tag="gam")
    vop(P, lambda e: e.reciprocal(out=sc11[:, 4:5], in_=sc11[:, 3:4]),
        [sc11[:, 3:4]], [sc11[:, 4:5]])
    vop(P, lambda e: e.tensor_scalar(out=sc11[:, 5:6], in0=sc11[:, 4:5], scalar1=QB,
                                     scalar2=None, op0=ALU.mult), [sc11[:, 4:5]], [sc11[:, 5:6]])
    _bcast_scalar(P, c, sc11[:, 5:6], hi128[:, 1:2], onesr16[:], hi11[:], lo11[:], "ss")
    vop(P, lambda e: e.tensor_tensor(out=sc11[:, 6:7], in0=sc11[:, 3:4], in1=sc11[:, 0:1],
                                     op=ALU.mult), [sc11[:, 3:4], sc11[:, 0:1]], [sc11[:, 6:7]])
    vop(P, lambda e: e.tensor_scalar(out=sc11[:, 7:8], in0=sc11[:, 6:7], scalar1=1.0 / QB,
                                     scalar2=None, op0=ALU.mult), [sc11[:, 6:7]], [sc11[:, 7:8]])
    _bcast_scalar(P, c, sc11[:, 7:8], hi128[:, 2:3], onesr16[:], hi11[:], lo11[:], "s2")

    for tt in range(NT):
        xa = c["xtm"][tt % 2]
        ot = c["otm"][0]
        P.dma("sp", xa[:], xin[tt * 128:(tt + 1) * 128, :], tag=f"q_ldx{tt}")
        emit_ln_affine_tile(P, c, xa, gbc, bbc, xn32[:], tag=f"q_ln{tt}_")
        vop(P, lambda e: e.tensor_scalar(out=xn32[:], in0=xn32[:],
                scalar1=hi128[:, 1:2], scalar2=None, op0=ALU.mult),
            [xn32[:], hi128[:, 1:2]], [xn32[:]], tag=f"q_s{tt}")
        vop(P, lambda e: e.tensor_scalar(out=xq16[:], in0=xn32[:],
                scalar1=-(QB - EPS_Q), scalar2=QB - EPS_Q, op0=ALU.max, op1=ALU.min),
            [xn32[:]], [xq16[:]], tag=f"q_c{tt}")
        for k in range(NF):
            pb = c["pT0"] if k % 2 == 0 else c["pT1"]
            slot = (k // 2) * 128
            tr(P, pb[:, slot:slot + 128], xq16[:, k * 128:(k + 1) * 128], c["ident16"][:],
               tag=f"q_tr{tt}_{k}")
            dst = xqT[:, k * 128:(k + 1) * 128]
            vop(P, lambda e, pb=pb, slot=slot, dst=dst: e.tensor_copy(out=dst, in_=pb[:, slot:slot + 128]),
                [pb[:, slot:slot + 128]], [dst])
        for n in range(2):
            ps = c["pB"][:, n * 512:(n + 1) * 512]
            for k in range(NF):
                mm(P, ps, xqT[:, k * 128:(k + 1) * 128], wqv[:, k, n * 512:(n + 1) * 512],
                   start=(k == 0), stop=(k == NF - 1), tag=f"q_mm{tt}_{n}")
            vop(P, lambda e, ps=ps, n=n: e.tensor_scalar(out=xn32[:, n * 512:(n + 1) * 512],
                    in0=ps, scalar1=hi128[:, 2:3], scalar2=None, op0=ALU.mult),
                [ps, hi128[:, 2:3]], [xn32[:, n * 512:(n + 1) * 512]], tag=f"q_sc{tt}_{n}")
            vop(P, lambda e, xa=xa, n=n, ot=ot: e.tensor_tensor(
                    out=ot[:, n * 512:(n + 1) * 512], in0=xn32[:, n * 512:(n + 1) * 512],
                    in1=xa[:, n * 512:(n + 1) * 512], op=ALU.add),
                [xn32[:, n * 512:(n + 1) * 512], xa[:, n * 512:(n + 1) * 512]],
                [ot[:, n * 512:(n + 1) * 512]], tag=f"q_add{tt}_{n}")
        P.dma("act", xout[tt * 128:(tt + 1) * 128, :], ot[:], tag=f"q_st{tt}")


STAGE_FUNCS["l1_ffn"] = dict(prep=prep_l1_ffn, dram=dram_l1_ffn, emit=emit_l1_ffn, extra=[])



EPS_PERF = 1e-6


def prep_l2_attn(inputs):
    wf = np.asarray(inputs["L2_attn_W_feat"], np.float32)
    waug = np.concatenate([wf, -np.ones((2, 128), np.float32)], axis=0)
    return {
        "l2a_wq": np.ascontiguousarray(inputs["L2_attn_Wq"]),
        "l2a_wk": np.ascontiguousarray(inputs["L2_attn_Wk"]),
        "l2a_wv": np.ascontiguousarray(inputs["L2_attn_Wv"]),
        "l2a_wo": np.ascontiguousarray(inputs["L2_attn_Wo"]),
        "l2a_waug": waug,
        "l2a_id16": _ident16(),
        "l2a_triu": _triu_incl16(),
    }


def dram_l2_attn(P):
    g = {}
    for nm in ("l2a_wq", "l2a_wk", "l2a_wv", "l2a_wo"):
        g[nm] = P.dram_in(nm, (D, D), F32)
    g["l2a_waug"] = P.dram_in("l2a_waug", (66, 128), F32)
    g["l2a_id16"] = P.dram_in("l2a_id16", (128, 128), BF16)
    g["l2a_triu"] = P.dram_in("l2a_triu", (128, 128), BF16)
    g["l2a_qTd"] = P.dram_tmp("l2a_qTd", (16, 64, S), BF16)
    g["l2a_kTd"] = P.dram_tmp("l2a_kTd", (16, 64, S), BF16)
    return g


def emit_l2_attn(P, xin, xout, G):
    c = alloc_common(P)
    ensure_ident(P, c, G, "l2a_id16")
    big = P.sbuf("l2a_big", [128, NF * S], BF16)
    vsb = P.sbuf("l2a_vsb", [128, NT * 1024], BF16)
    w16 = P.sbuf("l2a_w16", [128, 8192], BF16)
    wv_ = w16[:].rearrange("p (ki o) -> p ki o", o=1024)
    triu = P.sbuf("l2a_triu_s", [128, 128], BF16)
    waug16 = P.sbuf("l2a_waug_s", [66, 128], BF16)
    qbuf = P.sbuf("l2a_qbuf", [66, S], BF16)
    kbuf = P.sbuf("l2a_kbuf", [66, S], BF16)
    qp16 = P.sbuf("l2a_qp", [128, S], BF16)
    kp16 = P.sbuf("l2a_kp", [128, S], BF16)
    sq32 = P.sbuf("l2a_sq32", [64, 512], F32)
    q2h = P.sbuf("l2a_q2h", [64, 512], BF16)[:]
    q2l = P.sbuf("l2a_q2l", [64, 512], BF16)[:]
    sqh = w16[0:1, 4096:8192]
    sql = w16[0:1, 0:4096]
    ones64 = P.sbuf("l2a_on64", [64, 1], BF16)
    s16 = P.sbuf("l2a_s16", [128, 65], BF16)
    sf32 = P.sbuf("l2a_sf32", [128, 65], F32)
    vaug = P.sbuf("l2a_vaug", [128, 65], BF16)
    at16 = P.sbuf("l2a_at16", [128, 128], BF16)
    kpt16 = P.sbuf("l2a_kpt", [128, 128], BF16)
    y16 = P.sbuf("l2a_y16", [128, 64], BF16)
    den32 = P.sbuf("l2a_den", [128, 4], F32)
    atT16 = P.sbuf("l2a_atT", [128, 1024], BF16)
    P.dma("sp", triu[:], G("l2a_triu"))
    P.dma("pool", waug16[:], G("l2a_waug"), tag="ld_waug")
    vop(P, lambda e: e.memset(ones64[:], 1.0), [], [ones64[:]])
    vop(P, lambda e: e.memset(vaug[:, 64:65], 1.0), [], [vaug[:, 64:65]])

    emit_ln(P, c, xin, big)

    P.dma("pool", wv_, G("l2a_wv").rearrange("(ki p) o -> p ki o", p=128), tag="ld_wv")
    for tt in range(NT):
        for n in range(2):
            ps = c["pA"][:, n * 512:(n + 1) * 512]
            for ki in range(NF):
                mm(P, ps, big[:, ki * S + tt * 128:ki * S + (tt + 1) * 128],
                   wv_[:, ki, n * 512:(n + 1) * 512],
                   start=(ki == 0), stop=(ki == NF - 1), tag=f"v_mm{tt}_{n}")
            dst = vsb[:, tt * 1024 + n * 512: tt * 1024 + (n + 1) * 512]
            vop(P, lambda e, ps=ps, dst=dst: e.tensor_copy(out=dst, in_=ps),
                [ps], [dst], tag=f"v_ev{tt}_{n}")
    for qk, (wname, dnam) in enumerate([("l2a_wq", "l2a_qTd"), ("l2a_wk", "l2a_kTd")]):
        P.dma("pool", wv_, G(wname).rearrange("(ki p) o -> p ki o", p=128), tag=f"ld_w{qk}")
        dstd = G(dnam)
        for oc in range(NF):
            for tb in range(8):
                r = (oc * 8 + tb) % 2
                ps = c["pB"][:, r * 512:(r + 1) * 512]
                for ki in range(NF):
                    mm(P, ps, wv_[:, ki, oc * 128:(oc + 1) * 128],
                       big[:, ki * S + tb * 512:ki * S + (tb + 1) * 512],
                       start=(ki == 0), stop=(ki == NF - 1), tag=f"qk{qk}_mm{oc}_{tb}")
                rp = at16 if r == 0 else kpt16
                stg = qp16[:, 0:512] if r == 0 else qp16[:, 512:1024]
                vop(P, lambda e, ps=ps, stg=stg: e.tensor_copy(out=stg, in_=ps),
                    [ps], [stg], tag=f"qk{qk}_ev{oc}_{tb}")
                P.dma("act", dstd[2 * oc, :, tb * 512:(tb + 1) * 512], stg[0:64, :],
                      tag=f"qk{qk}_st0_{oc}_{tb}")
                P.dma("act", dstd[2 * oc + 1, :, tb * 512:(tb + 1) * 512], stg[64:128, :],
                      tag=f"qk{qk}_st1_{oc}_{tb}")

    lnc = P.constant(float(-np.log(np.sqrt(128.0))))
    epsc = P.constant(EPS_PERF)
    for h in range(16):
        P.dma("sp", qbuf[0:64, :], G("l2a_qTd")[h], tag=f"hd_ldq{h}")
        P.dma("sp", kbuf[0:64, :], G("l2a_kTd")[h], tag=f"hd_ldk{h}")
        for which, buf, phi in ((0, qbuf, qp16), (1, kbuf, kp16)):
            for bl in range(8):
                sl = slice(bl * 512, (bl + 1) * 512)
                sop(P, lambda e, buf=buf, sl=sl: e.activation(out=sq32[:], in_=buf[0:64, sl],
                                                              func=AF.Square),
                    [buf[0:64, sl]], [sq32[:]], tag=f"ph_sq{h}_{which}_{bl}")
                vop(P, lambda e: e.tensor_copy(out=q2h, in_=sq32[:]), [sq32[:]], [q2h])
                vop(P, lambda e: e.tensor_sub(out=q2l, in0=sq32[:], in1=q2h),
                    [sq32[:], q2h], [q2l])
                mm(P, c["pC"][0:1, 0:512], ones64[:], q2h, start=True, stop=False,
                   tag=f"ph_mh{h}_{which}_{bl}")
                mm(P, c["pC"][0:1, 0:512], ones64[:], q2l, start=False, stop=True,
                   tag=f"ph_ml{h}_{which}_{bl}")
                vop(P, lambda e, sl=sl: e.tensor_scalar(out=sqh[:, sl], in0=c["pC"][0:1, 0:512],
                        scalar1=0.5, scalar2=None, op0=ALU.mult),
                    [c["pC"][0:1, 0:512]], [sqh[:, sl]], tag=f"ph_hh{h}_{which}_{bl}")
                vop(P, lambda e, sl=sl: e.scalar_tensor_tensor(out=sql[:, sl],
                        in0=c["pC"][0:1, 0:512], scalar=0.5, in1=sqh[:, sl],
                        op0=ALU.mult, op1=ALU.subtract),
                    [c["pC"][0:1, 0:512], sqh[:, sl]], [sql[:, sl]], tag=f"ph_ll{h}_{which}_{bl}")
            P.dma("pool", buf[64:65, :], sqh, tag=f"ph_sth{h}_{which}")
            P.dma("pool", buf[65:66, :], sql, tag=f"ph_stl{h}_{which}")
            for bl in range(8):
                sl = slice(bl * 512, (bl + 1) * 512)
                ps = c["pB"][:, (bl % 2) * 512:(bl % 2) * 512 + 512]
                mm(P, ps, waug16[:], buf[:, sl], tag=f"ph_mm{h}_{which}_{bl}")
                sop(P, lambda e, ps=ps, phi=phi, sl=sl, lnc=lnc: e.activation(
                        out=phi[:, sl], in_=ps, func=AF.Exp, bias=lnc),
                    [ps, lnc], [phi[:, sl]], tag=f"ph_exp{h}_{which}_{bl}")
        for ct in range(NT):
            csl = slice(ct * 128, (ct + 1) * 128)
            if ct == 0:
                vop(P, lambda e: e.memset(s16[:], 0.0), [], [s16[:]], tag=f"sc_z{h}")
                vop(P, lambda e: e.memset(sf32[:], 0.0), [], [sf32[:]], tag=f"sc_zf{h}")
            else:
                vop(P, lambda e: e.tensor_copy(out=s16[:], in_=sf32[:]),
                    [sf32[:]], [s16[:]], tag=f"sc_s{h}_{ct}")
            vsrc = vsb[:, ct * 1024 + h * 64: ct * 1024 + h * 64 + 64]
            vop(P, lambda e, vsrc=vsrc: e.tensor_copy(out=vaug[:, 0:64], in_=vsrc),
                [vsrc], [vaug[:, 0:64]], tag=f"sc_v{h}_{ct}")
            mm(P, c["pA"][:, 0:65], qp16[:, csl], s16[:], start=True, stop=False,
               tag=f"sc_n1{h}_{ct}")
            mm(P, c["pA"][:, 512:640], kp16[:, csl], qp16[:, csl], tag=f"sc_at{h}_{ct}")
            vop(P, lambda e: e.tensor_tensor(out=at16[:], in0=c["pA"][:, 512:640],
                    in1=triu[:], op=ALU.mult),
                [c["pA"][:, 512:640], triu[:]], [at16[:]], tag=f"sc_m{h}_{ct}")
            mm(P, c["pA"][:, 0:65], at16[:], vaug[:], start=False, stop=True,
               tag=f"sc_n2{h}_{ct}")
            vop(P, lambda e, epsc=epsc: e.tensor_scalar(out=den32[:, 0:1],
                    in0=c["pA"][:, 64:65], scalar1=epsc, scalar2=None, op0=ALU.add),
                [c["pA"][:, 64:65], epsc], [den32[:, 0:1]], tag=f"sc_d{h}_{ct}")
            vop(P, lambda e: e.reciprocal(out=den32[:, 1:2], in_=den32[:, 0:1]),
                [den32[:, 0:1]], [den32[:, 1:2]], tag=f"sc_r{h}_{ct}")
            vop(P, lambda e: e.tensor_scalar(out=y16[:], in0=c["pA"][:, 0:64],
                    scalar1=den32[:, 1:2], scalar2=None, op0=ALU.mult),
                [c["pA"][:, 0:64], den32[:, 1:2]], [y16[:]], tag=f"sc_y{h}_{ct}")
            dst = big[:, ct * 1024 + h * 64: ct * 1024 + h * 64 + 64]
            vop(P, lambda e, dst=dst: e.tensor_copy(out=dst, in_=y16[:]),
                [y16[:]], [dst], tag=f"sc_yw{h}_{ct}")
            import os as _os2
            if _os2.environ.get("L2A_DBG") == "1" and h == 15 and ct == 0:
                dat = P.dram_out("dbg_at", (128, 128), BF16)
                dva = P.dram_out("dbg_va", (128, 65), BF16)
                dvs = P.dram_out("dbg_vs", (128, 64), BF16)
                dnm = P.dram_out("dbg_nm", (128, 65), BF16)
                dnm16 = P.sbuf("l2a_dnm16", [128, 65], BF16)
                dva16 = P.sbuf("l2a_dva16", [128, 65], BF16)
                dvs16 = P.sbuf("l2a_dvs16", [128, 64], BF16)
                dat16 = P.sbuf("l2a_dat16", [128, 128], BF16)
                vop(P, lambda e: e.tensor_copy(out=dat16[:], in_=at16[:]),
                    [at16[:]], [dat16[:]], tag="dbg_atev")
                vop(P, lambda e: e.tensor_copy(out=dva16[:], in_=vaug[:]),
                    [vaug[:]], [dva16[:]], tag="dbg_vaev")
                vop(P, lambda e: e.tensor_copy(out=dvs16[:], in_=vsb[:, 960:1024]),
                    [vsb[:, 960:1024]], [dvs16[:]], tag="dbg_vsev")
                vop(P, lambda e: e.tensor_copy(out=dnm16[:], in_=c["pA"][:, 0:65]),
                    [c["pA"][:, 0:65]], [dnm16[:]], tag="dbg_nmev")
                P.dma("act", dat, dat16[:])
                P.dma("act", dva, dva16[:])
                P.dma("act", dvs, dvs16[:])
                P.dma("act", dnm, dnm16[:])
            tr(P, c["pT0"][:, 0:128], kp16[:, csl], c["ident16"][:], tag=f"sc_tr{h}_{ct}")
            vop(P, lambda e: e.tensor_copy(out=kpt16[:], in_=c["pT0"][:, 0:128]),
                [c["pT0"][:, 0:128]], [kpt16[:]], tag=f"sc_te{h}_{ct}")
            mm(P, c["pC"][:, 0:65], kpt16[:], vaug[:], tag=f"sc_su{h}_{ct}")
            vop(P, lambda e: e.tensor_tensor(out=sf32[:], in0=sf32[:], in1=c["pC"][:, 0:65],
                                             op=ALU.add),
                [sf32[:], c["pC"][:, 0:65]], [sf32[:]], tag=f"sc_sa{h}_{ct}")

    import os as _os
    if _os.environ.get("L2A_DBG") == "1":
        dq = P.dram_out("dbg_qp", (128, S), BF16)
        dk = P.dram_out("dbg_kp", (128, S), BF16)
        dy = P.dram_out("dbg_y", (128, 1024), BF16)
        dqb = P.dram_out("dbg_qbuf", (66, S), BF16)
        P.dma("act", dq, qp16[:])
        P.dma("act", dk, kp16[:])
        P.dma("act", dy, big[:, 0:1024])
        P.dma("act", dqb, qbuf[:])

    P.dma("pool", wv_, G("l2a_wo").rearrange("(ki p) o -> p ki o", p=128), tag="ld_wo")
    for tt in range(NT):
        xa = c["xtm"][tt % 2]
        ot = c["otm"][tt % 2]
        for k in range(NF):
            pb = c["pT0"] if k % 2 == 0 else c["pT1"]
            slot = (k // 2) * 128
            tr(P, pb[:, slot:slot + 128], big[:, tt * 1024 + k * 128: tt * 1024 + (k + 1) * 128],
               c["ident16"][:], tag=f"wo_tr{tt}_{k}")
            dst = atT16[:, k * 128:(k + 1) * 128]
            vop(P, lambda e, pb=pb, slot=slot, dst=dst: e.tensor_copy(out=dst, in_=pb[:, slot:slot + 128]),
                [pb[:, slot:slot + 128]], [dst])
        P.dma("sp", xa[:], xin[tt * 128:(tt + 1) * 128, :], tag=f"wo_ldx{tt}")
        for n in range(2):
            ps = c["pB"][:, n * 512:(n + 1) * 512]
            for k in range(NF):
                mm(P, ps, atT16[:, k * 128:(k + 1) * 128], wv_[:, k, n * 512:(n + 1) * 512],
                   start=(k == 0), stop=(k == NF - 1), tag=f"wo_mm{tt}_{n}")
            vop(P, lambda e, ps=ps, xa=xa, n=n, ot=ot: e.tensor_tensor(
                    out=ot[:, n * 512:(n + 1) * 512], in0=ps,
                    in1=xa[:, n * 512:(n + 1) * 512], op=ALU.add),
                [ps, xa[:, n * 512:(n + 1) * 512]], [ot[:, n * 512:(n + 1) * 512]],
                tag=f"wo_add{tt}_{n}")
        P.dma("act", xout[tt * 128:(tt + 1) * 128, :], ot[:], tag=f"wo_st{tt}")


STAGE_FUNCS["l2_attn"] = dict(prep=lambda inputs, carry: prep_l2_attn(inputs),
                              dram=dram_l2_attn, emit=emit_l2_attn, extra=[])



L2F_DEPTH = 6
WIN = 256
SP1 = S + 8


def _l2f_rotary_host():
    half = np.arange(0, 64, 2, dtype=np.float64)
    inv_freq = 1.0 / 10000.0 ** (half / 64.0)
    t = np.arange(S, dtype=np.float64)
    fr = t[:, None] * inv_freq[None, :]
    freqs = np.concatenate([fr, fr], axis=1)
    base_scale = (half + 0.4 * 64) / (1.4 * 64)
    power = (t - S // 2) / (WIN // 2)
    sc = base_scale[None, :] ** power[:, None]
    scale = np.concatenate([sc, sc], axis=1)
    cos = np.cos(freqs); sin = np.sin(freqs)
    return (np.ascontiguousarray((cos * scale).astype(np.float32)),
            np.ascontiguousarray((sin * scale).astype(np.float32)),
            np.ascontiguousarray((cos / scale).astype(np.float32)),
            np.ascontiguousarray((sin / scale).astype(np.float32)))


def prep_l2_ffn(inputs, carry):
    qc, qs, kc, ks = _l2f_rotary_host()
    W1 = np.asarray(inputs["L2_ffn_W1"], np.float32)
    W2 = np.asarray(inputs["L2_ffn_W2"], np.float32)
    b1 = np.asarray(inputs["L2_ffn_b1"], np.float32)
    HP = 2816
    w1a = np.zeros((6, D, HP), np.float32); w1a[:, :, :2730] = W1[:, :, :2730]
    w1g = np.zeros((6, D, HP), np.float32); w1g[:, :, :2730] = W1[:, :, 2730:]
    w2p = np.zeros((6, HP, D), np.float32); w2p[:, :2730, :] = W2
    b1a = np.zeros((6, HP), np.float32); b1a[:, :2730] = b1[:, :2730]
    b1g = np.zeros((6, HP), np.float32); b1g[:, :2730] = b1[:, 2730:]
    return {
        "l2f_wqkv": np.ascontiguousarray(inputs["L2_ffn_Wqkv"]),
        "l2f_wo": np.ascontiguousarray(inputs["L2_ffn_Wo"]),
        "l2f_w1a": np.ascontiguousarray(w1a),
        "l2f_w1g": np.ascontiguousarray(w1g),
        "l2f_w2": np.ascontiguousarray(w2p),
        "l2f_b1a": np.ascontiguousarray(b1a.reshape(6, 22, 128).transpose(0, 2, 1)),
        "l2f_b1g": np.ascontiguousarray(b1g.reshape(6, 22, 128).transpose(0, 2, 1)),
        "l2f_b2bc": np.ascontiguousarray(np.broadcast_to(
            np.asarray(inputs["L2_ffn_b2"], np.float32)[:, None, :], (6, 128, 1024))),
        "l2f_lng": np.ascontiguousarray(np.broadcast_to(
            np.asarray(inputs["L2_ffn_ln_g"], np.float32)[:, None, :], (6, 128, 1024))),
        "l2f_lnb": np.ascontiguousarray(np.broadcast_to(
            np.asarray(inputs["L2_ffn_ln_b"], np.float32)[:, None, :], (6, 128, 1024))),
        "l2f_qg": np.ascontiguousarray(np.broadcast_to(
            np.asarray(inputs["L2_ffn_q_scale"], np.float32)[:, None, :], (6, 128, 64))),
        "l2f_kg": np.ascontiguousarray(np.broadcast_to(
            np.asarray(inputs["L2_ffn_k_scale"], np.float32)[:, None, :], (6, 128, 64))),
        "l2f_qc": qc, "l2f_qs": qs, "l2f_kc": kc, "l2f_ks": ks,
        "l2f_id16": _ident16(), "l2f_tri": _tri16(), "l2f_triu": _triu_incl16(),
    }


def dram_l2_ffn(P):
    g = {}
    g["l2f_wqkv"] = P.dram_in("l2f_wqkv", (6, D, 3 * D), F32)
    g["l2f_wo"] = P.dram_in("l2f_wo", (6, D, D), F32)
    g["l2f_w1a"] = P.dram_in("l2f_w1a", (6, D, 2816), F32)
    g["l2f_w1g"] = P.dram_in("l2f_w1g", (6, D, 2816), F32)
    g["l2f_w2"] = P.dram_in("l2f_w2", (6, 2816, D), F32)
    g["l2f_b1a"] = P.dram_in("l2f_b1a", (6, 128, 22), F32)
    g["l2f_b1g"] = P.dram_in("l2f_b1g", (6, 128, 22), F32)
    g["l2f_b2bc"] = P.dram_in("l2f_b2bc", (6, 128, 1024), F32)
    g["l2f_lng"] = P.dram_in("l2f_lng", (6, 128, 1024), F32)
    g["l2f_lnb"] = P.dram_in("l2f_lnb", (6, 128, 1024), F32)
    g["l2f_qg"] = P.dram_in("l2f_qg", (6, 128, 64), F32)
    g["l2f_kg"] = P.dram_in("l2f_kg", (6, 128, 64), F32)
    for nm in ("l2f_qc", "l2f_qs", "l2f_kc", "l2f_ks"):
        g[nm] = P.dram_in(nm, (S, 64), F32)
    g["l2f_id16"] = P.dram_in("l2f_id16", (128, 128), BF16)
    g["l2f_tri"] = P.dram_in("l2f_tri", (128, 128), BF16)
    g["l2f_triu"] = P.dram_in("l2f_triu", (128, 128), BF16)
    g["l2f_xi"] = P.dram_tmp("l2f_xi", (S, D), F32)
    return g


def emit_l2_ffn(P, xin, xout, G):
    c = alloc_common(P, psums=False)
    ensure_ident(P, c, G, "l2f_id16")
    pS = P.psum("l2f_pS", [128, 1536], F32)
    pV = P.psum("l2f_pV", [128, 512], F32)
    pQ = P.psum("l2f_pQ", [128, 1024], F32)
    pT0 = P.psum("l2f_pT0", [128, 1024], BF16)
    pT1 = P.psum("l2f_pT1", [128, 1024], BF16)
    c["pT0"] = pT0; c["pT1"] = pT1
    big2 = P.sbuf("l2f_big2", [128, NF * SP1], BF16)
    pool = P.sbuf("l2f_pool", [128, 49152], BF16)
    wqkv = pool[:, 0:24576].rearrange("p (ki o) -> p ki o", o=3072)
    wo16 = pool[:, 24576:32768].rearrange("p (ki o) -> p ki o", o=1024)
    qrT = pool[:, 32768:34816]
    krT = [pool[:, 34816:36864], pool[:, 36864:38912]]
    vw = [pool[:, 38912:40960], pool[:, 40960:43008]]
    yw = pool[:, 43008:45056]
    w2v = pool[:, 0:22528].rearrange("p (j o) -> p j o", o=1024)
    hgb = pool[:, 22528:45056].rearrange("p (j t) -> p j t", t=1024)
    w1b = [P.sbuf(f"l2f_w1b{i}", [128, 2048], BF16) for i in range(2)]
    tri16 = P.sbuf("l2f_tri_s", [128, 128], BF16)
    triu16 = P.sbuf("l2f_triu_s", [128, 128], BF16)
    qg = P.sbuf("l2f_qg_s", [128, 64], F32)
    kg = P.sbuf("l2f_kg_s", [128, 64], F32)
    rotq = [P.sbuf(f"l2f_rq{i}", [128, 64], F32) for i in range(2)]
    rotk = [P.sbuf(f"l2f_rk{i}", [128, 64], F32) for i in range(2)]
    lng = P.sbuf("l2f_lng_s", [128, 1024], BF16)
    lnb = P.sbuf("l2f_lnb_s", [128, 1024], BF16)
    b2bc = P.sbuf("l2f_b2_s", [128, 1024], BF16)
    b1a = P.sbuf("l2f_b1a_s", [128, 22], F32)
    b1g = P.sbuf("l2f_b1g_s", [128, 22], F32)
    sqb = c["sq"]
    qn16 = P.sbuf("l2f_qn", [128, 1024], BF16)
    rot16 = P.sbuf("l2f_rot", [128, 1024], BF16)
    qr16 = P.sbuf("l2f_qr", [128, 1024], BF16)
    st2 = P.sbuf("l2f_st2", [128, 32], F32)
    p16 = P.sbuf("l2f_p16", [128, 768], BF16)
    pt16 = P.sbuf("l2f_pt16", [128, 768], BF16)
    ge16 = P.sbuf("l2f_ge16", [128, 512], BF16)
    P.dma("sp", tri16[:], G("l2f_tri"))
    P.dma("sp", triu16[:], G("l2f_triu"))

    st = c["stats"]

    def ln_tile(P, xa, out16, eps, affine=None, tag=""):
        sop(P, lambda e, xa=xa: e.activation(out=c["sq"][:], in_=xa[:], func=AF.Square,
                                             accum_out=st[:, 1:2]), [xa[:]],
            [c["sq"][:], st[:, 1:2]], tag=tag + "sq")
        vop(P, lambda e, xa=xa: e.reduce_sum(out=st[:, 0:1], in_=xa[:], axis=AX.X),
            [xa[:]], [st[:, 0:1]], tag=tag + "su")
        vop(P, lambda e: e.tensor_scalar(out=st[:, 2:3], in0=st[:, 0:1], scalar1=1.0 / D,
                                         scalar2=None, op0=ALU.mult), [st[:, 0:1]], [st[:, 2:3]])
        vop(P, lambda e: e.tensor_scalar(out=st[:, 3:4], in0=st[:, 1:2], scalar1=1.0 / D,
                                         scalar2=None, op0=ALU.mult), [st[:, 1:2]], [st[:, 3:4]])
        vop(P, lambda e: e.tensor_tensor(out=st[:, 4:5], in0=st[:, 2:3], in1=st[:, 2:3],
                                         op=ALU.mult), [st[:, 2:3]], [st[:, 4:5]])
        vop(P, lambda e: e.tensor_sub(out=st[:, 5:6], in0=st[:, 3:4], in1=st[:, 4:5]),
            [st[:, 3:4], st[:, 4:5]], [st[:, 5:6]])
        epsap = P.constant(eps)
        sop(P, lambda e, epsap=epsap: e.activation(out=st[:, 6:7], in_=st[:, 5:6],
                                                   func=AF.Sqrt, bias=epsap),
            [st[:, 5:6], epsap], [st[:, 6:7]])
        vop(P, lambda e: e.reciprocal(out=st[:, 7:8], in_=st[:, 6:7]), [st[:, 6:7]], [st[:, 7:8]])
        vop(P, lambda e: e.tensor_tensor(out=st[:, 8:9], in0=st[:, 2:3], in1=st[:, 7:8],
                                         op=ALU.mult), [st[:, 2:3], st[:, 7:8]], [st[:, 8:9]])
        vop(P, lambda e: e.tensor_scalar(out=st[:, 9:10], in0=st[:, 8:9], scalar1=-1.0,
                                         scalar2=None, op0=ALU.mult), [st[:, 8:9]], [st[:, 9:10]])
        if affine is None:
            vop(P, lambda e, xa=xa, out16=out16: e.tensor_scalar(out=out16, in0=xa[:],
                    scalar1=st[:, 7:8], scalar2=st[:, 9:10], op0=ALU.mult, op1=ALU.add),
                [xa[:], st[:, 7:8], st[:, 9:10]], [out16], tag=tag + "n")
        else:
            gg, bb = affine
            vop(P, lambda e, xa=xa: e.tensor_scalar(out=c["sq"][:], in0=xa[:],
                    scalar1=st[:, 7:8], scalar2=st[:, 9:10], op0=ALU.mult, op1=ALU.add),
                [xa[:], st[:, 7:8], st[:, 9:10]], [c["sq"][:]], tag=tag + "n")
            vop(P, lambda e, gg=gg: e.tensor_mul(out=c["sq"][:], in0=c["sq"][:], in1=gg),
                [c["sq"][:], gg], [c["sq"][:]], tag=tag + "g")
            vop(P, lambda e, bb=bb, out16=out16: e.tensor_tensor(out=out16, in0=c["sq"][:],
                    in1=bb, op=ALU.add),
                [c["sq"][:], bb], [out16], tag=tag + "b")

    def to_big2(P, src16, tt, tag=""):
        for k in range(NF):
            pb = pT0 if k % 2 == 0 else pT1
            slot = (k // 2) * 128
            tr(P, pb[:, slot:slot + 128], src16[:, k * 128:(k + 1) * 128], c["ident16"][:],
               tag=f"{tag}tr{tt}_{k}")
            dst = big2[:, k * SP1 + 1 + tt * 128: k * SP1 + 1 + (tt + 1) * 128]
            vop(P, lambda e, pb=pb, slot=slot, dst=dst: e.tensor_copy(out=dst, in_=pb[:, slot:slot + 128]),
                [pb[:, slot:slot + 128]], [dst], tag=f"{tag}te{tt}_{k}")

    xi = G("l2f_xi")

    for tt in range(NT):
        xa = c["xtm"][tt % 2]
        P.dma("sp", xa[:], xin[tt * 128:(tt + 1) * 128, :], tag=f"in_ld{tt}")
        ln_tile(P, xa, c["lntile"][tt % 2][:], 1e-5, tag=f"in_ln{tt}_")
        ot = c["otm"][tt % 2]
        vop(P, lambda e, xa=xa, ot=ot: e.tensor_scalar(out=ot[:], in0=xa[:],
                scalar1=st[:, 7:8], scalar2=st[:, 9:10], op0=ALU.mult, op1=ALU.add),
            [xa[:], st[:, 7:8], st[:, 9:10]], [ot[:]], tag=f"in_f{tt}")
        P.dma("act", xi[tt * 128:(tt + 1) * 128, :], ot[:], tag=f"in_st{tt}")

    for L in range(L2F_DEPTH):
        for k in range(NF):
            vop(P, lambda e, k=k: e.memset(big2[:, k * SP1:k * SP1 + 1], 0.0),
                [], [big2[:, k * SP1:k * SP1 + 1]], tag=f"L{L}a_z{k}")
        P.dma("pool", wqkv, G("l2f_wqkv")[L].rearrange("(ki p) o -> p ki o", p=128),
              tag=f"L{L}_ldqkv")
        P.dma("pool", wo16, G("l2f_wo")[L].rearrange("(ki p) o -> p ki o", p=128),
              tag=f"L{L}_ldwo")
        P.dma("sp", qg[:], G("l2f_qg")[L], tag=f"L{L}_ldqg")
        P.dma("sp", kg[:], G("l2f_kg")[L], tag=f"L{L}_ldkg")
        for tt in range(NT):
            xa = c["xtm"][tt % 2]
            if tt == 0:
                P.dma("sp", xa[:, 0:512], xi[0:128, 0:512], tag=f"L{L}a_ld{tt}a")
                vop(P, lambda e, xa=xa: e.memset(xa[0:1, 512:1024], 0.0),
                    [], [xa[0:1, 512:1024]], tag=f"L{L}a_z0")
                P.dma("sp", xa[1:128, 512:1024], xi[0:127, 512:1024], tag=f"L{L}a_ld{tt}b")
            else:
                P.dma("sp", xa[:, 0:512], xi[tt * 128:(tt + 1) * 128, 0:512],
                      tag=f"L{L}a_ld{tt}a")
                P.dma("sp", xa[:, 512:1024], xi[tt * 128 - 1:(tt + 1) * 128 - 1, 512:1024],
                      tag=f"L{L}a_ld{tt}b")
            lt = c["lntile"][tt % 2]
            vop(P, lambda e, xa=xa, lt=lt: e.tensor_copy(out=lt[:], in_=xa[:]),
                [xa[:]], [lt[:]], tag=f"L{L}a_c{tt}")
            to_big2(P, lt, tt, tag=f"L{L}a_")
        for w in range(16):
            par = w % 2
            for t2 in range(2):
                tt = w * 2 + t2
                for part in range(3):
                    for n in range(2):
                        ps = pQ[:, n * 512:(n + 1) * 512]
                        for ki in range(NF):
                            mm(P, ps, big2[:, ki * SP1 + 1 + tt * 128: ki * SP1 + 1 + (tt + 1) * 128],
                               wqkv[:, ki, part * 1024 + n * 512: part * 1024 + (n + 1) * 512],
                               start=(ki == 0), stop=(ki == NF - 1),
                               tag=f"L{L}w{w}t{t2}p{part}n{n}")
                        if part == 2:
                            dst = vw[par][:, t2 * 1024 + n * 512: t2 * 1024 + (n + 1) * 512]
                            vop(P, lambda e, ps=ps, dst=dst: e.tensor_copy(out=dst, in_=ps),
                                [ps], [dst], tag=f"L{L}w{w}t{t2}vev{n}")
                        else:
                            dst = qn16[:, n * 512:(n + 1) * 512]
                            vop(P, lambda e, ps=ps, dst=dst: e.tensor_copy(out=dst, in_=ps),
                                [ps], [dst], tag=f"L{L}w{w}t{t2}qev{part}_{n}")
                    if part == 2:
                        continue
                    sop(P, lambda e: e.activation(out=sqb[:], in_=qn16[:], func=AF.Square),
                        [qn16[:]], [sqb[:]], tag=f"L{L}w{w}t{t2}sq{part}")
                    vop(P, lambda e: e.tensor_reduce(
                            out=st2[:, 0:16].unsqueeze(2),
                            in_=sqb[:].rearrange("p (h d) -> p h d", d=64), axis=AX.X,
                            op=ALU.add),
                        [sqb[:]], [st2[:, 0:16]], tag=f"L{L}w{w}t{t2}ss{part}")
                    sop(P, lambda e: e.activation(out=st2[:, 16:32], in_=st2[:, 0:16],
                                                  func=AF.Sqrt),
                        [st2[:, 0:16]], [st2[:, 16:32]], tag=f"L{L}w{w}t{t2}sr{part}")
                    vop(P, lambda e: e.tensor_scalar(out=st2[:, 16:32], in0=st2[:, 16:32],
                            scalar1=1e-12, scalar2=None, op0=ALU.max),
                        [st2[:, 16:32]], [st2[:, 16:32]], tag=f"L{L}w{w}t{t2}sm{part}")
                    vop(P, lambda e: e.reciprocal(out=st2[:, 0:16], in_=st2[:, 16:32]),
                        [st2[:, 16:32]], [st2[:, 0:16]], tag=f"L{L}w{w}t{t2}sc{part}")
                    gain = qg if part == 0 else kg
                    vop(P, lambda e: e.tensor_tensor(
                            out=qn16[:].rearrange("p (h d) -> p h d", d=64),
                            in0=qn16[:].rearrange("p (h d) -> p h d", d=64),
                            in1=st2[:, 0:16].unsqueeze(2).broadcast_to((128, 16, 64)),
                            op=ALU.mult),
                        [qn16[:], st2[:, 0:16]], [qn16[:]], tag=f"L{L}w{w}t{t2}nm{part}")
                    vop(P, lambda e, gain=gain: e.tensor_tensor(
                            out=qn16[:].rearrange("p (h d) -> p h d", d=64),
                            in0=qn16[:].rearrange("p (h d) -> p h d", d=64),
                            in1=gain[:].unsqueeze(1).rearrange("p o d -> p o d").broadcast_to((128, 16, 64)),
                            op=ALU.mult),
                        [qn16[:], gain[:]], [qn16[:]], tag=f"L{L}w{w}t{t2}gn{part}")
                    vop(P, lambda e: e.tensor_scalar(
                            out=rot16[:].rearrange("p (h d) -> p h d", d=64)[:, :, 0:32],
                            in0=qn16[:].rearrange("p (h d) -> p h d", d=64)[:, :, 32:64],
                            scalar1=-1.0, scalar2=None, op0=ALU.mult),
                        [qn16[:]], [rot16[:]], tag=f"L{L}w{w}t{t2}r1{part}")
                    vop(P, lambda e: e.tensor_copy(
                            out=rot16[:].rearrange("p (h d) -> p h d", d=64)[:, :, 32:64],
                            in_=qn16[:].rearrange("p (h d) -> p h d", d=64)[:, :, 0:32]),
                        [qn16[:]], [rot16[:]], tag=f"L{L}w{w}t{t2}r2{part}")
                    rbuf = rotq if part == 0 else rotk
                    cn, sn = ("l2f_qc", "l2f_qs") if part == 0 else ("l2f_kc", "l2f_ks")
                    P.dma("sp", rbuf[0][:], G(cn)[tt * 128:(tt + 1) * 128, :],
                          tag=f"L{L}w{w}t{t2}lc{part}")
                    P.dma("sp", rbuf[1][:], G(sn)[tt * 128:(tt + 1) * 128, :],
                          tag=f"L{L}w{w}t{t2}ls{part}")
                    vop(P, lambda e, rbuf=rbuf: e.tensor_tensor(
                            out=qn16[:].rearrange("p (h d) -> p h d", d=64),
                            in0=qn16[:].rearrange("p (h d) -> p h d", d=64),
                            in1=rbuf[0][:].unsqueeze(1).broadcast_to((128, 16, 64)),
                            op=ALU.mult),
                        [qn16[:], rbuf[0][:]], [qn16[:]], tag=f"L{L}w{w}t{t2}qc{part}")
                    vop(P, lambda e, rbuf=rbuf: e.tensor_tensor(
                            out=rot16[:].rearrange("p (h d) -> p h d", d=64),
                            in0=rot16[:].rearrange("p (h d) -> p h d", d=64),
                            in1=rbuf[1][:].unsqueeze(1).broadcast_to((128, 16, 64)),
                            op=ALU.mult),
                        [rot16[:], rbuf[1][:]], [rot16[:]], tag=f"L{L}w{w}t{t2}qs{part}")
                    vop(P, lambda e: e.tensor_add(out=qr16[:], in0=qn16[:], in1=rot16[:]),
                        [qn16[:], rot16[:]], [qr16[:]], tag=f"L{L}w{w}t{t2}qa{part}")
                    dstT = qrT if part == 0 else krT[par]
                    for k in range(NF):
                        pb = pT0 if k % 2 == 0 else pT1
                        slot = (k // 2) * 128
                        tr(P, pb[:, slot:slot + 128], qr16[:, k * 128:(k + 1) * 128],
                           c["ident16"][:], tag=f"L{L}w{w}t{t2}T{part}_{k}")
                        dst = dstT[:, t2 * 1024 + k * 128: t2 * 1024 + (k + 1) * 128]
                        vop(P, lambda e, pb=pb, slot=slot, dst=dst: e.tensor_copy(
                                out=dst, in_=pb[:, slot:slot + 128]),
                            [pb[:, slot:slot + 128]], [dst], tag=f"L{L}w{w}t{t2}E{part}_{k}")
            for hp in range(8):
                for t2 in range(2):
                    if t2 == 0:
                        acts = [(0, "prev", 0, "triu"), (1, "prev", 1, None),
                                (2, "cur", 0, "tri")]
                    else:
                        acts = [(0, "prev", 1, "triu"), (1, "cur", 0, None),
                                (2, "cur", 1, "tri")]
                    if w == 0:
                        acts = [a for a in acts if a[1] != "prev"]
                    nact = len(acts)
                    specs = []
                    for hh in range(2):
                        b0 = hh * 64
                        qh = qrT[b0:b0 + 64, t2 * 1024 + hp * 128: t2 * 1024 + hp * 128 + 128]
                        for ai, (ks, srcw, kt, msk) in enumerate(acts):
                            kbufp = krT[par] if srcw == "cur" else krT[1 - par]
                            kh = kbufp[b0:b0 + 64,
                                       kt * 1024 + hp * 128: kt * 1024 + hp * 128 + 128]
                            col = hh * 384 + ai * 128
                            specs.append((col, qh, kh))
                    banks = {}
                    for (col, qh, kh) in specs:
                        banks.setdefault(col // 512, []).append((col, qh, kh))
                    for bk, lst in banks.items():
                        for si, (col, qh, kh) in enumerate(lst):
                            mm(P, pS[:, col:col + 128], qh, kh,
                               start=(si == 0), stop=(si == len(lst) - 1),
                               tag=f"L{L}w{w}hp{hp}t{t2}s{col}")
                    wid = nact * 128
                    for hh in range(2):
                        sl = slice(hh * 384, hh * 384 + wid)
                        vop(P, lambda e, sl=sl, hh=hh: e.reduce_max(out=st2[:, 20 + hh:21 + hh],
                                in_=pS[:, sl], axis=AX.X),
                            [pS[:, sl]], [st2[:, 20 + hh:21 + hh]], tag=f"L{L}w{w}hp{hp}t{t2}mx{hh}")
                        vop(P, lambda e, hh=hh: e.tensor_scalar(out=st2[:, 22 + hh:23 + hh],
                                in0=st2[:, 20 + hh:21 + hh], scalar1=-8.0, scalar2=None,
                                op0=ALU.mult),
                            [st2[:, 20 + hh:21 + hh]], [st2[:, 22 + hh:23 + hh]],
                            tag=f"L{L}w{w}hp{hp}t{t2}nm{hh}")
                        sop(P, lambda e, sl=sl, hh=hh, wid=wid: e.activation(
                                out=p16[:, hh * 384:hh * 384 + wid], in_=pS[:, sl],
                                func=AF.Exp, scale=8.0, bias=st2[:, 22 + hh:23 + hh]),
                            [pS[:, sl], st2[:, 22 + hh:23 + hh]],
                            [p16[:, hh * 384:hh * 384 + wid]], tag=f"L{L}w{w}hp{hp}t{t2}e{hh}")
                        for ai, (ks, srcw, kt, msk) in enumerate(acts):
                            if msk is None:
                                continue
                            mt = tri16 if msk == "tri" else triu16
                            slm = slice(hh * 384 + ai * 128, hh * 384 + (ai + 1) * 128)
                            vop(P, lambda e, slm=slm, mt=mt: e.tensor_tensor(
                                    out=p16[:, slm], in0=p16[:, slm], in1=mt[:], op=ALU.mult),
                                [p16[:, slm], mt[:]], [p16[:, slm]],
                                tag=f"L{L}w{w}hp{hp}t{t2}m{hh}_{ai}")
                        vop(P, lambda e, hh=hh, wid=wid: e.reduce_sum(out=st2[:, 24 + hh:25 + hh],
                                in_=p16[:, hh * 384:hh * 384 + wid], axis=AX.X),
                            [p16[:, hh * 384:hh * 384 + wid]], [st2[:, 24 + hh:25 + hh]],
                            tag=f"L{L}w{w}hp{hp}t{t2}d{hh}")
                        vop(P, lambda e, hh=hh: e.reciprocal(out=st2[:, 26 + hh:27 + hh],
                                in_=st2[:, 24 + hh:25 + hh]),
                            [st2[:, 24 + hh:25 + hh]], [st2[:, 26 + hh:27 + hh]],
                            tag=f"L{L}w{w}hp{hp}t{t2}r{hh}")
                    nT = 2 * nact
                    for z in range(nT):
                        hh, ai = divmod(z, nact)
                        src = p16[:, hh * 384 + ai * 128: hh * 384 + (ai + 1) * 128]
                        pb = pT0 if z % 2 == 0 else pT1
                        slot = (z // 2 % 4) * 128
                        tr(P, pb[:, slot:slot + 128], src, c["ident16"][:],
                           tag=f"L{L}w{w}hp{hp}t{t2}pt{z}")
                        dst = pt16[:, z * 128:(z + 1) * 128]
                        vop(P, lambda e, pb=pb, slot=slot, dst=dst: e.tensor_copy(
                                out=dst, in_=pb[:, slot:slot + 128]),
                            [pb[:, slot:slot + 128]], [dst], tag=f"L{L}w{w}hp{hp}t{t2}pe{z}")
                    for hh in range(2):
                        h = 2 * hp + hh
                        for ai, (ks, srcw, kt, msk) in enumerate(acts):
                            vbufp = vw[par] if srcw == "cur" else vw[1 - par]
                            vsrc = vbufp[:, kt * 1024 + h * 64: kt * 1024 + h * 64 + 64]
                            mm(P, pV[:, hh * 64:(hh + 1) * 64],
                               pt16[:, (hh * nact + ai) * 128:(hh * nact + ai + 1) * 128],
                               vsrc, start=(ai == 0), stop=(ai == nact - 1),
                               tag=f"L{L}w{w}h{h}t{t2}pv{ai}")
                        dst = yw[:, t2 * 1024 + h * 64: t2 * 1024 + h * 64 + 64]
                        vop(P, lambda e, hh=hh, dst=dst: e.tensor_scalar(out=dst,
                                in0=pV[:, hh * 64:(hh + 1) * 64],
                                scalar1=st2[:, 26 + hh:27 + hh], scalar2=None, op0=ALU.mult),
                            [pV[:, hh * 64:(hh + 1) * 64], st2[:, 26 + hh:27 + hh]], [dst],
                            tag=f"L{L}w{w}h{h}t{t2}y")
            for t2 in range(2):
                tt = w * 2 + t2
                for k in range(NF):
                    pb = pT0 if k % 2 == 0 else pT1
                    slot = (k // 2) * 128
                    tr(P, pb[:, slot:slot + 128],
                       yw[:, t2 * 1024 + k * 128: t2 * 1024 + (k + 1) * 128],
                       c["ident16"][:], tag=f"L{L}w{w}t{t2}yT{k}")
                    dst = qr16[:, k * 128:(k + 1) * 128]
                    vop(P, lambda e, pb=pb, slot=slot, dst=dst: e.tensor_copy(
                            out=dst, in_=pb[:, slot:slot + 128]),
                        [pb[:, slot:slot + 128]], [dst], tag=f"L{L}w{w}t{t2}yE{k}")
                xa = c["xtm"][t2]
                P.dma("sp", xa[:], xi[tt * 128:(tt + 1) * 128, :], tag=f"L{L}w{w}t{t2}rx")
                ot = c["otm"][0]
                for n in range(2):
                    ps = pQ[:, n * 512:(n + 1) * 512]
                    for k in range(NF):
                        mm(P, ps, qr16[:, k * 128:(k + 1) * 128],
                           wo16[:, k, n * 512:(n + 1) * 512],
                           start=(k == 0), stop=(k == NF - 1), tag=f"L{L}w{w}t{t2}wo{n}")
                    vop(P, lambda e, ps=ps, xa=xa, n=n, ot=ot: e.tensor_tensor(
                            out=ot[:, n * 512:(n + 1) * 512], in0=ps,
                            in1=xa[:, n * 512:(n + 1) * 512], op=ALU.add),
                        [ps, xa[:, n * 512:(n + 1) * 512]], [ot[:, n * 512:(n + 1) * 512]],
                        tag=f"L{L}w{w}t{t2}wa{n}")
                P.dma("act", xi[tt * 128:(tt + 1) * 128, :], ot[:], tag=f"L{L}w{w}t{t2}ws")

        P.dma("pool", lng[:], G("l2f_lng")[L], tag=f"L{L}f_lg")
        P.dma("pool", lnb[:], G("l2f_lnb")[L], tag=f"L{L}f_lb")
        P.dma("pool", b2bc[:], G("l2f_b2bc")[L], tag=f"L{L}f_b2")
        P.dma("sp", b1a[:], G("l2f_b1a")[L], tag=f"L{L}f_b1a")
        P.dma("sp", b1g[:], G("l2f_b1g")[L], tag=f"L{L}f_b1g")
        P.dma("pool", w2v, G("l2f_w2")[L].rearrange("(j p) o -> p j o", p=128),
              tag=f"L{L}f_ldw2")
        for tt in range(NT):
            xa = c["xtm"][tt % 2]
            if tt == 0:
                P.dma("sp", xa[:, 0:512], xi[0:128, 0:512], tag=f"L{L}f_ld{tt}a")
                vop(P, lambda e, xa=xa: e.memset(xa[0:1, 512:1024], 0.0),
                    [], [xa[0:1, 512:1024]], tag=f"L{L}f_z0")
                P.dma("sp", xa[1:128, 512:1024], xi[0:127, 512:1024], tag=f"L{L}f_ld{tt}b")
            else:
                P.dma("sp", xa[:, 0:512], xi[tt * 128:(tt + 1) * 128, 0:512],
                      tag=f"L{L}f_ld{tt}a")
                P.dma("sp", xa[:, 512:1024], xi[tt * 128 - 1:(tt + 1) * 128 - 1, 512:1024],
                      tag=f"L{L}f_ld{tt}b")
            lt = c["lntile"][tt % 2]
            ln_tile(P, xa, lt[:], 1e-5, affine=(lng[:], lnb[:]), tag=f"L{L}f_ln{tt}_")
            to_big2(P, lt, tt, tag=f"L{L}f_")
        for tbb in range(4):
            for j in range(22):
                wb = w1b[j % 2][:].rearrange("p (ki o) -> p ki o", o=256)
                P.dma("pool", wb[:, :, 0:128],
                      G("l2f_w1a")[L][:, j * 128:(j + 1) * 128].rearrange(
                          "(ki p) o -> p ki o", p=128), tag=f"L{L}f_w1a{tbb}_{j}")
                P.dma("pool", wb[:, :, 128:256],
                      G("l2f_w1g")[L][:, j * 128:(j + 1) * 128].rearrange(
                          "(ki p) o -> p ki o", p=128), tag=f"L{L}f_w1g{tbb}_{j}")
                for half in range(2):
                    tok0 = tbb * 1024 + half * 512
                    psa = pQ[:, 0:512]
                    psg = pQ[:, 512:1024]
                    for ki in range(NF):
                        mm(P, psa, wb[:, ki, 0:128],
                           big2[:, ki * SP1 + 1 + tok0: ki * SP1 + 1 + tok0 + 512],
                           start=(ki == 0), stop=(ki == NF - 1),
                           tag=f"L{L}f_a{tbb}_{j}_{half}")
                    for ki in range(NF):
                        mm(P, psg, wb[:, ki, 128:256],
                           big2[:, ki * SP1 + 1 + tok0: ki * SP1 + 1 + tok0 + 512],
                           start=(ki == 0), stop=(ki == NF - 1),
                           tag=f"L{L}f_g{tbb}_{j}_{half}")
                    sop(P, lambda e, psg=psg, j=j: e.activation(out=ge16[:], in_=psg,
                            func=AF.Gelu, bias=b1g[:, j:j + 1]),
                        [psg, b1g[:, j:j + 1]], [ge16[:]], tag=f"L{L}f_ge{tbb}_{j}_{half}")
                    dst = hgb[:, j, half * 512:(half + 1) * 512]
                    vop(P, lambda e, psa=psa, j=j, dst=dst: e.scalar_tensor_tensor(
                            out=dst, in0=psa, scalar=b1a[:, j:j + 1], in1=ge16[:],
                            op0=ALU.add, op1=ALU.mult),
                        [psa, b1a[:, j:j + 1], ge16[:]], [dst], tag=f"L{L}f_hg{tbb}_{j}_{half}")
            for ts in range(8):
                tt = tbb * 8 + ts
                xa = c["xtm"][ts % 2]
                P.dma("sp", xa[:], xi[tt * 128:(tt + 1) * 128, :], tag=f"L{L}f_rx{tt}")
                ot = c["otm"][0]
                for n in range(2):
                    ps = pQ[:, n * 512:(n + 1) * 512]
                    for j in range(22):
                        mm(P, ps, hgb[:, j, ts * 128:(ts + 1) * 128],
                           w2v[:, j, n * 512:(n + 1) * 512],
                           start=(j == 0), stop=(j == 21), tag=f"L{L}f_o{tt}_{n}")
                    vop(P, lambda e, ps=ps, n=n: e.tensor_tensor(out=c["sq"][:, n * 512:(n + 1) * 512],
                            in0=ps, in1=b2bc[:, n * 512:(n + 1) * 512], op=ALU.add),
                        [ps, b2bc[:, n * 512:(n + 1) * 512]],
                        [c["sq"][:, n * 512:(n + 1) * 512]], tag=f"L{L}f_ob{tt}_{n}")
                    vop(P, lambda e, xa=xa, n=n, ot=ot: e.tensor_tensor(
                            out=ot[:, n * 512:(n + 1) * 512],
                            in0=c["sq"][:, n * 512:(n + 1) * 512],
                            in1=xa[:, n * 512:(n + 1) * 512], op=ALU.add),
                        [c["sq"][:, n * 512:(n + 1) * 512], xa[:, n * 512:(n + 1) * 512]],
                        [ot[:, n * 512:(n + 1) * 512]], tag=f"L{L}f_oa{tt}_{n}")
                P.dma("act", xi[tt * 128:(tt + 1) * 128, :], ot[:], tag=f"L{L}f_os{tt}")

    for tt in range(NT):
        xa = c["xtm"][tt % 2]
        ya = c["otm"][0]
        P.dma("sp", xa[:], xin[tt * 128:(tt + 1) * 128, :], tag=f"fz_ldx{tt}")
        P.dma("sp", ya[:], xi[tt * 128:(tt + 1) * 128, :], tag=f"fz_ldy{tt}")
        vop(P, lambda e, xa=xa, ya=ya: e.tensor_add(out=ya[:], in0=ya[:], in1=xa[:]),
            [xa[:], ya[:]], [ya[:]], tag=f"fz_add{tt}")
        P.dma("act", xout[tt * 128:(tt + 1) * 128, :], ya[:], tag=f"fz_st{tt}")


STAGE_FUNCS["l2_ffn"] = dict(prep=prep_l2_ffn, dram=dram_l2_ffn, emit=emit_l2_ffn, extra=[])



def prep_l3_attn(inputs):
    ar = (np.arange(1, S + 1, dtype=np.float32).reshape(32, 128).T).copy()
    return {
        "l3a_wqT": np.ascontiguousarray(inputs["L3_attn_Wq"].T),
        "l3a_wkT": np.ascontiguousarray(inputs["L3_attn_Wk"].T),
        "l3a_wvT": np.ascontiguousarray(inputs["L3_attn_Wv"].T),
        "l3a_woT": np.ascontiguousarray(inputs["L3_attn_Wo"].T),
        "l3a_bobc": np.ascontiguousarray(np.broadcast_to(
            np.asarray(inputs["L3_attn_bo"], np.float32)[None, :], (128, 1024))),
        "l3a_arange": ar,
        "l3a_id16": _ident16(), "l3a_triu": _triu_incl16(),
    }


def dram_l3_attn(P):
    g = {}
    for nm in ("l3a_wqT", "l3a_wkT", "l3a_wvT", "l3a_woT"):
        g[nm] = P.dram_in(nm, (D, D), F32)
    g["l3a_bobc"] = P.dram_in("l3a_bobc", (128, 1024), F32)
    g["l3a_arange"] = P.dram_in("l3a_arange", (128, 32), F32)
    g["l3a_id16"] = P.dram_in("l3a_id16", (128, 128), BF16)
    g["l3a_triu"] = P.dram_in("l3a_triu", (128, 128), BF16)
    g["l3a_qTd"] = P.dram_tmp("l3a_qTd", (16, 64, S), BF16)
    g["l3a_kTd"] = P.dram_tmp("l3a_kTd", (16, 64, S), BF16)
    return g


def emit_l3_attn(P, xin, xout, G):
    c = alloc_common(P)
    ensure_ident(P, c, G, "l3a_id16")
    big = P.sbuf("l3a_big", [128, NF * S], BF16)
    vsb = P.sbuf("l3a_vsb", [128, NT * 1024], BF16)
    w16 = P.sbuf("l3a_w16", [128, 8192], BF16)
    wv_ = w16[:].rearrange("p (ki o) -> p ki o", o=1024)
    triu = P.sbuf("l3a_triu_s", [128, 128], BF16)
    qbuf = P.sbuf("l3a_qbuf", [64, S], BF16)
    kbuf = P.sbuf("l3a_kbuf", [64, S], BF16)
    sq32 = P.sbuf("l3a_sq32", [64, 512], F32)
    q2h = P.sbuf("l3a_q2h", [64, 512], BF16)
    q2l = P.sbuf("l3a_q2l", [64, 512], BF16)
    nr = P.sbuf("l3a_nr", [1, S], F32)
    ones64 = P.sbuf("l3a_on64", [64, 1], BF16)
    onesr = P.sbuf("l3a_onr", [1, 128], BF16)
    sc8 = P.sbuf("l3a_sc8", [1, 8], F32)
    hi11 = P.sbuf("l3a_hi11", [1, 1], BF16)
    lo11 = P.sbuf("l3a_lo11", [1, 1], BF16)
    spp = P.sbuf("l3a_spp", [128, 1], F32)
    ar32 = P.sbuf("l3a_ar", [128, 32], F32)
    kvs32 = P.sbuf("l3a_kvs", [64, 65], F32)
    kvs16 = P.sbuf("l3a_kvs16", [64, 65], BF16)
    vs32 = P.sbuf("l3a_vs", [1, 64], F32)
    vsh = P.sbuf("l3a_vsh", [1, 64], BF16)
    vsl = P.sbuf("l3a_vsl", [1, 64], BF16)
    vaug = P.sbuf("l3a_vaug", [128, 65], BF16)
    at16 = P.sbuf("l3a_at16", [128, 128], BF16)
    ktk16 = P.sbuf("l3a_ktk", [128, 64], BF16)
    cum32 = P.sbuf("l3a_cum", [128, 64], F32)
    y16 = P.sbuf("l3a_y16", [128, 64], BF16)
    den32 = P.sbuf("l3a_den", [128, 4], F32)
    atT16 = P.sbuf("l3a_atT", [128, 1024], BF16)
    bobc = P.sbuf("l3a_bo_s", [128, 1024], BF16)
    P.dma("sp", triu[:], G("l3a_triu"))
    P.dma("sp", ar32[:], G("l3a_arange"), tag="ld_ar")
    vop(P, lambda e: e.memset(ones64[:], 1.0), [], [ones64[:]])
    vop(P, lambda e: e.memset(onesr[:], 1.0), [], [onesr[:]])
    vop(P, lambda e: e.memset(vaug[:, 64:65], 1.0), [], [vaug[:, 64:65]])

    emit_ln(P, c, xin, big)

    P.dma("pool", wv_, G("l3a_wvT").rearrange("(ki p) o -> p ki o", p=128), tag="ld_wv")
    for tt in range(NT):
        for n in range(2):
            ps = c["pA"][:, n * 512:(n + 1) * 512]
            for ki in range(NF):
                mm(P, ps, big[:, ki * S + tt * 128:ki * S + (tt + 1) * 128],
                   wv_[:, ki, n * 512:(n + 1) * 512],
                   start=(ki == 0), stop=(ki == NF - 1), tag=f"v_mm{tt}_{n}")
            dst = vsb[:, tt * 1024 + n * 512: tt * 1024 + (n + 1) * 512]
            vop(P, lambda e, ps=ps, dst=dst: e.tensor_copy(out=dst, in_=ps),
                [ps], [dst], tag=f"v_ev{tt}_{n}")
    for qk, (wname, dnam) in enumerate([("l3a_wqT", "l3a_qTd"), ("l3a_wkT", "l3a_kTd")]):
        P.dma("pool", wv_, G(wname).rearrange("(ki p) o -> p ki o", p=128), tag=f"ld_w{qk}")
        dstd = G(dnam)
        for oc in range(NF):
            for tb in range(8):
                r = (oc * 8 + tb) % 2
                ps = c["pB"][:, r * 512:(r + 1) * 512]
                for ki in range(NF):
                    mm(P, ps, wv_[:, ki, oc * 128:(oc + 1) * 128],
                       big[:, ki * S + tb * 512:ki * S + (tb + 1) * 512],
                       start=(ki == 0), stop=(ki == NF - 1), tag=f"qk{qk}_mm{oc}_{tb}")
                stg = atT16[:, 0:512] if r == 0 else atT16[:, 512:1024]
                vop(P, lambda e, ps=ps, stg=stg: e.tensor_copy(out=stg, in_=ps),
                    [ps], [stg], tag=f"qk{qk}_ev{oc}_{tb}")
                P.dma("act", dstd[2 * oc, :, tb * 512:(tb + 1) * 512], stg[0:64, :],
                      tag=f"qk{qk}_st0_{oc}_{tb}")
                P.dma("act", dstd[2 * oc + 1, :, tb * 512:(tb + 1) * 512], stg[64:128, :],
                      tag=f"qk{qk}_st1_{oc}_{tb}")

    for h in range(16):
        P.dma("sp", qbuf[:], G("l3a_qTd")[h], tag=f"hd_ldq{h}")
        P.dma("sp", kbuf[:], G("l3a_kTd")[h], tag=f"hd_ldk{h}")
        for which, buf in ((0, qbuf), (1, kbuf)):
            for bl in range(8):
                sl = slice(bl * 512, (bl + 1) * 512)
                sop(P, lambda e, buf=buf, sl=sl: e.activation(out=sq32[:], in_=buf[:, sl],
                                                              func=AF.Square),
                    [buf[:, sl]], [sq32[:]], tag=f"n_sq{h}_{which}_{bl}")
                vop(P, lambda e: e.tensor_copy(out=q2h[:], in_=sq32[:]), [sq32[:]], [q2h[:]])
                vop(P, lambda e: e.tensor_sub(out=q2l[:], in0=sq32[:], in1=q2h[:]),
                    [sq32[:], q2h[:]], [q2l[:]])
                mm(P, c["pC"][0:1, 0:512], ones64[:], q2h[:], start=True, stop=False,
                   tag=f"n_mh{h}_{which}_{bl}")
                mm(P, c["pC"][0:1, 0:512], ones64[:], q2l[:], start=False, stop=True,
                   tag=f"n_ml{h}_{which}_{bl}")
                vop(P, lambda e, sl=sl: e.tensor_copy(out=nr[:, sl], in_=c["pC"][0:1, 0:512]),
                    [c["pC"][0:1, 0:512]], [nr[:, sl]], tag=f"n_ev{h}_{which}_{bl}")
            vop(P, lambda e, which=which: e.reduce_max(out=sc8[:, which:which + 1],
                    in_=nr[:], axis=AX.X),
                [nr[:]], [sc8[:, which:which + 1]], tag=f"n_mx{h}_{which}")
        vop(P, lambda e: e.tensor_tensor(out=sc8[:, 2:3], in0=sc8[:, 0:1], in1=sc8[:, 1:2],
                                         op=ALU.mult), [sc8[:, 0:1], sc8[:, 1:2]], [sc8[:, 2:3]])
        sop(P, lambda e: e.activation(out=sc8[:, 3:4], in_=sc8[:, 2:3], func=AF.Sqrt),
            [sc8[:, 2:3]], [sc8[:, 3:4]])
        vop(P, lambda e: e.reciprocal(out=sc8[:, 4:5], in_=sc8[:, 3:4]),
            [sc8[:, 3:4]], [sc8[:, 4:5]])
        _bcast_scalar(P, c, sc8[:, 4:5], spp[:], onesr[:], hi11[:], lo11[:], f"sb{h}")
        vop(P, lambda e: e.memset(kvs32[:], 0.0), [], [kvs32[:]], tag=f"z_kv{h}")
        vop(P, lambda e: e.memset(kvs16[:], 0.0), [], [kvs16[:]], tag=f"z_kv16{h}")
        vop(P, lambda e: e.memset(vs32[:], 0.0), [], [vs32[:]], tag=f"z_vs{h}")
        vop(P, lambda e: e.memset(vsh[:], 0.0), [], [vsh[:]], tag=f"z_vsh{h}")
        vop(P, lambda e: e.memset(vsl[:], 0.0), [], [vsl[:]], tag=f"z_vsl{h}")
        for ct in range(NT):
            csl = slice(ct * 128, (ct + 1) * 128)
            arange = ar32[:, ct:ct + 1]
            vsrc = vsb[:, ct * 1024 + h * 64: ct * 1024 + h * 64 + 64]
            vop(P, lambda e, vsrc=vsrc: e.tensor_copy(out=vaug[:, 0:64], in_=vsrc),
                [vsrc], [vaug[:, 0:64]], tag=f"sc_v{h}_{ct}")
            mm(P, c["pB"][:, 0:64], triu[:], vaug[:, 0:64], start=True, stop=False,
               tag=f"sc_cu{h}_{ct}")
            mm(P, c["pB"][:, 0:64], onesr[:], vsh[:], start=False, stop=False,
               tag=f"sc_ch{h}_{ct}")
            mm(P, c["pB"][:, 0:64], onesr[:], vsl[:], start=False, stop=True,
               tag=f"sc_cl{h}_{ct}")
            vop(P, lambda e: e.tensor_copy(out=cum32[:], in_=c["pB"][:, 0:64]),
                [c["pB"][:, 0:64]], [cum32[:]], tag=f"sc_ce{h}_{ct}")
            mm(P, c["pA"][:, 0:65], qbuf[:, csl], kvs16[:], start=True, stop=False,
               tag=f"sc_n1{h}_{ct}")
            mm(P, c["pA"][:, 512:640], kbuf[:, csl], qbuf[:, csl], tag=f"sc_at{h}_{ct}")
            vop(P, lambda e: e.tensor_tensor(out=at16[:], in0=c["pA"][:, 512:640],
                    in1=triu[:], op=ALU.mult),
                [c["pA"][:, 512:640], triu[:]], [at16[:]], tag=f"sc_m{h}_{ct}")
            mm(P, c["pA"][:, 0:65], at16[:], vaug[:], start=False, stop=True,
               tag=f"sc_n2{h}_{ct}")
            vop(P, lambda e, arange=arange: e.scalar_tensor_tensor(out=den32[:, 0:1],
                    in0=c["pA"][:, 64:65], scalar=spp[:, 0:1], in1=arange[:],
                    op0=ALU.mult, op1=ALU.add),
                [c["pA"][:, 64:65], spp[:, 0:1], arange[:]], [den32[:, 0:1]],
                tag=f"sc_d{h}_{ct}")
            vop(P, lambda e: e.reciprocal(out=den32[:, 1:2], in_=den32[:, 0:1]),
                [den32[:, 0:1]], [den32[:, 1:2]], tag=f"sc_r{h}_{ct}")
            vop(P, lambda e: e.scalar_tensor_tensor(out=cum32[:],
                    in0=c["pA"][:, 0:64], scalar=spp[:, 0:1], in1=cum32[:],
                    op0=ALU.mult, op1=ALU.add),
                [c["pA"][:, 0:64], spp[:, 0:1], cum32[:]], [cum32[:]], tag=f"sc_nf{h}_{ct}")
            vop(P, lambda e: e.tensor_scalar(out=y16[:], in0=cum32[:],
                    scalar1=den32[:, 1:2], scalar2=None, op0=ALU.mult),
                [cum32[:], den32[:, 1:2]], [y16[:]], tag=f"sc_y{h}_{ct}")
            dst = big[:, ct * 1024 + h * 64: ct * 1024 + h * 64 + 64]
            vop(P, lambda e, dst=dst: e.tensor_copy(out=dst, in_=y16[:]),
                [y16[:]], [dst], tag=f"sc_yw{h}_{ct}")
            import os as _os4
            if _os4.environ.get("L3A_DBG") == "1" and h == 0 and ct == 0:
                _d1 = P.dram_out("dbg_cum", (128, 64), F32)
                _d2 = P.dram_out("dbg_num", (128, 65), BF16)
                _d3 = P.dram_out("dbg_den", (128, 1), F32)
                _d4 = P.dram_out("dbg_at0", (128, 128), BF16)
                _d5 = P.dram_out("dbg_va0", (128, 65), BF16)
                _d6 = P.dram_out("dbg_cv", (128, 64), BF16)
                _s1 = P.sbuf("l3a_dbg1", [128, 65], BF16)
                _s2 = P.sbuf("l3a_dbg2", [128, 65], BF16)
                _s3 = P.sbuf("l3a_dbg3", [128, 128], BF16)
                _s4 = P.sbuf("l3a_dbg4", [128, 64], BF16)
                vop(P, lambda e: e.tensor_copy(out=_s1[:], in_=c["pA"][:, 0:65]),
                    [c["pA"][:, 0:65]], [_s1[:]], tag="dbg_l3_1")
                vop(P, lambda e: e.tensor_copy(out=_s2[:], in_=vaug[:]),
                    [vaug[:]], [_s2[:]], tag="dbg_l3_2")
                vop(P, lambda e: e.tensor_copy(out=_s3[:], in_=at16[:]),
                    [at16[:]], [_s3[:]], tag="dbg_l3_3")
                vop(P, lambda e: e.tensor_copy(out=_s4[:], in_=c["pB"][:, 0:64]),
                    [c["pB"][:, 0:64]], [_s4[:]], tag="dbg_l3_4")
                P.dma("act", _d1, cum32[:])
                P.dma("act", _d2, _s1[:])
                P.dma("act", _d3, den32[:, 0:1])
                P.dma("act", _d4, _s3[:])
                P.dma("act", _d5, _s2[:])
                P.dma("act", _d6, _s4[:])
            tr(P, c["pT0"][:, 0:64], kbuf[:, csl], c["ident16"][0:64, 0:64],
               tag=f"sc_tr{h}_{ct}")
            vop(P, lambda e: e.tensor_copy(out=ktk16[:], in_=c["pT0"][:, 0:64]),
                [c["pT0"][:, 0:64]], [ktk16[:]], tag=f"sc_te{h}_{ct}")
            mm(P, c["pC"][0:64, 0:65], ktk16[:], vaug[:], tag=f"sc_kv{h}_{ct}")
            vop(P, lambda e: e.tensor_tensor(out=kvs32[:], in0=kvs32[:],
                    in1=c["pC"][0:64, 0:65], op=ALU.add),
                [kvs32[:], c["pC"][0:64, 0:65]], [kvs32[:]], tag=f"sc_ka{h}_{ct}")
            vop(P, lambda e: e.tensor_copy(out=kvs16[:], in_=kvs32[:]),
                [kvs32[:]], [kvs16[:]], tag=f"sc_kc{h}_{ct}")
            mm(P, c["pD"][0:1, 0:64], ones64[:].rearrange("a b -> a b") if False else ones64[:],
               vaug[0:64, 0:64], tag=f"_dummy{h}_{ct}") if False else None
            mm(P, c["pD"][0:1, 0:64], onesr[:].rearrange("a b -> b a") if False else triu[:, 0:1],
               vaug[:, 0:64], tag=f"sc_vs{h}_{ct}") if False else None
            mm(P, c["pD"][0:1, 0:64], vaug[:, 64:65], vaug[:, 0:64], tag=f"sc_vs{h}_{ct}")
            vop(P, lambda e: e.tensor_tensor(out=vs32[:], in0=vs32[:],
                    in1=c["pD"][0:1, 0:64], op=ALU.add),
                [vs32[:], c["pD"][0:1, 0:64]], [vs32[:]], tag=f"sc_va{h}_{ct}")
            vop(P, lambda e: e.tensor_copy(out=vsh[:], in_=vs32[:]), [vs32[:]], [vsh[:]],
                tag=f"sc_vh{h}_{ct}")
            vop(P, lambda e: e.tensor_sub(out=vsl[:], in0=vs32[:], in1=vsh[:]),
                [vs32[:], vsh[:]], [vsl[:]], tag=f"sc_vl{h}_{ct}")

    import os as _os3
    if _os3.environ.get("L3A_DBG") == "1":
        dy = P.dram_out("dbg_y0", (128, 1024), BF16)
        dsp = P.dram_out("dbg_spp", (128, 1), F32)
        dq = P.dram_out("dbg_q15", (64, S), BF16)
        P.dma("act", dy, big[:, 0:1024])
        P.dma("act", dsp, spp[:])
        P.dma("act", dq, qbuf[:])

    P.dma("pool", wv_, G("l3a_woT").rearrange("(ki p) o -> p ki o", p=128), tag="ld_wo")
    P.dma("pool", bobc[:], G("l3a_bobc"), tag="ld_bo")
    for tt in range(NT):
        xa = c["xtm"][tt % 2]
        ot = c["otm"][tt % 2]
        for k in range(NF):
            pb = c["pT0"] if k % 2 == 0 else c["pT1"]
            slot = (k // 2) * 128
            tr(P, pb[:, slot:slot + 128], big[:, tt * 1024 + k * 128: tt * 1024 + (k + 1) * 128],
               c["ident16"][:], tag=f"wo_tr{tt}_{k}")
            dst = atT16[:, k * 128:(k + 1) * 128]
            vop(P, lambda e, pb=pb, slot=slot, dst=dst: e.tensor_copy(out=dst, in_=pb[:, slot:slot + 128]),
                [pb[:, slot:slot + 128]], [dst])
        P.dma("sp", xa[:], xin[tt * 128:(tt + 1) * 128, :], tag=f"wo_ldx{tt}")
        for n in range(2):
            ps = c["pB"][:, n * 512:(n + 1) * 512]
            for k in range(NF):
                mm(P, ps, atT16[:, k * 128:(k + 1) * 128], wv_[:, k, n * 512:(n + 1) * 512],
                   start=(k == 0), stop=(k == NF - 1), tag=f"wo_mm{tt}_{n}")
            vop(P, lambda e, ps=ps, n=n: e.tensor_tensor(out=c["sq"][:, n * 512:(n + 1) * 512],
                    in0=ps, in1=bobc[:, n * 512:(n + 1) * 512], op=ALU.add),
                [ps, bobc[:, n * 512:(n + 1) * 512]], [c["sq"][:, n * 512:(n + 1) * 512]],
                tag=f"wo_b{tt}_{n}")
            vop(P, lambda e, xa=xa, n=n, ot=ot: e.tensor_tensor(
                    out=ot[:, n * 512:(n + 1) * 512], in0=c["sq"][:, n * 512:(n + 1) * 512],
                    in1=xa[:, n * 512:(n + 1) * 512], op=ALU.add),
                [c["sq"][:, n * 512:(n + 1) * 512], xa[:, n * 512:(n + 1) * 512]],
                [ot[:, n * 512:(n + 1) * 512]], tag=f"wo_add{tt}_{n}")
        P.dma("act", xout[tt * 128:(tt + 1) * 128, :], ot[:], tag=f"wo_st{tt}")


STAGE_FUNCS["l3_attn"] = dict(prep=lambda inputs, carry: prep_l3_attn(inputs),
                              dram=dram_l3_attn, emit=emit_l3_attn, extra=[])



CAP = 512
EPS_RMS = 1e-6


def prep_l3_ffn(inputs):
    DFFP = 2816
    w1 = np.asarray(inputs["L3_ffn_W1"], np.float32)
    w2 = np.asarray(inputs["L3_ffn_W2"], np.float32)
    w3 = np.asarray(inputs["L3_ffn_W3"], np.float32)
    w1T = np.zeros((D, DFFP), np.float32); w1T[:, :2730] = w1.T
    w2T = np.zeros((D, DFFP), np.float32); w2T[:, :2730] = w2.T
    w3T = np.zeros((DFFP, D), np.float32); w3T[:2730, :] = w3.T
    cm = np.zeros((4, 128, CAP), np.float32)
    for qt in range(4):
        i = qt * 128 + np.arange(128)[:, None]
        j = np.arange(CAP)[None, :]
        cm[qt] = (j <= i).astype(np.float32)
    iota = np.broadcast_to(np.arange(CAP, dtype=np.float32)[None, :], (128, CAP)).copy()
    ltstrip = (np.arange(1024)[None, :] < np.arange(128)[:, None] + 512).astype(np.float32)
    return {
        "l3f_wr": np.ascontiguousarray(np.asarray(inputs["L3_ffn_w_router"],
                                                  np.float32).reshape(8, 128).T),
        "l3f_wqkvT": np.ascontiguousarray(inputs["L3_ffn_W_qkv"].T),
        "l3f_woutT": np.ascontiguousarray(inputs["L3_ffn_W_out"].T),
        "l3f_w1T": w1T, "l3f_w2T": w2T, "l3f_w3T": w3T,
        "l3f_g1bc": np.ascontiguousarray(np.broadcast_to(
            np.asarray(inputs["L3_ffn_g1"], np.float32)[None, :], (128, 1024))),
        "l3f_g2bc": np.ascontiguousarray(np.broadcast_to(
            np.asarray(inputs["L3_ffn_g2"], np.float32)[None, :], (128, 1024))),
        "l3f_cmask": cm.astype(BF),
        "l3f_iota": iota,
        "l3f_lts": ltstrip.astype(BF),
        "l3f_id16": _ident16(),
    }


def dram_l3_ffn(P):
    g = {}
    g["l3f_wr"] = P.dram_in("l3f_wr", (128, 8), F32)
    g["l3f_wqkvT"] = P.dram_in("l3f_wqkvT", (D, 3 * D), F32)
    g["l3f_woutT"] = P.dram_in("l3f_woutT", (D, D), F32)
    g["l3f_w1T"] = P.dram_in("l3f_w1T", (D, 2816), F32)
    g["l3f_w2T"] = P.dram_in("l3f_w2T", (D, 2816), F32)
    g["l3f_w3T"] = P.dram_in("l3f_w3T", (2816, D), F32)
    g["l3f_g1bc"] = P.dram_in("l3f_g1bc", (128, 1024), F32)
    g["l3f_g2bc"] = P.dram_in("l3f_g2bc", (128, 1024), F32)
    g["l3f_cmask"] = P.dram_in("l3f_cmask", (4, 128, CAP), BF16)
    g["l3f_iota"] = P.dram_in("l3f_iota", (128, CAP), F32)
    g["l3f_lts"] = P.dram_in("l3f_lts", (128, 1024), BF16)
    g["l3f_id16"] = P.dram_in("l3f_id16", (128, 128), BF16)
    g["l3f_xlntm"] = P.dram_tmp("l3f_xlntm", (S, D), BF16)
    return g


def emit_l3_ffn(P, xin, xout, G):
    c = alloc_common(P, notm=1)
    ensure_ident(P, c, G, "l3f_id16")
    st = c["stats"]
    big = P.sbuf("l3f_big", [128, NF * S], BF16)
    PTall = big[:, 0:16384]
    Pall = big[:, 16384:32768]
    shrow = big[0:1, 2 * 16384 // 2:2 * 16384 // 2 + S]
    slrow = big[0:1, 16384 + S:16384 + 2 * S]
    pool = P.sbuf("l3f_pool", [128, 33792], BF16)
    qT_s = pool[:, 0:4096]
    kT_s = pool[:, 4096:8192]
    vtok = pool[:, 8192:12288]
    osel = pool[:, 12288:16384]
    wout16 = pool[:, 16384:24576].rearrange("p (ki o) -> p ki o", o=1024)
    w3v = pool[:, 0:22528].rearrange("p (j o) -> p j o", o=1024)
    hselv = pool[:, 22528:33792].rearrange("p (j t) -> p j t", t=CAP)
    xp16v = pool[:, 16384:20480]
    hT = P.sbuf("l3f_hT", [128, 4 * 1024], BF16)
    xsel32 = P.sbuf("l3f_xsel", [128, 4 * 1024], F32)
    h16 = P.sbuf("l3f_h16", [128, 1024], BF16)
    wrh = P.sbuf("l3f_wrh", [128, 8], BF16)
    wrl = P.sbuf("l3f_wrl", [128, 8], BF16)
    wr32 = P.sbuf("l3f_wr32", [128, 8], F32)
    xlo = pool[:, 1024:2048]
    xloT = pool[:, 0:1024]
    scol = P.sbuf("l3f_scol", [128, 32], F32)
    rank32 = P.sbuf("l3f_rank", [128, 32], F32)
    nscol = P.sbuf("l3f_ns", [128, 32], F32)
    onesr = P.sbuf("l3f_onr", [1, 128], BF16)
    iota32 = P.sbuf("l3f_iota_s", [128, CAP], F32)
    cmask = P.sbuf("l3f_cm", [128, 4 * CAP], BF16)
    g1bc = P.sbuf("l3f_g1_s", [128, 1024], BF16)
    g2bc = P.sbuf("l3f_g2_s", [128, 1024], BF16)
    wbufT = P.sbuf("l3f_wbT", [128, 4096], BF16)
    wbuf = [wbufT[:, 0:2048], wbufT[:, 2048:4096]]
    s16 = P.sbuf("l3f_s16", [128, CAP], BF16)
    p16 = P.sbuf("l3f_p16", [128, CAP], BF16)
    pt16 = P.sbuf("l3f_pt16", [128, CAP], BF16)
    tmp32 = P.sbuf("l3f_tmp", [128, 1024], F32)
    xp16 = xp16v
    P.dma("sp", iota32[:], G("l3f_iota"))
    lts = P.sbuf("l3f_lts_s", [128, 1024], BF16)
    P.dma("sp", lts[:], G("l3f_lts"))
    for _q in range(4):
        P.dma("sp", cmask[:, _q * CAP:(_q + 1) * CAP], G("l3f_cmask")[_q], tag=f"ld_cm{_q}")
    P.dma("pool", g1bc[:], G("l3f_g1bc"))
    P.dma("pool", g2bc[:], G("l3f_g2bc"))
    P.dma("sp", wr32[:], G("l3f_wr"))
    vop(P, lambda e: e.memset(onesr[:], 1.0), [], [onesr[:]])
    vop(P, lambda e: e.tensor_copy(out=wrh[:], in_=wr32[:]), [wr32[:]], [wrh[:]])
    vop(P, lambda e: e.tensor_sub(out=wrl[:], in0=wr32[:], in1=wrh[:]),
        [wr32[:], wrh[:]], [wrl[:]])

    xlntm = G("l3f_xlntm")
    for tt in range(NT):
        xa = c["xtm"][tt % 2]
        lnt = c["lntile"][tt % 2]
        P.dma("sp", xa[:], xin[tt * 128:(tt + 1) * 128, :], tag=f"ln_ld{tt}")
        sop(P, lambda e, xa=xa: e.activation(out=c["sq"][:], in_=xa[:], func=AF.Square,
                                             accum_out=st[:, 1:2]),
            [xa[:]], [c["sq"][:], st[:, 1:2]])
        vop(P, lambda e, xa=xa: e.reduce_sum(out=st[:, 0:1], in_=xa[:], axis=AX.X),
            [xa[:]], [st[:, 0:1]])
        vop(P, lambda e: e.tensor_scalar(out=st[:, 2:3], in0=st[:, 0:1], scalar1=1.0 / D,
                                         scalar2=None, op0=ALU.mult), [st[:, 0:1]], [st[:, 2:3]])
        vop(P, lambda e: e.tensor_scalar(out=st[:, 3:4], in0=st[:, 1:2], scalar1=1.0 / D,
                                         scalar2=None, op0=ALU.mult), [st[:, 1:2]], [st[:, 3:4]])
        vop(P, lambda e: e.tensor_tensor(out=st[:, 4:5], in0=st[:, 2:3], in1=st[:, 2:3],
                                         op=ALU.mult), [st[:, 2:3]], [st[:, 4:5]])
        vop(P, lambda e: e.tensor_sub(out=st[:, 5:6], in0=st[:, 3:4], in1=st[:, 4:5]),
            [st[:, 3:4], st[:, 4:5]], [st[:, 5:6]])
        epsap = P.constant(EPS_LN)
        sop(P, lambda e, epsap=epsap: e.activation(out=st[:, 6:7], in_=st[:, 5:6],
                                                   func=AF.Sqrt, bias=epsap),
            [st[:, 5:6], epsap], [st[:, 6:7]])
        vop(P, lambda e: e.reciprocal(out=st[:, 7:8], in_=st[:, 6:7]), [st[:, 6:7]], [st[:, 7:8]])
        vop(P, lambda e: e.tensor_tensor(out=st[:, 8:9], in0=st[:, 2:3], in1=st[:, 7:8],
                                         op=ALU.mult), [st[:, 2:3], st[:, 7:8]], [st[:, 8:9]])
        vop(P, lambda e: e.tensor_scalar(out=st[:, 9:10], in0=st[:, 8:9], scalar1=-1.0,
                                         scalar2=None, op0=ALU.mult), [st[:, 8:9]], [st[:, 9:10]])
        lnt32 = c["sq"][:]
        vop(P, lambda e, xa=xa: e.tensor_scalar(out=lnt32, in0=xa[:],
                scalar1=st[:, 7:8], scalar2=st[:, 9:10], op0=ALU.mult, op1=ALU.add),
            [xa[:], st[:, 7:8], st[:, 9:10]], [lnt32], tag=f"ln_n{tt}")
        vop(P, lambda e, lnt=lnt: e.tensor_copy(out=lnt[:], in_=lnt32), [lnt32], [lnt[:]])
        vop(P, lambda e, lnt=lnt: e.tensor_sub(out=xlo[:], in0=lnt32, in1=lnt[:]),
            [lnt32, lnt[:]], [xlo[:]])
        P.dma("act", xlntm[tt * 128:(tt + 1) * 128, :], lnt[:], tag=f"ln_tm{tt}")
        for k in range(NF):
            pb = c["pT0"] if k % 2 == 0 else c["pT1"]
            slot = (k // 2) * 128
            tr(P, pb[:, slot:slot + 128], lnt[:, k * 128:(k + 1) * 128], c["ident16"][:])
            dst = big[:, k * S + tt * 128:k * S + (tt + 1) * 128]
            vop(P, lambda e, pb=pb, slot=slot, dst=dst: e.tensor_copy(out=dst, in_=pb[:, slot:slot + 128]),
                [pb[:, slot:slot + 128]], [dst])
        for k in range(NF):
            pb = c["pT0"] if k % 2 == 0 else c["pT1"]
            slot = (k // 2) * 128
            tr(P, pb[:, slot:slot + 128], xlo[:, k * 128:(k + 1) * 128], c["ident16"][:])
            dst = xloT[:, k * 128:(k + 1) * 128]
            vop(P, lambda e, pb=pb, slot=slot, dst=dst: e.tensor_copy(out=dst, in_=pb[:, slot:slot + 128]),
                [pb[:, slot:slot + 128]], [dst])
        ps1 = c["pB"][0:1, 0:128]
        for k in range(NF):
            xhiT = big[:, k * S + tt * 128:k * S + (tt + 1) * 128]
            mm(P, ps1, wrh[:, k:k + 1], xhiT, start=(k == 0), stop=False, tag=f"sc_h{tt}_{k}")
            mm(P, ps1, wrl[:, k:k + 1], xhiT, start=False, stop=False, tag=f"sc_l{tt}_{k}")
            mm(P, ps1, wrh[:, k:k + 1], xloT[:, k * 128:(k + 1) * 128],
               start=False, stop=(k == NF - 1), tag=f"sc_x{tt}_{k}")
        dsth = shrow[:, tt * 128:(tt + 1) * 128]
        dstl = slrow[:, tt * 128:(tt + 1) * 128]
        vop(P, lambda e, ps1=ps1, dsth=dsth: e.tensor_copy(out=dsth, in_=ps1),
            [ps1], [dsth], tag=f"sc_evh{tt}")
        vop(P, lambda e, ps1=ps1, dsth=dsth, dstl=dstl: e.tensor_tensor(
                out=dstl, in0=ps1, in1=dsth, op=ALU.subtract),
            [ps1, dsth], [dstl], tag=f"sc_evl{tt}")

    for tt in range(NT):
        tr(P, c["pT0"][0:128, 0:1], shrow[:, tt * 128:(tt + 1) * 128],
           c["ident16"][0:1, 0:1], tag=f"rk_t1{tt}")
        vop(P, lambda e, tt=tt: e.tensor_copy(out=scol[:, tt:tt + 1], in_=c["pT0"][0:128, 0:1]),
            [c["pT0"][0:128, 0:1]], [scol[:, tt:tt + 1]], tag=f"rk_e1{tt}")
        tr(P, c["pT1"][0:128, 0:1], slrow[:, tt * 128:(tt + 1) * 128],
           c["ident16"][0:1, 0:1], tag=f"rk_t2{tt}")
        vop(P, lambda e, tt=tt: e.tensor_copy(out=tmp32[:, 0:1], in_=c["pT1"][0:128, 0:1]),
            [c["pT1"][0:128, 0:1]], [tmp32[:, 0:1]], tag=f"rk_e2{tt}")
        vop(P, lambda e, tt=tt: e.tensor_tensor(out=scol[:, tt:tt + 1], in0=scol[:, tt:tt + 1],
                in1=tmp32[:, 0:1], op=ALU.add),
            [scol[:, tt:tt + 1], tmp32[:, 0:1]], [scol[:, tt:tt + 1]], tag=f"rk_a{tt}")
    vop(P, lambda e: e.memset(rank32[:], 0.0), [], [rank32[:]], tag="rk_z")
    for bb in range(8):
        psb = c["pA"][:, 0:512]
        mm(P, psb, onesr[:], shrow[:, bb * 512:(bb + 1) * 512], start=True, stop=False,
           tag=f"rk_bh{bb}")
        mm(P, psb, onesr[:], slrow[:, bb * 512:(bb + 1) * 512], start=False, stop=True,
           tag=f"rk_bl{bb}")
        for tt in range(NT):
            vop(P, lambda e, psb=psb, tt=tt: e.tensor_scalar(out=c["sq"][:, 0:512], in0=psb,
                    scalar1=scol[:, tt:tt + 1], scalar2=None, op0=ALU.is_gt, op1=ALU.add,
                    accum_out=st[:, 12:13]),
                [psb, scol[:, tt:tt + 1]], [c["sq"][:, 0:512], st[:, 12:13]],
                tag=f"rk_c{bb}_{tt}")
            vop(P, lambda e, tt=tt: e.tensor_tensor(out=rank32[:, tt:tt + 1],
                    in0=rank32[:, tt:tt + 1], in1=st[:, 12:13], op=ALU.add),
                [rank32[:, tt:tt + 1], st[:, 12:13]], [rank32[:, tt:tt + 1]],
                tag=f"rk_u{bb}_{tt}")
            if bb < tt // 4:
                vop(P, lambda e, psb=psb, tt=tt: e.tensor_scalar(out=c["sq"][:, 0:512],
                        in0=psb, scalar1=scol[:, tt:tt + 1], scalar2=None, op0=ALU.is_equal,
                        op1=ALU.add, accum_out=st[:, 12:13]),
                    [psb, scol[:, tt:tt + 1]], [c["sq"][:, 0:512], st[:, 12:13]],
                    tag=f"rk_q{bb}_{tt}")
                vop(P, lambda e, tt=tt: e.tensor_tensor(out=rank32[:, tt:tt + 1],
                        in0=rank32[:, tt:tt + 1], in1=st[:, 12:13], op=ALU.add),
                    [rank32[:, tt:tt + 1], st[:, 12:13]], [rank32[:, tt:tt + 1]],
                    tag=f"rk_qa{bb}_{tt}")
            elif bb == tt // 4:
                j0 = tt % 4
                off = 512 - 128 * j0
                vop(P, lambda e, psb=psb, tt=tt: e.tensor_scalar(out=c["sq"][:, 0:512],
                        in0=psb, scalar1=scol[:, tt:tt + 1], scalar2=None, op0=ALU.is_equal),
                    [psb, scol[:, tt:tt + 1]], [c["sq"][:, 0:512]], tag=f"rk_q{bb}_{tt}")
                vop(P, lambda e, off=off: e.tensor_tensor(out=c["sq"][:, 0:512],
                        in0=c["sq"][:, 0:512], in1=lts[:, off:off + 512], op=ALU.mult),
                    [c["sq"][:, 0:512], lts[:, off:off + 512]], [c["sq"][:, 0:512]],
                    tag=f"rk_qm{bb}_{tt}")
                vop(P, lambda e: e.reduce_sum(out=st[:, 12:13], in_=c["sq"][:, 0:512],
                                              axis=AX.X),
                    [c["sq"][:, 0:512]], [st[:, 12:13]], tag=f"rk_qs{bb}_{tt}")
                vop(P, lambda e, tt=tt: e.tensor_tensor(out=rank32[:, tt:tt + 1],
                        in0=rank32[:, tt:tt + 1], in1=st[:, 12:13], op=ALU.add),
                    [rank32[:, tt:tt + 1], st[:, 12:13]], [rank32[:, tt:tt + 1]],
                    tag=f"rk_qa{bb}_{tt}")
    for tt in range(NT):
        dst = PTall[:, tt * 512:(tt + 1) * 512]
        vop(P, lambda e, dst=dst, tt=tt: e.tensor_single_scalar(out=dst, in_=iota32[:],
                scalar=rank32[:, tt:tt + 1], op=ALU.is_equal),
            [iota32[:], rank32[:, tt:tt + 1]], [dst], tag=f"pt_{tt}")
        vop(P, lambda e, tt=tt: e.tensor_single_scalar(out=nscol[:, tt:tt + 1],
                in_=rank32[:, tt:tt + 1], scalar=float(CAP), op=ALU.is_ge),
            [rank32[:, tt:tt + 1]], [nscol[:, tt:tt + 1]], tag=f"ns_{tt}")
    for tt in range(NT):
        for pc in range(4):
            pb = c["pT0"] if pc % 2 == 0 else c["pT1"]
            slot = (pc // 2) * 128
            tr(P, pb[:, slot:slot + 128], PTall[:, tt * 512 + pc * 128: tt * 512 + (pc + 1) * 128],
               c["ident16"][:], tag=f"pp_{tt}_{pc}")
            dst = Pall[:, pc * S + tt * 128:pc * S + (tt + 1) * 128]
            vop(P, lambda e, pb=pb, slot=slot, dst=dst: e.tensor_copy(out=dst, in_=pb[:, slot:slot + 128]),
                [pb[:, slot:slot + 128]], [dst], tag=f"ppe_{tt}_{pc}")

    for half in range(2):
        groups = [(ps_, n_) for ps_ in range(2) for n_ in range(2)]
        for tt in range(NT):
            lnt = c["lntile"][tt % 2]
            P.dma("sp", lnt[:], xlntm[tt * 128:(tt + 1) * 128, :], tag=f"ga_ld{half}_{tt}")
            for gi, (ps_, n_) in enumerate(groups):
                psel = half * 2 + ps_
                psum = (c["pA"] if gi < 2 else c["pB"])[:, (gi % 2) * 512:(gi % 2) * 512 + 512]
                mm(P, psum, PTall[:, tt * 512 + psel * 128: tt * 512 + (psel + 1) * 128],
                   lnt[:, n_ * 512:(n_ + 1) * 512],
                   start=(tt == 0), stop=(tt == NT - 1), tag=f"ga_{half}_{tt}_{gi}")
        for gi, (ps_, n_) in enumerate(groups):
            psel = half * 2 + ps_
            psum = (c["pA"] if gi < 2 else c["pB"])[:, (gi % 2) * 512:(gi % 2) * 512 + 512]
            dst = xsel32[:, psel * 1024 + n_ * 512: psel * 1024 + (n_ + 1) * 512]
            vop(P, lambda e, psum=psum, dst=dst: e.tensor_copy(out=dst, in_=psum),
                [psum], [dst], tag=f"ga_ev{half}_{gi}")

    def rmsnorm_to_hT(P, src32, gbc, tag):
        for sl4 in range(4):
            xa = src32[:, sl4 * 1024:(sl4 + 1) * 1024]
            sop(P, lambda e, xa=xa: e.activation(out=tmp32[:], in_=xa, func=AF.Square,
                                                 accum_out=st[:, 1:2]),
                [xa], [tmp32[:], st[:, 1:2]], tag=f"{tag}sq{sl4}")
            vop(P, lambda e: e.tensor_scalar(out=st[:, 2:3], in0=st[:, 1:2], scalar1=1.0 / D,
                                             scalar2=None, op0=ALU.mult),
                [st[:, 1:2]], [st[:, 2:3]])
            epsap = P.constant(EPS_RMS)
            sop(P, lambda e, epsap=epsap: e.activation(out=st[:, 3:4], in_=st[:, 2:3],
                                                       func=AF.Sqrt, bias=epsap),
                [st[:, 2:3], epsap], [st[:, 3:4]])
            vop(P, lambda e: e.reciprocal(out=st[:, 4:5], in_=st[:, 3:4]),
                [st[:, 3:4]], [st[:, 4:5]])
            vop(P, lambda e, xa=xa: e.tensor_scalar(out=tmp32[:], in0=xa,
                    scalar1=st[:, 4:5], scalar2=None, op0=ALU.mult),
                [xa, st[:, 4:5]], [tmp32[:]], tag=f"{tag}n{sl4}")
            vop(P, lambda e, gbc=gbc: e.tensor_tensor(out=h16[:], in0=tmp32[:], in1=gbc[:],
                                                      op=ALU.mult),
                [tmp32[:], gbc[:]], [h16[:]], tag=f"{tag}g{sl4}")
            for k in range(NF):
                pb = c["pT0"] if k % 2 == 0 else c["pT1"]
                slot = (k // 2) * 128
                tr(P, pb[:, slot:slot + 128], h16[:, k * 128:(k + 1) * 128], c["ident16"][:],
                   tag=f"{tag}tr{sl4}_{k}")
                dst = hT[:, k * 512 + sl4 * 128: k * 512 + (sl4 + 1) * 128]
                vop(P, lambda e, pb=pb, slot=slot, dst=dst: e.tensor_copy(out=dst, in_=pb[:, slot:slot + 128]),
                    [pb[:, slot:slot + 128]], [dst], tag=f"{tag}te{sl4}_{k}")

    rmsnorm_to_hT(P, xsel32[:], g1bc, "r1_")
    for qk, dstT in ((0, qT_s), (1, kT_s)):
        for oc in range(NF):
            wb = wbuf[oc % 2].rearrange("p (ki o) -> p ki o", o=256)
            P.dma("pool", wb[:, :, 0:128],
                  G("l3f_wqkvT")[:, qk * 1024 + oc * 128: qk * 1024 + (oc + 1) * 128].rearrange(
                      "(ki p) o -> p ki o", p=128), tag=f"at_w{qk}_{oc}")
            ps = c["pC"][:, 0:512]
            for ki in range(NF):
                mm(P, ps, wb[:, ki, 0:128], hT[:, ki * 512:(ki + 1) * 512],
                   start=(ki == 0), stop=(ki == NF - 1), tag=f"at_qk{qk}_{oc}")
            dst = dstT[:, oc * 512:(oc + 1) * 512]
            vop(P, lambda e, ps=ps, dst=dst: e.tensor_copy(out=dst, in_=ps),
                [ps], [dst], tag=f"at_ev{qk}_{oc}")
    wv512 = wbufT
    for n in range(2):
        P.dma("pool", wv512[:].rearrange("p (ki o) -> p ki o", o=512) if hasattr(wv512, "ap") and False else wv512[:].rearrange("p (ki o) -> p ki o", o=512),
              G("l3f_wqkvT")[:, 2048 + n * 512: 2048 + (n + 1) * 512].rearrange(
                  "(ki p) o -> p ki o", p=128), tag=f"at_wv{n}")
        wvv = wv512[:].rearrange("p (ki o) -> p ki o", o=512)
        for st4 in range(4):
            ps = c["pC"][:, 0:512]
            for ki in range(NF):
                mm(P, ps, hT[:, ki * 512 + st4 * 128: ki * 512 + (st4 + 1) * 128],
                   wvv[:, ki, :], start=(ki == 0), stop=(ki == NF - 1),
                   tag=f"at_v{n}_{st4}")
            dst = vtok[:, st4 * 1024 + n * 512: st4 * 1024 + (n + 1) * 512]
            vop(P, lambda e, ps=ps, dst=dst: e.tensor_copy(out=dst, in_=ps),
                [ps], [dst], tag=f"at_vev{n}_{st4}")
    for h in range(16):
        b0 = (h % 2) * 64
        ck = (h // 2) * 512
        for qt in range(4):
            qh = qT_s[b0:b0 + 64, ck + qt * 128: ck + (qt + 1) * 128]
            kh = kT_s[b0:b0 + 64, ck: ck + 512]
            ps = c["pD"][:, 0:512]
            mm(P, ps, qh, kh, tag=f"la_s{h}_{qt}")
            vop(P, lambda e, ps=ps: e.reduce_max(out=st[:, 10:11], in_=ps, axis=AX.X),
                [ps], [st[:, 10:11]], tag=f"la_mx{h}_{qt}")
            vop(P, lambda e: e.tensor_scalar(out=st[:, 11:12], in0=st[:, 10:11],
                    scalar1=-0.125, scalar2=None, op0=ALU.mult),
                [st[:, 10:11]], [st[:, 11:12]], tag=f"la_nm{h}_{qt}")
            sop(P, lambda e, ps=ps: e.activation(out=p16[:], in_=ps, func=AF.Exp,
                    scale=0.125, bias=st[:, 11:12]),
                [ps, st[:, 11:12]], [p16[:]], tag=f"la_e{h}_{qt}")
            vop(P, lambda e, qt=qt: e.tensor_tensor(out=p16[:], in0=p16[:],
                    in1=cmask[:, qt * CAP:(qt + 1) * CAP], op=ALU.mult),
                [p16[:], cmask[:, qt * CAP:(qt + 1) * CAP]], [p16[:]], tag=f"la_m{h}_{qt}")
            vop(P, lambda e: e.reduce_sum(out=st[:, 12:13], in_=p16[:], axis=AX.X),
                [p16[:]], [st[:, 12:13]], tag=f"la_d{h}_{qt}")
            vop(P, lambda e: e.reciprocal(out=st[:, 13:14], in_=st[:, 12:13]),
                [st[:, 12:13]], [st[:, 13:14]], tag=f"la_r{h}_{qt}")
            for ks in range(qt + 1):
                pb = c["pT0"] if ks % 2 == 0 else c["pT1"]
                slot = 0
                tr(P, pb[:, slot:slot + 128], p16[:, ks * 128:(ks + 1) * 128],
                   c["ident16"][:], tag=f"la_t{h}_{qt}_{ks}")
                dst = pt16[:, ks * 128:(ks + 1) * 128]
                vop(P, lambda e, pb=pb, slot=slot, dst=dst: e.tensor_copy(out=dst, in_=pb[:, slot:slot + 128]),
                    [pb[:, slot:slot + 128]], [dst], tag=f"la_pe{h}_{qt}_{ks}")
            for ks in range(qt + 1):
                mm(P, c["pC"][:, 0:64], pt16[:, ks * 128:(ks + 1) * 128],
                   vtok[:, ks * 1024 + h * 64: ks * 1024 + h * 64 + 64],
                   start=(ks == 0), stop=(ks == qt), tag=f"la_pv{h}_{qt}_{ks}")
            dst = osel[:, qt * 1024 + h * 64: qt * 1024 + h * 64 + 64]
            vop(P, lambda e, dst=dst: e.tensor_scalar(out=dst, in0=c["pC"][:, 0:64],
                    scalar1=st[:, 13:14], scalar2=None, op0=ALU.mult),
                [c["pC"][:, 0:64], st[:, 13:14]], [dst], tag=f"la_y{h}_{qt}")
    P.dma("pool", wout16, G("l3f_woutT").rearrange("(ki p) o -> p ki o", p=128),
          tag="wo_ld")
    for st4 in range(4):
        for k in range(NF):
            pb = c["pT0"] if k % 2 == 0 else c["pT1"]
            slot = (k // 2) * 128
            tr(P, pb[:, slot:slot + 128],
               osel[:, st4 * 1024 + k * 128: st4 * 1024 + (k + 1) * 128],
               c["ident16"][:], tag=f"wo_tr{st4}_{k}")
            dst = hT[:, k * 512 + st4 * 128: k * 512 + (st4 + 1) * 128]
            vop(P, lambda e, pb=pb, slot=slot, dst=dst: e.tensor_copy(out=dst, in_=pb[:, slot:slot + 128]),
                [pb[:, slot:slot + 128]], [dst], tag=f"wo_te{st4}_{k}")
    for st4 in range(4):
        for n in range(2):
            ps = c["pC"][:, 0:512]
            for k in range(NF):
                mm(P, ps, hT[:, k * 512 + st4 * 128: k * 512 + (st4 + 1) * 128],
                   wout16[:, k, n * 512:(n + 1) * 512],
                   start=(k == 0), stop=(k == NF - 1), tag=f"wo_mm{st4}_{n}")
            dst = xsel32[:, st4 * 1024 + n * 512: st4 * 1024 + (n + 1) * 512]
            vop(P, lambda e, ps=ps, dst=dst: e.tensor_tensor(out=dst, in0=ps, in1=dst,
                                                             op=ALU.add),
                [ps, dst], [dst], tag=f"wo_a{st4}_{n}")
    rmsnorm_to_hT(P, xsel32[:], g2bc, "r2_")
    for j in range(22):
        wb = wbuf[j % 2].rearrange("p (ki o) -> p ki o", o=256)
        P.dma("pool", wb[:, :, 0:128],
              G("l3f_w1T")[:, j * 128:(j + 1) * 128].rearrange("(ki p) o -> p ki o", p=128),
              tag=f"ff_w1{j}")
        P.dma("pool", wb[:, :, 128:256],
              G("l3f_w2T")[:, j * 128:(j + 1) * 128].rearrange("(ki p) o -> p ki o", p=128),
              tag=f"ff_w2{j}")
        ps1 = c["pC"][:, 0:512]
        ps2 = c["pD"][:, 0:512]
        for ki in range(NF):
            mm(P, ps1, wb[:, ki, 0:128], hT[:, ki * 512:(ki + 1) * 512],
               start=(ki == 0), stop=(ki == NF - 1), tag=f"ff_u1{j}")
        for ki in range(NF):
            mm(P, ps2, wb[:, ki, 128:256], hT[:, ki * 512:(ki + 1) * 512],
               start=(ki == 0), stop=(ki == NF - 1), tag=f"ff_u2{j}")
        sop(P, lambda e, ps1=ps1: e.activation(out=s16[:], in_=ps1, func=AF.Silu),
            [ps1], [s16[:]], tag=f"ff_si{j}")
        dst = hselv[:, j, :]
        vop(P, lambda e, ps2=ps2, dst=dst: e.tensor_tensor(out=dst, in0=ps2, in1=s16[:],
                                                           op=ALU.mult),
            [ps2, s16[:]], [dst], tag=f"ff_hh{j}")
    P.dma("pool", w3v, G("l3f_w3T").rearrange("(j p) o -> p j o", p=128), tag="ff_w3")
    for st4 in range(4):
        for n in range(2):
            ps = c["pC"][:, 0:512]
            for j in range(22):
                mm(P, ps, hselv[:, j, st4 * 128:(st4 + 1) * 128],
                   w3v[:, j, n * 512:(n + 1) * 512],
                   start=(j == 0), stop=(j == 21), tag=f"ff_o{st4}_{n}")
            dst = xsel32[:, st4 * 1024 + n * 512: st4 * 1024 + (n + 1) * 512]
            vop(P, lambda e, ps=ps, dst=dst: e.tensor_tensor(out=dst, in0=ps, in1=dst,
                                                             op=ALU.add),
                [ps, dst], [dst], tag=f"ff_a{st4}_{n}")
    for st4 in range(4):
        src = xsel32[:, st4 * 1024:(st4 + 1) * 1024]
        dst = xp16[:, st4 * 1024:(st4 + 1) * 1024]
        vop(P, lambda e, src=src, dst=dst: e.tensor_copy(out=dst, in_=src),
            [src], [dst], tag=f"xp_{st4}")

    for tt in range(NT):
        xa = c["xtm"][tt % 2]
        lnt = c["lntile"][tt % 2]
        P.dma("sp", xa[:], xin[tt * 128:(tt + 1) * 128, :], tag=f"fo_ldx{tt}")
        P.dma("sp", lnt[:], xlntm[tt * 128:(tt + 1) * 128, :], tag=f"fo_ldl{tt}")
        ot = c["otm"][0]
        for n in range(2):
            ps = c["pB"][:, n * 512:(n + 1) * 512]
            for pc in range(4):
                mm(P, ps, Pall[:, pc * S + tt * 128:pc * S + (tt + 1) * 128],
                   xp16[:, pc * 1024 + n * 512: pc * 1024 + (n + 1) * 512],
                   start=(pc == 0), stop=(pc == 3), tag=f"fo_mm{tt}_{n}")
            vop(P, lambda e, lnt=lnt, n=n, tt=tt: e.tensor_scalar(
                    out=tmp32[:, n * 512:(n + 1) * 512], in0=lnt[:, n * 512:(n + 1) * 512],
                    scalar1=nscol[:, tt:tt + 1], scalar2=None, op0=ALU.mult),
                [lnt[:, n * 512:(n + 1) * 512], nscol[:, tt:tt + 1]],
                [tmp32[:, n * 512:(n + 1) * 512]], tag=f"fo_ns{tt}_{n}")
            vop(P, lambda e, xa=xa, n=n: e.tensor_tensor(out=tmp32[:, n * 512:(n + 1) * 512],
                    in0=tmp32[:, n * 512:(n + 1) * 512], in1=xa[:, n * 512:(n + 1) * 512],
                    op=ALU.add),
                [tmp32[:, n * 512:(n + 1) * 512], xa[:, n * 512:(n + 1) * 512]],
                [tmp32[:, n * 512:(n + 1) * 512]], tag=f"fo_x{tt}_{n}")
            vop(P, lambda e, ps=ps, n=n, ot=ot: e.tensor_tensor(
                    out=ot[:, n * 512:(n + 1) * 512], in0=ps,
                    in1=tmp32[:, n * 512:(n + 1) * 512], op=ALU.add),
                [ps, tmp32[:, n * 512:(n + 1) * 512]], [ot[:, n * 512:(n + 1) * 512]],
                tag=f"fo_s{tt}_{n}")
        P.dma("act", xout[tt * 128:(tt + 1) * 128, :], ot[:], tag=f"fo_st{tt}")


STAGE_FUNCS["l3_ffn"] = dict(prep=lambda inputs, carry: prep_l3_ffn(inputs),
                             dram=dram_l3_ffn, emit=emit_l3_ffn, extra=[])


# version 02 (final)
# speedup vs baseline: 1.0008x; 1.0008x over previous
import sys
sys.path.insert(0, '/opt/trn_rl_repo')
import math
import numpy as np
import ml_dtypes
from contextlib import ExitStack

import concourse.bass as bass
import concourse.mybir as mybir
from concourse.bass_utils import run_bass_kernel_spmd

F32 = mybir.dt.float32
BF16 = mybir.dt.bfloat16
I32 = mybir.dt.int32
AF = mybir.ActivationFunctionType
ALU = mybir.AluOpType
AX = mybir.AxisListType

N_CORES = 8
S = 4096
D = 1024
NT = S // 128
NF = D // 128

BF = ml_dtypes.bfloat16



class _Op:
    __slots__ = ("eng", "fn", "is_dma", "idx", "need_inc", "inc_no", "deps", "tag",
                 "sem_key", "sem_val", "slot_prior")
    def __init__(self, eng, fn, is_dma, idx, tag):
        self.eng = eng; self.fn = fn; self.is_dma = is_dma
        self.idx = idx; self.need_inc = is_dma; self.inc_no = None
        self.deps = []
        self.tag = tag
        self.sem_key = None; self.sem_val = None; self.slot_prior = 0


def _ap_box(ap):
    if not hasattr(ap, "tensor"):
        ap = ap[:]
    t = ap.tensor
    name = t.name
    sp = str(ap.space)
    if sp == "MemorySpace.PSUM":
        return (name, 0, 128, 0, 1 << 30)
    es = mybir.dt.size(ap.dtype)
    shape = ap.shape
    apl = ap.ap
    if sp == "MemorySpace.SBUF":
        row = apl[0][0]
        p0 = (ap.offset // row) if row else 0
        p1 = p0 + shape[0]
        off = ap.offset - p0 * row
        span = 1
        for (st, cnt) in apl[1:]:
            span += abs(st) * (cnt - 1)
        return (name, p0, p1, off * es, (off + span) * es)
    else:
        off = ap.offset
        span = 1
        for (st, cnt) in apl:
            span += abs(st) * (cnt - 1)
        return (name, 0, 1, off * es, (off + span) * es)


def _overlap(a, b):
    return (a[0] == b[0] and a[1] < b[2] and b[1] < a[2]
            and a[3] < b[4] and b[3] < a[4])


def _covers(a, b):
    return a[0] == b[0] and a[1] <= b[1] and a[2] >= b[2] and a[3] <= b[3] and a[4] >= b[4]


class Prog:

    ENGINES = ("pe", "dve", "act", "pool", "sp")

    def __init__(self, name="prog", serial=False):
        self.nc = bass.Bass("TRN2", target_bir_lowering=False, name=name)
        self.serial = serial
        self.ops = {e: [] for e in self.ENGINES}
        self.order = []
        self.writers = {}
        self.readers = {}
        self.ctx = ExitStack()
        self._tensors = {}
        self._finished = False
        self._nin = {}
        self._ninputs = []
        self._constvals = {}
        self._const_t = None
        self._scopes = []
        self._names = set()
        self._barrier_deps = []

    def _uniq(self, name):
        if name in self._names:
            i = 2
            while f"{name}_{i}" in self._names:
                i += 1
            name = f"{name}_{i}"
        self._names.add(name)
        return name

    def _alloc(self, maker, name, shape, dtype):
        name = self._uniq(name)
        cm = maker(name, list(shape), dtype)
        t = cm.__enter__()
        if self._scopes:
            self._scopes[-1].append(cm)
        else:
            self.ctx.callback(cm.__exit__, None, None, None)
        self._tensors[name] = t
        return t

    def sbuf(self, name, shape, dtype):
        return self._alloc(self.nc.sbuf_tensor, name, shape, dtype)

    def psum(self, name, shape, dtype=F32):
        return self._alloc(self.nc.psum_tensor, name, shape, dtype)

    def begin_scope(self):
        self.constant(0.0)
        self._scopes.append([])

    def end_scope(self):
        for cm in reversed(self._scopes.pop()):
            cm.__exit__(None, None, None)

    def barrier(self):
        deps = []
        for e in self.ENGINES:
            if self.ops[e]:
                o = self.ops[e][-1]
                o.need_inc = True
                deps.append(o)
        self._barrier_deps = deps

    def dram_in(self, name, shape, dtype=F32):
        ap = self.nc.dram_tensor(name, list(shape), dtype, kind="ExternalInput").ap()
        self._ninputs.append(name)
        return ap

    def dram_out(self, name, shape, dtype=F32):
        return self.nc.dram_tensor(name, list(shape), dtype, kind="ExternalOutput").ap()

    def dram_tmp(self, name, shape, dtype=F32):
        return self.nc.dram_tensor(name, list(shape), dtype, kind="Internal").ap()

    def constant(self, val):
        val = float(val)
        if self._const_t is None:
            sc = self._scopes; self._scopes = []
            self._const_t = self.sbuf("_consts_sb", [128, 32], F32)
            self._scopes = sc
            self._const_d = self.dram_in("_consts", (128, 32), F32)
            self.dma("sp", self._const_t[:], self._const_d, tag="ld_consts")
        if val not in self._constvals:
            j = len(self._constvals)
            assert j < 32
            self._constvals[val] = j
        j = self._constvals[val]
        return self._const_t[:, j:j + 1]

    def const_array(self):
        a = np.zeros((128, 32), np.float32)
        for v, j in self._constvals.items():
            a[:, j] = v
        return a

    def op(self, eng, fn, reads, writes, tag=""):
        assert not self._finished
        o = _Op(eng, fn, False, len(self.ops[eng]), tag)
        self._record(o, reads, writes)
        return o

    def dma(self, queue, out_ap, in_ap, tag="", **kw):
        assert queue in ("sp", "act", "pool")
        fn = lambda e: e.dma_start(out=out_ap, in_=in_ap, **kw)
        o = _Op(queue, fn, True, len(self.ops[queue]), tag)
        self._record(o, [in_ap], [out_ap])
        return o

    def _record(self, o, reads, writes):
        rboxes = [_ap_box(a) for a in reads]
        wboxes = [_ap_box(a) for a in writes]
        deps = list(self._barrier_deps)
        if self.serial and self.order:
            deps.append(self.order[-1])
        for b in rboxes:
            for (wb, wop) in self.writers.get(b[0], ()):
                if _overlap(b, wb):
                    deps.append(wop)
        for b in wboxes:
            for (wb, wop) in self.writers.get(b[0], ()):
                if _overlap(b, wb):
                    deps.append(wop)
            for (rb, rop) in self.readers.get(b[0], ()):
                if _overlap(b, rb):
                    deps.append(rop)
        seen = set()
        for d in deps:
            if id(d) in seen or d is o:
                continue
            seen.add(id(d))
            o.deps.append(d)
            d.need_inc = True
        for b in wboxes:
            wl = self.writers.setdefault(b[0], [])
            wl[:] = [(wb, wop) for (wb, wop) in wl if not _covers(b, wb)]
            wl.append((b, o))
            rl = self.readers.setdefault(b[0], [])
            rl[:] = [(rb, rop) for (rb, rop) in rl if not _covers(b, rb)]
        for b in rboxes:
            self.readers.setdefault(b[0], []).append((b, o))
        self.ops[o.eng].append(o)
        self.order.append(o)

    DMA_SLOTS = 8

    def finish(self):
        assert not self._finished
        self._finished = True
        nc = self.nc
        W = self.DMA_SLOTS
        for e in self.ENGINES:
            if self.ops[e]:
                self.ops[e][-1].need_inc = True
        inc_totals = {}
        dma_idx = {e: 0 for e in self.ENGINES}
        slot_uses = {}
        for e in self.ENGINES:
            cum = 0
            for o in self.ops[e]:
                if o.is_dma:
                    s = dma_idx[e] % W
                    dma_idx[e] += 1
                    key = (e, s)
                    prior = slot_uses.get(key, 0)
                    o.sem_key = key
                    o.sem_val = 16 * (prior + 1)
                    o.slot_prior = 16 * prior
                    slot_uses[key] = prior + 1
                else:
                    if o.need_inc:
                        cum += 1
                        o.sem_key = e
                        o.sem_val = cum
            inc_totals[e] = cum
        sems = {}
        for e in self.ENGINES:
            sems[e] = self.ctx.enter_context(nc.semaphore(f"sem_{e}"))
        for (q, s) in slot_uses:
            sems[(q, s)] = self.ctx.enter_context(nc.semaphore(f"dsem_{q}{s}"))
        block = self.ctx.enter_context(nc.Block())

        def emit(e, engobj):
            hw = {}
            def wait(key, val):
                if val > hw.get(key, 0):
                    engobj.wait_ge(sems[key], val)
                    hw[key] = val
            for o in self.ops[e]:
                for d in o.deps:
                    assert getattr(d, "sem_key", None) is not None, (e, o.tag, d.eng, d.tag)
                    wait(d.sem_key, d.sem_val)
                if o.is_dma and o.slot_prior > 0:
                    wait(o.sem_key, o.slot_prior)
                inst = o.fn(engobj)
                if o.is_dma:
                    inst.then_inc(sems[o.sem_key], 16)
                elif o.need_inc:
                    inst.then_inc(sems[e], 1)

        @block.tensor
        def _(e):
            emit("pe", e)

        @block.vector
        def _(e):
            emit("dve", e)

        @block.scalar
        def _(e):
            emit("act", e)

        @block.sync
        def _(e):
            emit("sp", e)

        @block.gpsimd
        def _(e):
            emit("pool", e)
            for p in self.ENGINES:
                if p != "pool" and inc_totals[p]:
                    e.wait_ge(sems[p], inc_totals[p])
            if inc_totals["pool"]:
                e.wait_ge(sems["pool"], inc_totals["pool"])
            for key, uses in slot_uses.items():
                e.wait_ge(sems[key], 16 * uses)

        self.ctx.close()
        return nc



def vop(P, fn, reads, writes, tag=""):
    return P.op("dve", fn, reads, writes, tag)

def sop(P, fn, reads, writes, tag=""):
    return P.op("act", fn, reads, writes, tag)

def gop(P, fn, reads, writes, tag=""):
    return P.op("pool", fn, reads, writes, tag)

def mm(P, out, lhsT, rhs, start=True, stop=True, tag=""):
    return P.op("pe", lambda e: e.matmul(out=out, lhsT=lhsT, rhs=rhs,
                                         start=start, stop=stop),
                [lhsT, rhs], [out], tag)

def tr(P, out, in_, ident, tag=""):
    return P.op("pe", lambda e: e.transpose(out=out, in_=in_, identity=ident),
                [in_, ident], [out], tag)



EPS_LN = 1e-5


def _rope_tables():
    half = 32
    inv_freq = 10000.0 ** (-(np.arange(half, dtype=np.float64) * 2.0 / 64.0))
    t = np.arange(S, dtype=np.float64)
    ang = t[:, None] * inv_freq[None, :]
    cos = np.cos(ang); sin = np.sin(ang)
    CC = np.empty((128, S), np.float32)
    SS = np.empty((128, S), np.float32)
    for p in range(128):
        i = (p % 64) // 2
        CC[p] = cos[:, i]
        SS[p] = (-sin[:, i]) if (p % 2 == 0) else sin[:, i]
    return CC.astype(BF), SS.astype(BF)


def _perm_swap16():
    M = np.zeros((128, 128), np.float32)
    for i in range(64):
        M[2 * i, 2 * i + 1] = 1.0
        M[2 * i + 1, 2 * i] = 1.0
    return M.astype(BF)


def _ident16():
    return np.eye(128, dtype=np.float32).astype(BF)


def _tri16():
    return np.tril(np.ones((128, 128), np.float32)).astype(BF)


def alloc_common(P, notm=2, psums=True):
    c = {}
    c["ident16"] = P.sbuf("c_ident16", [128, 128], BF16)
    c["stats"] = P.sbuf("c_stats", [128, 32], F32)
    c["xtm"] = [P.sbuf(f"c_xtm{i}", [128, 1024], F32) for i in range(2)]
    c["otm"] = [P.sbuf(f"c_otm{i}", [128, 1024], F32) for i in range(max(1, notm))]
    if notm == 1:
        c["otm"] = [c["otm"][0], c["otm"][0]]
    c["sq"] = c["otm"][0]
    c["lntile"] = [P.sbuf(f"c_lnt{i}", [128, 1024], BF16) for i in range(2)]
    if not psums:
        return c
    c["pA"] = P.psum("p_A", [128, 1024], F32)
    c["pB"] = P.psum("p_B", [128, 1024], F32)
    c["pC"] = P.psum("p_C", [128, 512], F32)
    c["pT0"] = P.psum("p_T0", [128, 1024], BF16)
    c["pT1"] = P.psum("p_T1", [128, 1024], BF16)
    c["pD"] = P.psum("p_D", [128, 512], F32)
    return c


def load_const(P, c, G, names):
    for nm, dst in names:
        P.dma("sp", dst[:], G(nm), tag=f"const_{nm}")
        if dst is c.get("ident16"):
            c["_ident_loaded"] = True


def ensure_ident(P, c, G, name):
    if not c.get("_ident_loaded"):
        P.dma("sp", c["ident16"][:], G(name), tag="const_ident")
        c["_ident_loaded"] = True


def emit_ln(P, c, x_hbm, big, tt_range=None, out_off=0):
    st = c["stats"]
    for tt in (tt_range or range(NT)):
        xa = c["xtm"][tt % 2]
        lnt = c["lntile"][tt % 2]
        P.dma("sp", xa[:], x_hbm[tt * 128:(tt + 1) * 128, :], tag=f"ln_ld{tt}")
        sop(P, lambda e, xa=xa: e.activation(out=c["sq"][:], in_=xa[:], func=AF.Square,
                                             accum_out=st[:, 1:2]),
            [xa[:]], [c["sq"][:], st[:, 1:2]], tag="ln_sq")
        vop(P, lambda e, xa=xa: e.reduce_sum(out=st[:, 0:1], in_=xa[:], axis=AX.X),
            [xa[:]], [st[:, 0:1]], tag="ln_sum")
        vop(P, lambda e: e.tensor_scalar(out=st[:, 2:3], in0=st[:, 0:1], scalar1=1.0 / D,
                                         scalar2=None, op0=ALU.mult),
            [st[:, 0:1]], [st[:, 2:3]])
        vop(P, lambda e: e.tensor_scalar(out=st[:, 3:4], in0=st[:, 1:2], scalar1=1.0 / D,
                                         scalar2=None, op0=ALU.mult),
            [st[:, 1:2]], [st[:, 3:4]])
        vop(P, lambda e: e.tensor_tensor(out=st[:, 4:5], in0=st[:, 2:3], in1=st[:, 2:3],
                                         op=ALU.mult),
            [st[:, 2:3]], [st[:, 4:5]])
        vop(P, lambda e: e.tensor_sub(out=st[:, 5:6], in0=st[:, 3:4], in1=st[:, 4:5]),
            [st[:, 3:4], st[:, 4:5]], [st[:, 5:6]])
        epsap = P.constant(EPS_LN)
        sop(P, lambda e, epsap=epsap: e.activation(out=st[:, 6:7], in_=st[:, 5:6], func=AF.Sqrt,
                                      bias=epsap),
            [st[:, 5:6], epsap], [st[:, 6:7]])
        vop(P, lambda e: e.reciprocal(out=st[:, 7:8], in_=st[:, 6:7]),
            [st[:, 6:7]], [st[:, 7:8]])
        vop(P, lambda e: e.tensor_tensor(out=st[:, 8:9], in0=st[:, 2:3], in1=st[:, 7:8],
                                         op=ALU.mult),
            [st[:, 2:3], st[:, 7:8]], [st[:, 8:9]])
        vop(P, lambda e: e.tensor_scalar(out=st[:, 9:10], in0=st[:, 8:9], scalar1=-1.0,
                                         scalar2=None, op0=ALU.mult),
            [st[:, 8:9]], [st[:, 9:10]])
        vop(P, lambda e, xa=xa, lnt=lnt: e.tensor_scalar(out=lnt[:], in0=xa[:],
                scalar1=st[:, 7:8], scalar2=st[:, 9:10], op0=ALU.mult, op1=ALU.add),
            [xa[:], st[:, 7:8], st[:, 9:10]], [lnt[:]], tag="ln_norm")
        for k in range(NF):
            pb = c["pT0"] if k % 2 == 0 else c["pT1"]
            slot = (k // 2) * 128
            tr(P, pb[:, slot:slot + 128], lnt[:, k * 128:(k + 1) * 128], c["ident16"][:],
               tag=f"ln_tr{tt}_{k}")
            dst = big[:, out_off + k * S + tt * 128: out_off + k * S + (tt + 1) * 128]
            vop(P, lambda e, pb=pb, slot=slot, dst=dst: e.tensor_copy(out=dst, in_=pb[:, slot:slot + 128]),
                [pb[:, slot:slot + 128]], [dst], tag="ln_ev")



def prep_l0_attn(inputs):
    CC, SS = _rope_tables()
    return {
        "l0a_wqT": np.ascontiguousarray(inputs["L0_attn_w_q"].T),
        "l0a_wkT": np.ascontiguousarray(inputs["L0_attn_w_k"].T),
        "l0a_wvT": np.ascontiguousarray(inputs["L0_attn_w_v"].T),
        "l0a_woT": np.ascontiguousarray(inputs["L0_attn_w_o"].T),
        "l0a_cc": CC, "l0a_ss": SS,
        "l0a_perm": _perm_swap16(), "l0a_ident": _ident16(), "l0a_tri": _tri16(),
        **prep_router("l0c_", inputs["L0_ffn_router_w"], inputs["L0_ffn_router_b"]),
    }


def dram_l0_attn(P):
    g = {}
    for nm in ("l0a_wqT", "l0a_wkT", "l0a_wvT", "l0a_woT"):
        g[nm] = P.dram_in(nm, (D, D), F32)
    g["l0a_cc"] = P.dram_in("l0a_cc", (128, S), BF16)
    g["l0a_ss"] = P.dram_in("l0a_ss", (128, S), BF16)
    g["l0a_perm"] = P.dram_in("l0a_perm", (128, 128), BF16)
    g["l0a_ident"] = P.dram_in("l0a_ident", (128, 128), BF16)
    g["l0a_tri"] = P.dram_in("l0a_tri", (128, 128), BF16)
    g["l0a_qTd"] = P.dram_tmp("l0a_qTd", (16, 64, S), BF16)
    g["l0a_kTd"] = P.dram_tmp("l0a_kTd", (16, 64, S), BF16)
    g.update(dram_router(P, "l0c_"))
    g["moe_counts"] = P.dram_out("moe_counts", (1, 8), F32)
    return g


def emit_l0_attn(P, xin, xout, G):
    c = alloc_common(P)
    big = P.sbuf("l0a_big", [128, NF * S], BF16)
    vsb = P.sbuf("l0a_vsb", [128, NT * 1024], BF16)
    w16 = P.sbuf("l0a_w16", [128, 16384], BF16)
    wA = w16[:, 0:8192].rearrange("p (ki o) -> p ki o", o=1024)
    wB = w16[:, 8192:16384].rearrange("p (ki o) -> p ki o", o=1024)
    cc16 = w16[:, 8192:12288]
    ss16 = w16[:, 12288:16384]
    kbuf = w16[0:64, 8192:12288]
    qbuf = w16[0:64, 12288:16384]
    perm16 = P.sbuf("l0a_perm16", [128, 128], BF16)
    tri16 = P.sbuf("l0a_tri16", [128, 128], BF16)
    p16 = [P.sbuf(f"l0a_p16_{i}", [128, 1024], BF16) for i in range(2)]
    pm16 = P.sbuf("l0a_pm16", [128, 128], BF16)
    pt16 = P.sbuf("l0a_pt16", [128, 512], BF16)
    att16 = P.sbuf("l0a_att16", [128, 64], BF16)
    qf16 = [P.sbuf("l0a_qf16_0", [128, 512], BF16)] * 2
    tb32 = [P.sbuf("l0a_tb_0", [128, 512], F32)] * 2
    ub32 = [P.sbuf("l0a_ub_0", [128, 512], F32)] * 2
    rp16 = [P.sbuf(f"l0a_rp_{i}", [128, 512], BF16) for i in range(2)]
    dsum = P.sbuf("l0a_dsum", [128, 8], F32)

    load_const(P, c, G, [("l0a_ident", c["ident16"]), ("l0a_perm", perm16),
                         ("l0a_tri", tri16)])

    emit_ln(P, c, xin, big)

    P.dma("pool", wB, G("l0a_wvT").rearrange("(ki p) o -> p ki o", p=128), tag="ld_wv")
    for tt in range(NT):
        for n in range(2):
            ps = c["pA"][:, n * 512:(n + 1) * 512]
            for ki in range(NF):
                mm(P, ps, big[:, ki * S + tt * 128:ki * S + (tt + 1) * 128],
                   wB[:, ki, n * 512:(n + 1) * 512],
                   start=(ki == 0), stop=(ki == NF - 1), tag=f"v_mm{tt}_{n}")
            dst = vsb[:, tt * 1024 + n * 512: tt * 1024 + (n + 1) * 512]
            vop(P, lambda e, ps=ps, dst=dst: e.tensor_copy(out=dst, in_=ps),
                [ps], [dst], tag=f"v_ev{tt}_{n}")

    P.dma("sp", w16[:, 8192:12288], G("l0a_cc"), tag="ld_cc")
    P.dma("sp", w16[:, 12288:16384], G("l0a_ss"), tag="ld_ss")
    for qk, (wname, dnam) in enumerate([("l0a_wqT", "l0a_qTd"), ("l0a_wkT", "l0a_kTd")]):
        P.dma("pool", wA, G(wname).rearrange("(ki p) o -> p ki o", p=128), tag=f"ld_w{qk}")
        dstd = G(dnam)
        for oc in range(NF):
            for tb in range(8):
                r = (oc * 8 + tb) % 2
                ps = c["pB"][:, r * 512:(r + 1) * 512]
                for ki in range(NF):
                    mm(P, ps, wA[:, ki, oc * 128:(oc + 1) * 128],
                       big[:, ki * S + tb * 512:ki * S + (tb + 1) * 512],
                       start=(ki == 0), stop=(ki == NF - 1), tag=f"qk_mm{qk}_{oc}_{tb}")
                qf = qf16[r][:]; tb_ = tb32[r][:]; ub_ = ub32[r][:]; rp = rp16[r]
                vop(P, lambda e, ps=ps, qf=qf: e.tensor_copy(out=qf, in_=ps),
                    [ps], [qf], tag="rope_cast")
                swp = c["pC"][:, 0:512]
                mm(P, swp, perm16[:], qf, tag="rope_swap")
                ccs = cc16[:, tb * 512:(tb + 1) * 512]
                sss = ss16[:, tb * 512:(tb + 1) * 512]
                vop(P, lambda e, ps=ps, ccs=ccs, tb_=tb_: e.tensor_tensor(out=tb_, in0=ps, in1=ccs, op=ALU.mult),
                    [ps, ccs], [tb_], tag="rope_cc")
                vop(P, lambda e, swp=swp, sss=sss, ub_=ub_: e.tensor_tensor(out=ub_, in0=swp, in1=sss, op=ALU.mult),
                    [swp, sss], [ub_], tag="rope_ss")
                vop(P, lambda e, tb_=tb_, ub_=ub_, rp=rp: e.tensor_add(out=rp[:], in0=tb_, in1=ub_),
                    [tb_, ub_], [rp[:]], tag="rope_add")
                P.dma("act", dstd[2 * oc, :, tb * 512:(tb + 1) * 512], rp[0:64, :], tag="rope_st0")
                P.dma("act", dstd[2 * oc + 1, :, tb * 512:(tb + 1) * 512], rp[64:128, :], tag="rope_st1")

    P.dma("pool", wA, G("l0a_woT").rearrange("(ki p) o -> p ki o", p=128), tag="ld_wo")
    for h in range(16):
        P.dma("sp", kbuf[:], G("l0a_kTd")[h], tag=f"fl_ldk{h}")
        P.dma("sp", qbuf[:], G("l0a_qTd")[h], tag=f"fl_ldq{h}")
        for qt in range(NT):
            npair_pv = qt // 8
            pd = qt // 8
            for pi in range(4):
                ps2 = c["pA"] if pi % 2 == 0 else c["pB"]
                lhs = qbuf[:, qt * 128:(qt + 1) * 128]
                mm(P, ps2[:, 0:512], lhs, kbuf[:, pi * 1024:pi * 1024 + 512],
                   tag=f"fl_s{h}_{qt}_{pi}a")
                mm(P, ps2[:, 512:1024], lhs, kbuf[:, pi * 1024 + 512:(pi + 1) * 1024],
                   tag=f"fl_s{h}_{qt}_{pi}b")
                pbuf = p16[pi % 2]
                sh30 = P.constant(-30.0)
                sop(P, lambda e, ps2=ps2, pbuf=pbuf, pi=pi, sh30=sh30: e.activation(
                        out=pbuf[:], in_=ps2[:], func=AF.Exp, scale=0.125, bias=sh30,
                        accum_out=dsum[:, pi:pi + 1]),
                    [ps2[:], sh30], [pbuf[:], dsum[:, pi:pi + 1]], tag=f"fl_exp{h}_{qt}_{pi}")
                kt0 = pi * 4
                nsub = min(4, qt + 1 - kt0)
                if nsub <= 0:
                    continue
                for j in range(nsub):
                    kt = kt0 + j
                    src = pbuf[:, j * 128:(j + 1) * 128]
                    if kt == qt:
                        vop(P, lambda e, src=src: e.tensor_tensor(out=pm16[:], in0=src,
                                                                  in1=tri16[:], op=ALU.mult),
                            [src, tri16[:]], [pm16[:]], tag="fl_mask")
                        src = pm16[:]
                    pb = c["pT1"]
                    tr(P, pb[:, j * 128:(j + 1) * 128], src, c["ident16"][:],
                       tag=f"fl_tr{h}_{qt}_{kt}")
                ev = pt16[:, 0:nsub * 128]
                vop(P, lambda e, ev=ev, nsub=nsub: e.tensor_copy(out=ev, in_=c["pT1"][:, 0:nsub * 128]),
                    [c["pT1"][:, 0:nsub * 128]], [ev], tag="fl_ptev")
                for j in range(nsub):
                    kt = kt0 + j
                    mm(P, c["pC"][:, 0:64], pt16[:, j * 128:(j + 1) * 128],
                       vsb[:, kt * 1024 + h * 64: kt * 1024 + h * 64 + 64],
                       start=(kt == 0), stop=(kt == qt), tag=f"fl_pv{h}_{qt}_{kt}")
            st = c["stats"]
            vop(P, lambda e: e.reduce_sum(out=st[:, 0:1], in_=dsum[:, 0:4], axis=AX.X),
                [dsum[:, 0:4]], [st[:, 0:1]], tag="fl_den")
            vop(P, lambda e: e.reciprocal(out=st[:, 1:2], in_=st[:, 0:1]),
                [st[:, 0:1]], [st[:, 1:2]], tag="fl_rden")
            vop(P, lambda e: e.tensor_scalar(out=att16[:], in0=c["pC"][:, 0:64],
                                             scalar1=st[:, 1:2], scalar2=None, op0=ALU.mult),
                [c["pC"][:, 0:64], st[:, 1:2]], [att16[:]], tag="fl_div")
            tr(P, c["pT0"][0:64, 0:128], att16[:], c["ident16"][:], tag="fl_trout")
            ko = h // 2
            po = (h % 2) * 64
            dst = big[po:po + 64, ko * S + qt * 128: ko * S + (qt + 1) * 128]
            vop(P, lambda e, dst=dst: e.tensor_copy(out=dst, in_=c["pT0"][0:64, 0:128]),
                [c["pT0"][0:64, 0:128]], [dst], tag="fl_attev")

    for tt in range(NT):
        xa = c["xtm"][tt % 2]
        ot = c["otm"][tt % 2]
        P.dma("sp", xa[:], xin[tt * 128:(tt + 1) * 128, :], tag=f"wo_ldx{tt}")
        for n in range(2):
            ps = c["pB"][:, n * 512:(n + 1) * 512]
            for ki in range(NF):
                mm(P, ps, big[:, ki * S + tt * 128:ki * S + (tt + 1) * 128],
                   wA[:, ki, n * 512:(n + 1) * 512],
                   start=(ki == 0), stop=(ki == NF - 1), tag=f"wo_mm{tt}_{n}")
            vop(P, lambda e, ps=ps, xa=xa, n=n, ot=ot: e.tensor_tensor(
                    out=ot[:, n * 512:(n + 1) * 512], in0=ps,
                    in1=xa[:, n * 512:(n + 1) * 512], op=ALU.add),
                [ps, xa[:, n * 512:(n + 1) * 512]], [ot[:, n * 512:(n + 1) * 512]],
                tag=f"wo_add{tt}_{n}")
        P.dma("act", xout[tt * 128:(tt + 1) * 128, :], ot[:], tag=f"wo_st{tt}")

    import os as _os
    if _os.environ.get("SKIP_TAIL") != "1":
        emit_router(P, c, xout, big, G, "l0c_", counts_ap=G("moe_counts"))




STAGE_FUNCS = {}

GROUPS = [["l0_attn"], ["l0_ffn", "l1_attn"],
          ["l1_ffn", "l2_attn", "l2_ffn", "l3_attn", "l3_ffn"]]

_PROG_CACHE = {}


def build_group(names, serial=False):
    key = (tuple(names), serial)
    if key in _PROG_CACHE:
        return _PROG_CACHE[key]
    P = Prog("g_" + "_".join(names), serial=serial)
    xin = P.dram_in("x_in", (S, D), F32)
    xcur = xin
    for i, nm in enumerate(names):
        sf = STAGE_FUNCS[nm]
        g = sf["dram"](P)
        last = (i == len(names) - 1)
        xo = P.dram_out("x_out", (S, D), F32) if last else P.dram_tmp(f"x_mid{i}", (S, D), F32)
        P.begin_scope()
        sf["emit"](P, xcur, xo, lambda k, g=g: g[k])
        P.end_scope()
        P.barrier()
        xcur = xo
    consts = P.const_array() if P._constvals else None
    nc = P.finish()
    _PROG_CACHE[key] = (nc, consts)
    return _PROG_CACHE[key]


def run_groups(groups, xs, inputs, serial=False, trace=False, carry0=None):
    carry = dict(carry0 or {})
    nlaunch = 0
    for names in groups:
        nc, consts = build_group(names, serial=serial)
        preps = {}
        extras = []
        for nm in names:
            sf = STAGE_FUNCS[nm]
            preps.update(sf["prep"](inputs, carry))
            extras += sf["extra"]
        in_maps = []
        for b in range(N_CORES):
            m = {"x_in": xs[b]}
            if consts is not None:
                m["_consts"] = consts
            for k, v in preps.items():
                m[k] = v[b] if isinstance(v, list) else v
            in_maps.append(m)
        res = run_bass_kernel_spmd(nc, in_maps, core_ids=list(range(N_CORES)))
        nlaunch += 1
        xs = [np.asarray(res.results[b]["x_out"]) for b in range(N_CORES)]
        for ex in extras:
            carry[ex] = [np.asarray(res.results[b][ex]) for b in range(N_CORES)]
    return xs, carry


def kernel(**inputs):
    x = inputs["L0_attn_x"]
    xs = [np.ascontiguousarray(x[b]) for b in range(N_CORES)]
    xs, _ = run_groups(GROUPS, xs, inputs)
    return np.stack(xs, axis=0)



def _triu_incl16():
    return np.triu(np.ones((128, 128), np.float32)).astype(BF)


def dram_router(P, pfx):
    g = {}
    g[pfx + "rwT"] = P.dram_in(pfx + "rwT", (D, 8), F32)
    g[pfx + "rbbc"] = P.dram_in(pfx + "rbbc", (128, 8), F32)
    g[pfx + "triu"] = P.dram_in(pfx + "triu", (128, 128), BF16)
    g[pfx + "id16"] = P.dram_in(pfx + "id16", (128, 128), BF16)
    return g


def prep_router(pfx, rw, rb):
    return {
        pfx + "rwT": np.ascontiguousarray(rw.T),
        pfx + "rbbc": np.ascontiguousarray(np.broadcast_to(rb.reshape(1, 8), (128, 8))).astype(np.float32),
        pfx + "triu": _triu_incl16(),
        pfx + "id16": _ident16(),
    }


def emit_router(P, c, xin, big, G, pfx, counts_ap=None, full=None):
    import os as _os
    LVL = float(_os.environ.get("RT_LVL", "9"))
    st = c["stats"]
    ensure_ident(P, c, G, pfx + "id16")
    triu = P.sbuf(pfx + "triu_s", [128, 128], BF16)
    rw32 = P.sbuf(pfx + "rw32", [128, 64], F32)
    rwh = P.sbuf(pfx + "rwh", [128, 64], BF16)
    rwl = P.sbuf(pfx + "rwl", [128, 64], BF16)
    rbbc = P.sbuf(pfx + "rbbc_s", [128, 8], F32)
    ones_c = P.sbuf(pfx + "onc", [128, 1], BF16)
    ones_r = P.sbuf(pfx + "onr", [1, 128], BF16)
    lnt32 = c["sq"][:]
    xlo = P.sbuf(pfx + "xlo", [128, 1024], BF16)
    xloT = P.sbuf(pfx + "xloT", [128, 1024], BF16)
    lgt = P.sbuf(pfx + "lgt", [128, 8], F32)
    msk1 = P.sbuf(pfx + "msk1", [128, 8], F32)
    msk2 = P.sbuf(pfx + "msk2", [128, 8], F32)
    tmp8 = P.sbuf(pfx + "tmp8", [128, 8], F32)
    m16 = P.sbuf(pfx + "m16", [128, 8], BF16)
    carry = P.sbuf(pfx + "carry", [1, 16], F32)
    carh = P.sbuf(pfx + "carh", [1, 8], BF16)
    carl = P.sbuf(pfx + "carl", [1, 8], BF16)
    _init_carhl = True
    cnt = P.sbuf(pfx + "cnt", [1, 8], F32)
    P.dma("sp", triu[:], G(pfx + "triu"))
    P.dma("sp", rw32[:].rearrange("p (ki e) -> p ki e", e=8),
          G(pfx + "rwT").rearrange("(ki p) e -> p ki e", p=128))
    P.dma("sp", rbbc[:], G(pfx + "rbbc"))
    vop(P, lambda e: e.memset(ones_c[:], 1.0), [], [ones_c[:]])
    vop(P, lambda e: e.memset(ones_r[:], 1.0), [], [ones_r[:]])
    vop(P, lambda e: e.memset(carry[:], 0.0), [], [carry[:]])
    vop(P, lambda e: e.memset(carh[:], 0.0), [], [carh[:]])
    vop(P, lambda e: e.memset(carl[:], 0.0), [], [carl[:]])
    vop(P, lambda e: e.tensor_copy(out=rwh[:], in_=rw32[:]), [rw32[:]], [rwh[:]])
    vop(P, lambda e: e.tensor_sub(out=rwl[:], in0=rw32[:], in1=rwh[:]),
        [rw32[:], rwh[:]], [rwl[:]])
    rwhv = rwh[:].rearrange("p (ki e) -> p ki e", e=8)
    rwlv = rwl[:].rearrange("p (ki e) -> p ki e", e=8)

    if full is not None:
        cm = P.sbuf(pfx + "cm", [8, 8], F32)
        cmh = P.sbuf(pfx + "cmh", [8, 8], BF16)
        cml = P.sbuf(pfx + "cml", [8, 8], BF16)
        pm16 = P.sbuf(pfx + "pm16", [8, 1], BF16)
        thr = P.sbuf(pfx + "thr", [1, 16], F32)
        thrh = P.sbuf(pfx + "thrh", [1, 8], BF16)
        thrl = P.sbuf(pfx + "thrl", [1, 8], BF16)
        thbc = P.sbuf(pfx + "thbc", [128, 8], F32)
        gall = P.sbuf(pfx + "gall", [128, 8], F32)
        g1b = P.sbuf(pfx + "g1b", [128, 8], F32)
        gf16 = P.sbuf(pfx + "gf16", [128, 8], BF16)
        keep = P.sbuf(pfx + "keep", [128, 8], F32)
        gfT8 = full["gfT8_buf"] if "gfT8_buf" in full else P.sbuf(pfx + "gfT8", [8, S], BF16)[:]
        P.dma("sp", cm[:], full["cmat"])
        P.dma("pool", pm16[:], full["prefmask"])
        vop(P, lambda e: e.tensor_copy(out=cmh[:], in_=cm[:]), [cm[:]], [cmh[:]])
        vop(P, lambda e: e.tensor_sub(out=cml[:], in0=cm[:], in1=cmh[:]),
            [cm[:], cmh[:]], [cml[:]])
        mm(P, c["pB"][0:1, 0:8], pm16[:], cmh[:], start=True, stop=False, tag="rtr_oh")
        mm(P, c["pB"][0:1, 0:8], pm16[:], cml[:], start=False, stop=True, tag="rtr_ol")
        vop(P, lambda e: e.tensor_scalar(out=thr[:, 0:8], in0=c["pB"][0:1, 0:8],
                                         scalar1=-1.0, scalar2=2048.0,
                                         op0=ALU.mult, op1=ALU.add),
            [c["pB"][0:1, 0:8]], [thr[:, 0:8]], tag="rtr_thr")
        vop(P, lambda e: e.tensor_copy(out=thrh[:], in_=thr[:, 0:8]), [thr[:, 0:8]], [thrh[:]])
        vop(P, lambda e: e.tensor_sub(out=thrl[:], in0=thr[:, 0:8], in1=thrh[:]),
            [thr[:, 0:8], thrh[:]], [thrl[:]])
        mm(P, c["pB"][:, 0:8], ones_r[:], thrh[:], start=True, stop=False, tag="rtr_tbh")
        mm(P, c["pB"][:, 0:8], ones_r[:], thrl[:], start=False, stop=True, tag="rtr_tbl")
        vop(P, lambda e: e.tensor_copy(out=thbc[:], in_=c["pB"][:, 0:8]),
            [c["pB"][:, 0:8]], [thbc[:]], tag="rtr_tbev")

    for tt in range(NT):
        xa = c["xtm"][tt % 2]
        lnt = c["lntile"][tt % 2]
        P.dma("sp", xa[:], xin[tt * 128:(tt + 1) * 128, :], tag=f"rt_ld{tt}")
        sop(P, lambda e, xa=xa: e.activation(out=c["sq"][:], in_=xa[:], func=AF.Square,
                                             accum_out=st[:, 1:2]),
            [xa[:]], [c["sq"][:], st[:, 1:2]])
        vop(P, lambda e, xa=xa: e.reduce_sum(out=st[:, 0:1], in_=xa[:], axis=AX.X),
            [xa[:]], [st[:, 0:1]])
        vop(P, lambda e: e.tensor_scalar(out=st[:, 2:3], in0=st[:, 0:1], scalar1=1.0 / D,
                                         scalar2=None, op0=ALU.mult), [st[:, 0:1]], [st[:, 2:3]])
        vop(P, lambda e: e.tensor_scalar(out=st[:, 3:4], in0=st[:, 1:2], scalar1=1.0 / D,
                                         scalar2=None, op0=ALU.mult), [st[:, 1:2]], [st[:, 3:4]])
        vop(P, lambda e: e.tensor_tensor(out=st[:, 4:5], in0=st[:, 2:3], in1=st[:, 2:3],
                                         op=ALU.mult), [st[:, 2:3]], [st[:, 4:5]])
        vop(P, lambda e: e.tensor_sub(out=st[:, 5:6], in0=st[:, 3:4], in1=st[:, 4:5]),
            [st[:, 3:4], st[:, 4:5]], [st[:, 5:6]])
        epsap = P.constant(EPS_LN)
        sop(P, lambda e, epsap=epsap: e.activation(out=st[:, 6:7], in_=st[:, 5:6],
                                                   func=AF.Sqrt, bias=epsap),
            [st[:, 5:6], epsap], [st[:, 6:7]])
        vop(P, lambda e: e.reciprocal(out=st[:, 7:8], in_=st[:, 6:7]),
            [st[:, 6:7]], [st[:, 7:8]])
        vop(P, lambda e: e.tensor_tensor(out=st[:, 8:9], in0=st[:, 2:3], in1=st[:, 7:8],
                                         op=ALU.mult), [st[:, 2:3], st[:, 7:8]], [st[:, 8:9]])
        vop(P, lambda e: e.tensor_scalar(out=st[:, 9:10], in0=st[:, 8:9], scalar1=-1.0,
                                         scalar2=None, op0=ALU.mult), [st[:, 8:9]], [st[:, 9:10]])
        vop(P, lambda e, xa=xa: e.tensor_scalar(out=lnt32[:], in0=xa[:],
                scalar1=st[:, 7:8], scalar2=st[:, 9:10], op0=ALU.mult, op1=ALU.add),
            [xa[:], st[:, 7:8], st[:, 9:10]], [lnt32[:]], tag="rt_norm")
        vop(P, lambda e, lnt=lnt: e.tensor_copy(out=lnt[:], in_=lnt32[:]),
            [lnt32[:]], [lnt[:]], tag="rt_cast")
        vop(P, lambda e, lnt=lnt: e.tensor_sub(out=xlo[:], in0=lnt32[:], in1=lnt[:]),
            [lnt32[:], lnt[:]], [xlo[:]], tag="rt_lo")
        for k in range(NF):
            pb = c["pT0"] if k % 2 == 0 else c["pT1"]
            slot = (k // 2) * 128
            tr(P, pb[:, slot:slot + 128], lnt[:, k * 128:(k + 1) * 128], c["ident16"][:])
            dst = big[:, k * S + tt * 128:k * S + (tt + 1) * 128]
            vop(P, lambda e, pb=pb, slot=slot, dst=dst: e.tensor_copy(out=dst, in_=pb[:, slot:slot + 128]),
                [pb[:, slot:slot + 128]], [dst])
        for k in range(NF):
            pb = c["pT0"] if k % 2 == 0 else c["pT1"]
            slot = (k // 2) * 128
            tr(P, pb[:, slot:slot + 128], xlo[:, k * 128:(k + 1) * 128], c["ident16"][:],
               tag="rt_lotr")
            dst = xloT[:, k * 128:(k + 1) * 128]
            vop(P, lambda e, pb=pb, slot=slot, dst=dst: e.tensor_copy(out=dst, in_=pb[:, slot:slot + 128]),
                [pb[:, slot:slot + 128]], [dst])
        if LVL < 2:
            continue
        ps8 = c["pB"][:, 0:8]
        for k in range(NF):
            xhiT = big[:, k * S + tt * 128:k * S + (tt + 1) * 128]
            mm(P, ps8, xhiT, rwhv[:, k, :], start=(k == 0), stop=False, tag=f"rt_lgh{tt}_{k}")
            mm(P, ps8, xhiT, rwlv[:, k, :], start=False, stop=False, tag=f"rt_lgl{tt}_{k}")
            mm(P, ps8, xloT[:, k * 128:(k + 1) * 128], rwhv[:, k, :],
               start=False, stop=(k == NF - 1), tag=f"rt_lgx{tt}_{k}")
        vop(P, lambda e: e.tensor_tensor(out=lgt[:], in0=ps8, in1=rbbc[:], op=ALU.add),
            [ps8, rbbc[:]], [lgt[:]], tag="rt_lgev")
        if LVL < 3:
            continue
        vop(P, lambda e: e.reduce_max(out=st[:, 10:11], in_=lgt[:], axis=AX.X),
            [lgt[:]], [st[:, 10:11]])
        vop(P, lambda e: e.tensor_single_scalar(out=msk1[:], in_=lgt[:], scalar=st[:, 10:11],
                                                op=ALU.is_equal), [lgt[:], st[:, 10:11]], [msk1[:]])
        vop(P, lambda e: e.scalar_tensor_tensor(out=tmp8[:], in0=msk1[:], scalar=-1e30,
                                                in1=lgt[:], op0=ALU.mult, op1=ALU.add),
            [msk1[:], lgt[:]], [tmp8[:]])
        vop(P, lambda e: e.reduce_max(out=st[:, 11:12], in_=tmp8[:], axis=AX.X),
            [tmp8[:]], [st[:, 11:12]])
        vop(P, lambda e: e.tensor_single_scalar(out=msk2[:], in_=tmp8[:], scalar=st[:, 11:12],
                                                op=ALU.is_equal), [tmp8[:], st[:, 11:12]], [msk2[:]])
        vop(P, lambda e: e.tensor_add(out=m16[:], in0=msk1[:], in1=msk2[:]),
            [msk1[:], msk2[:]], [m16[:]], tag="rt_m")
        if LVL < 4:
            continue
        mm(P, c["pD"][0:1, 0:8], ones_c[:], m16[:],
           start=(tt == 0), stop=(tt == NT - 1), tag=f"rt_cnt{tt}")
        if full is not None:
            vop(P, lambda e: e.tensor_sub(out=st[:, 12:13], in0=st[:, 10:11], in1=st[:, 11:12]),
                [st[:, 10:11], st[:, 11:12]], [st[:, 12:13]])
            sop(P, lambda e: e.activation(out=st[:, 13:14], in_=st[:, 12:13], func=AF.Sigmoid),
                [st[:, 12:13]], [st[:, 13:14]])
            vop(P, lambda e: e.tensor_scalar(out=st[:, 14:15], in0=st[:, 13:14], scalar1=-1.0,
                                             scalar2=1.0, op0=ALU.mult, op1=ALU.add),
                [st[:, 13:14]], [st[:, 14:15]])
            vop(P, lambda e: e.tensor_single_scalar(out=g1b[:], in_=msk1[:], scalar=st[:, 13:14],
                                                    op=ALU.mult), [msk1[:], st[:, 13:14]], [g1b[:]])
            vop(P, lambda e: e.scalar_tensor_tensor(out=gall[:], in0=msk2[:], scalar=st[:, 14:15],
                                                    in1=g1b[:], op0=ALU.mult, op1=ALU.add),
                [msk2[:], st[:, 14:15], g1b[:]], [gall[:]])
            pcu = c["pA"][:, 0:8]
            mm(P, pcu, triu[:], m16[:], start=True, stop=False, tag=f"rt_cum{tt}")
            mm(P, pcu, ones_r[:], carh[:], start=False, stop=False, tag=f"rt_cah{tt}")
            mm(P, pcu, ones_r[:], carl[:], start=False, stop=True, tag=f"rt_cal{tt}")
            vop(P, lambda e: e.tensor_tensor(out=keep[:], in0=pcu, in1=thbc[:], op=ALU.is_le),
                [pcu, thbc[:]], [keep[:]], tag="rt_keep")
            vop(P, lambda e: e.tensor_mul(out=g1b[:], in0=gall[:], in1=keep[:]),
                [gall[:], keep[:]], [g1b[:]], tag="rt_gf")
            vop(P, lambda e: e.tensor_copy(out=gf16[:], in_=g1b[:]), [g1b[:]], [gf16[:]])
            tr(P, c["pT0"][0:8, 0:128], gf16[:], c["ident16"][:], tag="rt_gtr")
            gslice = gfT8[:, tt * 128:(tt + 1) * 128]
            vop(P, lambda e, gslice=gslice: e.tensor_copy(out=gslice, in_=c["pT0"][0:8, 0:128]),
                [c["pT0"][0:8, 0:128]], [gslice])
            pcs = c["pA"][0:1, 8:16]
            mm(P, pcs, ones_c[:], m16[:], tag=f"rt_cs{tt}")
            vop(P, lambda e: e.tensor_tensor(out=carry[:, 0:8], in0=carry[:, 0:8], in1=pcs,
                                             op=ALU.add),
                [carry[:, 0:8], pcs], [carry[:, 0:8]], tag="rt_carup")
            vop(P, lambda e: e.tensor_copy(out=carh[:], in_=carry[:, 0:8]),
                [carry[:, 0:8]], [carh[:]])
            vop(P, lambda e: e.tensor_sub(out=carl[:], in0=carry[:, 0:8], in1=carh[:]),
                [carry[:, 0:8], carh[:]], [carl[:]])
    if counts_ap is not None and LVL >= 4:
        vop(P, lambda e: e.tensor_copy(out=cnt[:], in_=c["pD"][0:1, 0:8]),
            [c["pD"][0:1, 0:8]], [cnt[:]], tag="rt_cntev")
        P.dma("act", counts_ap, cnt[:], tag="rt_cntst")
    if full is not None:
        P.dma("act", full["ghbm"], gfT8, tag="rt_gfst")



def prep_l0_ffn(inputs, carry):
    d = prep_router("l0f_", inputs["L0_ffn_router_w"], inputs["L0_ffn_router_b"])
    d["l0f_W"] = np.ascontiguousarray(inputs["L0_ffn_W"])
    d["l0f_V"] = np.ascontiguousarray(inputs["L0_ffn_V"])
    d["l0f_Wout"] = np.ascontiguousarray(inputs["L0_ffn_W_out"])
    counts = carry["moe_counts"]
    cmat = np.concatenate([c.reshape(1, 8) for c in counts], axis=0).astype(np.float32)
    d["l0f_cmat"] = cmat
    d["l0f_pref"] = [np.asarray([[1.0]] * b + [[0.0]] * (8 - b), np.float32).reshape(8, 1)
                     for b in range(8)]
    return d


def dram_l0_ffn(P):
    g = dram_router(P, "l0f_")
    g["l0f_W"] = P.dram_in("l0f_W", (8, D, 2048), F32)
    g["l0f_V"] = P.dram_in("l0f_V", (8, D, 2048), F32)
    g["l0f_Wout"] = P.dram_in("l0f_Wout", (8, 2048, D), F32)
    g["l0f_cmat"] = P.dram_in("l0f_cmat", (8, 8), F32)
    g["l0f_pref"] = P.dram_in("l0f_pref", (8, 1), F32)
    g["l0f_ghbm"] = P.dram_tmp("l0f_ghbm", (8, S), BF16)
    g["l0f_yacc"] = P.dram_tmp("l0f_yacc", (S, D), F32)
    return g


def emit_l0_ffn(P, xin, xout, G):
    c = alloc_common(P, notm=1)
    big = P.sbuf("l0f_big", [128, NF * S], BF16)
    wW = P.sbuf("l0f_wW", [128, 16384], BF16)
    wV = P.sbuf("l0f_wV", [128, 16384], BF16)
    wO = P.sbuf("l0f_wO", [128, 16384], BF16)
    emit_router(P, c, xin, big, G, "l0f_",
                full=dict(cmat=G("l0f_cmat"), prefmask=G("l0f_pref"), ghbm=G("l0f_ghbm"),
                          gfT8_buf=wO[0:8, 0:S]))
    wWv = wW[:].rearrange("p (ki o) -> p ki o", o=2048)
    wVv = wV[:].rearrange("p (ki o) -> p ki o", o=2048)
    wOv = wO[:].rearrange("p (mi o) -> p mi o", o=1024)
    hgb = P.sbuf("l0f_hgb", [128, 16 * 512], BF16)
    hgv = hgb[:].rearrange("p (mi t) -> p mi t", t=512)
    grow16 = P.sbuf("l0f_grow16", [1, 512], BF16)
    gbc16 = P.sbuf("l0f_gbc", [128, 512], BF16)
    ag16 = P.sbuf("l0f_ag", [128, 512], BF16)
    sb16 = P.sbuf("l0f_sb", [128, 512], BF16)
    yst = [P.sbuf("l0f_yst0", [128, 512], F32)] * 2
    ones_r16 = P.sbuf("l0f_onr16", [1, 128], BF16)
    vop(P, lambda e: e.memset(ones_r16[:], 1.0), [], [ones_r16[:]])

    yacc = G("l0f_yacc")
    for ex in range(8):
        P.dma("pool", wWv, G("l0f_W")[ex].rearrange("(ki p) o -> p ki o", p=128),
              tag=f"ex{ex}_ldW")
        P.dma("pool", wVv, G("l0f_V")[ex].rearrange("(ki p) o -> p ki o", p=128),
              tag=f"ex{ex}_ldV")
        P.dma("pool", wOv, G("l0f_Wout")[ex].rearrange("(mi p) o -> p mi o", p=128),
              tag=f"ex{ex}_ldO")

        for tb in range(8):
            P.dma("sp", grow16[:], G("l0f_ghbm")[ex:ex + 1, tb * 512:(tb + 1) * 512],
                  tag=f"ex{ex}_ldg{tb}")
            mm(P, c["pD"][:, 0:512], ones_r16[:], grow16[:],
               tag=f"ex{ex}_gbc{tb}")
            vop(P, lambda e: e.tensor_copy(out=gbc16[:], in_=c["pD"][:, 0:512]),
                [c["pD"][:, 0:512]], [gbc16[:]], tag=f"ex{ex}_gbcev{tb}")
            for m in range(16):
                psa = c["pA"][:, (m % 2) * 512:(m % 2) * 512 + 512]
                psb_ = c["pB"][:, (m % 2) * 512:(m % 2) * 512 + 512]
                for ki in range(NF):
                    mm(P, psa, wWv[:, ki, m * 128:(m + 1) * 128],
                       big[:, ki * S + tb * 512:ki * S + (tb + 1) * 512],
                       start=(ki == 0), stop=(ki == NF - 1), tag=f"ex{ex}_a{tb}_{m}")
                for ki in range(NF):
                    mm(P, psb_, wVv[:, ki, m * 128:(m + 1) * 128],
                       big[:, ki * S + tb * 512:ki * S + (tb + 1) * 512],
                       start=(ki == 0), stop=(ki == NF - 1), tag=f"ex{ex}_b{tb}_{m}")
                vop(P, lambda e, psa=psa: e.tensor_tensor(out=ag16[:], in0=psa, in1=gbc16[:],
                                                          op=ALU.mult),
                    [psa, gbc16[:]], [ag16[:]], tag=f"ex{ex}_ag{tb}_{m}")
                sop(P, lambda e, psb_=psb_: e.activation(out=sb16[:], in_=psb_, func=AF.Silu),
                    [psb_], [sb16[:]], tag=f"ex{ex}_si{tb}_{m}")
                dst = hgv[:, m, :]
                vop(P, lambda e, dst=dst: e.tensor_tensor(out=dst, in0=ag16[:], in1=sb16[:],
                                                          op=ALU.mult),
                    [ag16[:], sb16[:]], [dst], tag=f"ex{ex}_hg{tb}_{m}")
            for ts in range(4):
                for n in range(2):
                    ps = c["pC"][:, 0:512] if (ts * 2 + n) % 2 == 0 else c["pD"][:, 0:512]
                    for m in range(16):
                        mm(P, ps, hgv[:, m, ts * 128:(ts + 1) * 128],
                           wOv[:, m, n * 512:(n + 1) * 512],
                           start=(m == 0), stop=(m == 15), tag=f"ex{ex}_o{tb}_{ts}_{n}")
                    stg = yst[(ts * 2 + n) % 2]
                    vop(P, lambda e, ps=ps, stg=stg: e.tensor_copy(out=stg[:], in_=ps),
                        [ps], [stg[:]], tag=f"ex{ex}_oev{tb}_{ts}_{n}")
                    dst = yacc[tb * 512 + ts * 128: tb * 512 + (ts + 1) * 128,
                               n * 512:(n + 1) * 512]
                    if ex == 0:
                        P.dma("pool", dst, stg[:], tag=f"ex{ex}_yst{tb}_{ts}_{n}")
                    else:
                        P.dma("pool", dst, stg[:], accum_op=ALU.add,
                              tag=f"ex{ex}_yacc{tb}_{ts}_{n}")
    for tt in range(NT):
        xa = c["xtm"][tt % 2]
        ya = c["otm"][tt % 2]
        P.dma("sp", xa[:], xin[tt * 128:(tt + 1) * 128, :], tag=f"fin_ldx{tt}")
        P.dma("sp", ya[:], yacc[tt * 128:(tt + 1) * 128, :], tag=f"fin_ldy{tt}")
        vop(P, lambda e, xa=xa, ya=ya: e.tensor_add(out=ya[:], in0=ya[:], in1=xa[:]),
            [xa[:], ya[:]], [ya[:]], tag=f"fin_add{tt}")
        P.dma("act", xout[tt * 128:(tt + 1) * 128, :], ya[:], tag=f"fin_st{tt}")


STAGE_FUNCS["l0_attn"] = dict(prep=lambda inputs, carry: prep_l0_attn(inputs),
                              dram=dram_l0_attn, emit=emit_l0_attn,
                              extra=["moe_counts"])
STAGE_FUNCS["l0_ffn"] = dict(prep=prep_l0_ffn, dram=dram_l0_ffn, emit=emit_l0_ffn,
                             extra=[])



def _strip_mask_1e10():
    u = np.arange(1024)[None, :]
    i = np.arange(128)[:, None]
    return np.where(u <= i + 512, 0.0, -1e10).astype(np.float32)


def prep_l1_attn(inputs):
    b = inputs["L1_attn_b_attn"]
    return {
        "l1a_wqkv": np.ascontiguousarray(inputs["L1_attn_w_attn"]),
        "l1a_wproj": np.ascontiguousarray(inputs["L1_attn_w_proj"]),
        "l1a_bqk": np.ascontiguousarray(
            np.stack([b[0:1024].reshape(8, 128).T, b[1024:2048].reshape(8, 128).T], axis=0)),
        "l1a_bvbc": np.ascontiguousarray(np.broadcast_to(b[2048:3072][None, :], (128, 1024))),
        "l1a_bpbc": np.ascontiguousarray(np.broadcast_to(
            inputs["L1_attn_b_proj"][None, :], (128, 1024))),
        "l1a_strip": _strip_mask_1e10().astype(BF),
        "l1a_id16": _ident16(),
        "l1a_gbc": np.ascontiguousarray(np.broadcast_to(
            inputs["L1_ffn_ln_gamma"][None, :], (128, 1024))),
        "l1a_bbc": np.ascontiguousarray(np.broadcast_to(
            inputs["L1_ffn_ln_beta"][None, :], (128, 1024))),
    }


def dram_l1_attn(P):
    g = {}
    g["l1a_wqkv"] = P.dram_in("l1a_wqkv", (D, 3 * D), F32)
    g["l1a_wproj"] = P.dram_in("l1a_wproj", (D, D), F32)
    g["l1a_bqk"] = P.dram_in("l1a_bqk", (2, 128, 8), F32)
    g["l1a_bvbc"] = P.dram_in("l1a_bvbc", (128, 1024), F32)
    g["l1a_bpbc"] = P.dram_in("l1a_bpbc", (128, 1024), F32)
    g["l1a_strip"] = P.dram_in("l1a_strip", (128, 1024), BF16)
    g["l1a_id16"] = P.dram_in("l1a_id16", (128, 128), BF16)
    g["l1a_gbc"] = P.dram_in("l1a_gbc", (128, 1024), F32)
    g["l1a_bbc"] = P.dram_in("l1a_bbc", (128, 1024), F32)
    g["xn_max"] = P.dram_out("xn_max", (1, 1), F32)
    g["l1a_qTd"] = P.dram_tmp("l1a_qTd", (D, S), BF16)
    g["l1a_kTd"] = P.dram_tmp("l1a_kTd", (D, S), BF16)
    g["l1a_pv"] = P.dram_tmp("l1a_pv", (S, D), F32)
    return g


def emit_l1_attn(P, xin, xout, G):
    c = alloc_common(P)
    ensure_ident(P, c, G, "l1a_id16")
    big = P.sbuf("l1a_big", [128, NF * S], BF16)
    vsb = P.sbuf("l1a_vsb", [128, NT * 1024], BF16)
    w16 = P.sbuf("l1a_w16", [128, 16384], BF16)
    wv_ = w16[:, 0:8192].rearrange("p (ki o) -> p ki o", o=1024)
    bqk = P.sbuf("l1a_bqk_s", [128, 16], F32)
    bvbc = P.sbuf("l1a_bvbc_s", [128, 1024], F32)
    bpbc = P.sbuf("l1a_bpbc_s", [128, 1024], F32)
    strip = P.sbuf("l1a_strip_s", [128, 1024], BF16)
    kblk = P.sbuf("l1a_kblk", [128, 8 * 512], BF16)
    kbv = kblk[:].rearrange("p (ki k) -> p ki k", k=512)
    p16 = P.sbuf("l1a_p16", [128, 512], BF16)
    pm32 = P.sbuf("l1a_pm32", [128, 512], F32)
    pt16 = P.sbuf("l1a_pt16", [128, 512], BF16)
    dall = P.sbuf("l1a_dall", [128, 32], F32)
    att16 = P.sbuf("l1a_att16", [128, 1024], BF16)
    atT16 = P.sbuf("l1a_atT16", [128, 1024], BF16)
    yst = None
    P.dma("sp", bqk[:, 0:8], G("l1a_bqk")[0])
    P.dma("sp", bqk[:, 8:16], G("l1a_bqk")[1])
    P.dma("sp", bvbc[:], G("l1a_bvbc"))
    P.dma("sp", strip[:], G("l1a_strip"))

    emit_ln(P, c, xin, big)

    P.dma("pool", w16[:, 0:8192].rearrange("p (ki o) -> p ki o", o=1024),
          G("l1a_wqkv")[:, 2048:3072].rearrange("(ki p) o -> p ki o", p=128), tag="ld_wv")
    for tt in range(NT):
        for n in range(2):
            ps = c["pA"][:, n * 512:(n + 1) * 512]
            for ki in range(NF):
                mm(P, ps, big[:, ki * S + tt * 128:ki * S + (tt + 1) * 128],
                   wv_[:, ki, n * 512:(n + 1) * 512],
                   start=(ki == 0), stop=(ki == NF - 1), tag=f"v_mm{tt}_{n}")
            dst = vsb[:, tt * 1024 + n * 512: tt * 1024 + (n + 1) * 512]
            vop(P, lambda e, ps=ps, dst=dst, n=n: e.tensor_tensor(
                    out=dst, in0=ps, in1=bvbc[:, n * 512:(n + 1) * 512], op=ALU.add),
                [ps, bvbc[:, n * 512:(n + 1) * 512]], [dst], tag=f"v_ev{tt}_{n}")
    for qk, dnam in enumerate(["l1a_qTd", "l1a_kTd"]):
        P.dma("pool", w16[:, 0:8192].rearrange("p (ki o) -> p ki o", o=1024),
              G("l1a_wqkv")[:, qk * 1024:(qk + 1) * 1024].rearrange("(ki p) o -> p ki o", p=128),
              tag=f"ld_w{qk}")
        wA = w16[:, 0:8192].rearrange("p (ki o) -> p ki o", o=1024)
        dstd = G(dnam)
        for oc in range(NF):
            for tb in range(8):
                r = (oc * 8 + tb) % 2
                ps = c["pB"][:, r * 512:(r + 1) * 512]
                for ki in range(NF):
                    mm(P, ps, wA[:, ki, oc * 128:(oc + 1) * 128],
                       big[:, ki * S + tb * 512:ki * S + (tb + 1) * 512],
                       start=(ki == 0), stop=(ki == NF - 1), tag=f"qk{qk}_mm{oc}_{tb}")
                rp = p16 if r == 0 else pt16
                vop(P, lambda e, ps=ps, rp=rp, qk=qk, oc=oc: e.tensor_scalar(
                        out=rp[:], in0=ps, scalar1=bqk[:, qk * 8 + oc:qk * 8 + oc + 1],
                        scalar2=None, op0=ALU.add),
                    [ps, bqk[:, qk * 8 + oc:qk * 8 + oc + 1]], [rp[:]], tag=f"qk{qk}_ev{oc}_{tb}")
                P.dma("act", dstd[oc * 128:(oc + 1) * 128, tb * 512:(tb + 1) * 512], rp[:],
                      tag=f"qk{qk}_st{oc}_{tb}")
    for ki in range(NF):
        P.dma("sp", big[:, ki * S:(ki + 1) * S], G("l1a_qTd")[ki * 128:(ki + 1) * 128, :],
              tag=f"ldq{ki}")

    pvd = G("l1a_pv")
    vop(P, lambda e: e.memset(dall[:], 0.0), [], [dall[:]])
    sh20 = P.constant(-20.0)
    for kc in range(8):
        P.dma("sp", kbv, G("l1a_kTd")[:, kc * 512:(kc + 1) * 512].rearrange(
            "(ki p) k -> p ki k", p=128), tag=f"fl_ldk{kc}")
        for qt in range(4 * kc, NT):
            j = qt - 4 * kc
            pss = c["pC"][:, 0:512] if qt % 2 == 0 else c["pD"][:, 0:512]
            for ki in range(NF):
                mm(P, pss, big[:, ki * S + qt * 128:ki * S + (qt + 1) * 128],
                   kbv[:, ki, :], start=(ki == 0), stop=(ki == NF - 1),
                   tag=f"fl_s{kc}_{qt}_{ki}")
            if j < 4:
                off = 512 - 128 * j
                vop(P, lambda e, pss=pss, off=off: e.tensor_tensor(
                        out=pm32[:], in0=pss, in1=strip[:, off:off + 512], op=ALU.add),
                    [pss, strip[:, off:off + 512]], [pm32[:]], tag=f"fl_msk{kc}_{qt}")
                esrc = pm32[:]
            else:
                esrc = pss
            sop(P, lambda e, esrc=esrc: e.activation(
                    out=p16[:], in_=esrc, func=AF.Exp, scale=1.0 / 32.0, bias=sh20,
                    accum_out=c["stats"][:, 0:1]),
                [esrc, sh20], [p16[:], c["stats"][:, 0:1]], tag=f"fl_exp{kc}_{qt}")
            vop(P, lambda e, qt=qt: e.tensor_tensor(out=dall[:, qt:qt + 1],
                    in0=dall[:, qt:qt + 1], in1=c["stats"][:, 0:1], op=ALU.add),
                [dall[:, qt:qt + 1], c["stats"][:, 0:1]], [dall[:, qt:qt + 1]],
                tag=f"fl_dacc{kc}_{qt}")
            nsub = min(4, qt + 1 - 4 * kc)
            for jj in range(nsub):
                tr(P, c["pT1"][:, jj * 128:(jj + 1) * 128], p16[:, jj * 128:(jj + 1) * 128],
                   c["ident16"][:], tag=f"fl_tr{kc}_{qt}_{jj}")
            vop(P, lambda e, nsub=nsub: e.tensor_copy(out=pt16[:, 0:nsub * 128],
                                                      in_=c["pT1"][:, 0:nsub * 128]),
                [c["pT1"][:, 0:nsub * 128]], [pt16[:, 0:nsub * 128]], tag=f"fl_ptev{kc}_{qt}")
            for n in range(2):
                ps = c["pA"][:, n * 512:(n + 1) * 512]
                for jj in range(nsub):
                    kt = kc * 4 + jj
                    mm(P, ps, pt16[:, jj * 128:(jj + 1) * 128],
                       vsb[:, kt * 1024 + n * 512: kt * 1024 + (n + 1) * 512],
                       start=(jj == 0), stop=(jj == nsub - 1), tag=f"fl_pv{kc}_{qt}_{jj}_{n}")
                stg = pm32[:]
                vop(P, lambda e, ps=ps, stg=stg: e.tensor_copy(out=stg, in_=ps),
                    [ps], [stg], tag=f"fl_pvev{kc}_{qt}_{n}")
                dst = pvd[qt * 128:(qt + 1) * 128, n * 512:(n + 1) * 512]
                if kc == 0:
                    P.dma("pool", dst, stg, tag=f"fl_pvst{kc}_{qt}_{n}")
                else:
                    P.dma("pool", dst, stg, accum_op=ALU.add, tag=f"fl_pvacc{kc}_{qt}_{n}")

    P.dma("pool", w16[:, 0:8192].rearrange("p (ki o) -> p ki o", o=1024),
          G("l1a_wproj").rearrange("(ki p) o -> p ki o", p=128), tag="ld_wp")
    P.dma("sp", bpbc[:], G("l1a_bpbc"), tag="ld_bp")
    wp = w16[:, 0:8192].rearrange("p (ki o) -> p ki o", o=1024)
    for tt in range(NT):
        xa = c["xtm"][tt % 2]
        ot = c["otm"][tt % 2]
        P.dma("sp", xa[:], pvd[tt * 128:(tt + 1) * 128, :], tag=f"pr_ldpv{tt}")
        vop(P, lambda e, tt=tt: e.reciprocal(out=c["stats"][:, 1:2], in_=dall[:, tt:tt + 1]),
            [dall[:, tt:tt + 1]], [c["stats"][:, 1:2]], tag=f"pr_rd{tt}")
        vop(P, lambda e, xa=xa: e.tensor_scalar(out=att16[:], in0=xa[:],
                scalar1=c["stats"][:, 1:2], scalar2=None, op0=ALU.mult),
            [xa[:], c["stats"][:, 1:2]], [att16[:]], tag=f"pr_div{tt}")
        for k in range(NF):
            pb = c["pT0"] if k % 2 == 0 else c["pT1"]
            slot = (k // 2) * 128
            tr(P, pb[:, slot:slot + 128], att16[:, k * 128:(k + 1) * 128], c["ident16"][:],
               tag=f"pr_tr{tt}_{k}")
            dst = atT16[:, k * 128:(k + 1) * 128]
            vop(P, lambda e, pb=pb, slot=slot, dst=dst: e.tensor_copy(out=dst, in_=pb[:, slot:slot + 128]),
                [pb[:, slot:slot + 128]], [dst], tag=f"pr_trev{tt}_{k}")
        P.dma("sp", xa[:], xin[tt * 128:(tt + 1) * 128, :], tag=f"pr_ldx{tt}")
        for n in range(2):
            ps = c["pB"][:, n * 512:(n + 1) * 512]
            for k in range(NF):
                mm(P, ps, atT16[:, k * 128:(k + 1) * 128], wp[:, k, n * 512:(n + 1) * 512],
                   start=(k == 0), stop=(k == NF - 1), tag=f"pr_mm{tt}_{n}")
            vop(P, lambda e, ps=ps, n=n: e.tensor_tensor(out=pm32[:], in0=ps,
                    in1=bpbc[:, n * 512:(n + 1) * 512], op=ALU.add),
                [ps, bpbc[:, n * 512:(n + 1) * 512]], [pm32[:]], tag=f"pr_b{tt}_{n}")
            vop(P, lambda e, xa=xa, n=n, ot=ot: e.tensor_tensor(
                    out=ot[:, n * 512:(n + 1) * 512], in0=pm32[:],
                    in1=xa[:, n * 512:(n + 1) * 512], op=ALU.add),
                [pm32[:], xa[:, n * 512:(n + 1) * 512]], [ot[:, n * 512:(n + 1) * 512]],
                tag=f"pr_add{tt}_{n}")
        P.dma("act", xout[tt * 128:(tt + 1) * 128, :], ot[:], tag=f"pr_st{tt}")

    st = c["stats"]
    xmx = st[:, 16:17]
    vop(P, lambda e: e.memset(xmx, 0.0), [], [xmx])
    P.dma("sp", bvbc[:], G("l1a_gbc"), tag="tl_ldg")
    P.dma("sp", bpbc[:], G("l1a_bbc"), tag="tl_ldb")
    for tt in range(NT):
        xa = c["xtm"][tt % 2]
        P.dma("sp", xa[:], xout[tt * 128:(tt + 1) * 128, :], tag=f"tl_ld{tt}")
        emit_ln_affine_tile(P, c, xa, bvbc, bpbc, c["sq"][:], tag=f"tl_ln{tt}_")
        vop(P, lambda e: e.tensor_reduce(out=st[:, 17:18], in_=c["sq"][:], axis=AX.X,
                                         op=ALU.max, apply_absolute_value=True),
            [c["sq"][:]], [st[:, 17:18]], tag=f"tl_mx{tt}")
        vop(P, lambda e: e.tensor_tensor(out=xmx, in0=xmx, in1=st[:, 17:18], op=ALU.max),
            [xmx, st[:, 17:18]], [xmx], tag=f"tl_acc{tt}")
    mh = P.sbuf("l1a_mh", [128, 1], BF16)
    ml = P.sbuf("l1a_ml", [128, 1], BF16)
    rh = P.sbuf("l1a_rh", [1, 128], F32)
    rl = P.sbuf("l1a_rl", [1, 128], F32)
    mx11 = P.sbuf("l1a_mx11", [1, 1], F32)
    vop(P, lambda e: e.tensor_copy(out=mh[:], in_=xmx), [xmx], [mh[:]])
    vop(P, lambda e: e.tensor_sub(out=ml[:], in0=xmx, in1=mh[:]), [xmx, mh[:]], [ml[:]])
    tr(P, c["pT0"][0:1, 0:128], mh[:], c["ident16"][:], tag="tl_trh")
    vop(P, lambda e: e.tensor_copy(out=rh[:], in_=c["pT0"][0:1, 0:128]),
        [c["pT0"][0:1, 0:128]], [rh[:]])
    tr(P, c["pT1"][0:1, 0:128], ml[:], c["ident16"][:], tag="tl_trl")
    vop(P, lambda e: e.tensor_copy(out=rl[:], in_=c["pT1"][0:1, 0:128]),
        [c["pT1"][0:1, 0:128]], [rl[:]])
    vop(P, lambda e: e.tensor_add(out=rh[:], in0=rh[:], in1=rl[:]), [rh[:], rl[:]], [rh[:]])
    vop(P, lambda e: e.reduce_max(out=mx11[:], in_=rh[:], axis=AX.X), [rh[:]], [mx11[:]])
    P.dma("act", G("xn_max"), mx11[:], tag="tl_st")


STAGE_FUNCS["l1_attn"] = dict(prep=lambda inputs, carry: prep_l1_attn(inputs),
                              dram=dram_l1_attn, emit=emit_l1_attn, extra=["xn_max"])



LN_EPS_L1 = 1e-3
QB = 128.0
EPS_Q = 1e-5


def emit_ln_affine_tile(P, c, xa, gbc, bbc, out32, tag=""):
    st = c["stats"]
    sop(P, lambda e: e.activation(out=c["sq"][:], in_=xa[:], func=AF.Square,
                                  accum_out=st[:, 1:2]), [xa[:]], [c["sq"][:], st[:, 1:2]],
        tag=tag + "sq")
    vop(P, lambda e: e.reduce_sum(out=st[:, 0:1], in_=xa[:], axis=AX.X),
        [xa[:]], [st[:, 0:1]], tag=tag + "sum")
    vop(P, lambda e: e.tensor_scalar(out=st[:, 2:3], in0=st[:, 0:1], scalar1=1.0 / D,
                                     scalar2=None, op0=ALU.mult), [st[:, 0:1]], [st[:, 2:3]])
    vop(P, lambda e: e.tensor_scalar(out=st[:, 3:4], in0=st[:, 1:2], scalar1=1.0 / D,
                                     scalar2=None, op0=ALU.mult), [st[:, 1:2]], [st[:, 3:4]])
    vop(P, lambda e: e.tensor_tensor(out=st[:, 4:5], in0=st[:, 2:3], in1=st[:, 2:3],
                                     op=ALU.mult), [st[:, 2:3]], [st[:, 4:5]])
    vop(P, lambda e: e.tensor_sub(out=st[:, 5:6], in0=st[:, 3:4], in1=st[:, 4:5]),
        [st[:, 3:4], st[:, 4:5]], [st[:, 5:6]])
    epsap = P.constant(LN_EPS_L1)
    sop(P, lambda e, epsap=epsap: e.activation(out=st[:, 6:7], in_=st[:, 5:6],
                                               func=AF.Sqrt, bias=epsap),
        [st[:, 5:6], epsap], [st[:, 6:7]])
    vop(P, lambda e: e.reciprocal(out=st[:, 7:8], in_=st[:, 6:7]), [st[:, 6:7]], [st[:, 7:8]])
    vop(P, lambda e: e.tensor_tensor(out=st[:, 8:9], in0=st[:, 2:3], in1=st[:, 7:8],
                                     op=ALU.mult), [st[:, 2:3], st[:, 7:8]], [st[:, 8:9]])
    vop(P, lambda e: e.tensor_scalar(out=st[:, 9:10], in0=st[:, 8:9], scalar1=-1.0,
                                     scalar2=None, op0=ALU.mult), [st[:, 8:9]], [st[:, 9:10]])
    vop(P, lambda e: e.tensor_scalar(out=out32, in0=xa[:],
            scalar1=st[:, 7:8], scalar2=st[:, 9:10], op0=ALU.mult, op1=ALU.add),
        [xa[:], st[:, 7:8], st[:, 9:10]], [out32], tag=tag + "n")
    vop(P, lambda e: e.tensor_mul(out=out32, in0=out32, in1=gbc[:]),
        [out32, gbc[:]], [out32], tag=tag + "g")
    vop(P, lambda e: e.tensor_add(out=out32, in0=out32, in1=bbc[:]),
        [out32, bbc[:]], [out32], tag=tag + "b")


def prep_l1_ffn(inputs, carry):
    xm = carry["xn_max"]
    return {
        "l1f_w": np.ascontiguousarray(inputs["L1_ffn_weight"]),
        "l1f_gbc": np.ascontiguousarray(np.broadcast_to(
            inputs["L1_ffn_ln_gamma"][None, :], (128, 1024))),
        "l1f_bbc": np.ascontiguousarray(np.broadcast_to(
            inputs["L1_ffn_ln_beta"][None, :], (128, 1024))),
        "l1f_xm8": np.concatenate([np.asarray(x).reshape(1, 1) for x in xm],
                                  axis=1).astype(np.float32),
        "l1f_id16": _ident16(),
    }


def dram_l1_ffn(P):
    g = {}
    g["l1f_w"] = P.dram_in("l1f_w", (D, D), F32)
    g["l1f_gbc"] = P.dram_in("l1f_gbc", (128, 1024), F32)
    g["l1f_bbc"] = P.dram_in("l1f_bbc", (128, 1024), F32)
    g["l1f_xm8"] = P.dram_in("l1f_xm8", (1, 8), F32)
    g["l1f_id16"] = P.dram_in("l1f_id16", (128, 128), BF16)
    return g


def _bcast_scalar(P, c, src11, dst, onesr16, hi, lo, tag):
    vop(P, lambda e: e.tensor_copy(out=hi, in_=src11), [src11], [hi], tag=tag + "h")
    vop(P, lambda e: e.tensor_sub(out=lo, in0=src11, in1=hi), [src11, hi], [lo], tag=tag + "l")
    mm(P, c["pD"][:, 0:1], onesr16, hi, start=True, stop=False, tag=tag + "mh")
    mm(P, c["pD"][:, 0:1], onesr16, lo, start=False, stop=True, tag=tag + "ml")
    vop(P, lambda e: e.tensor_copy(out=dst, in_=c["pD"][:, 0:1]),
        [c["pD"][:, 0:1]], [dst], tag=tag + "ev")


def emit_l1_ffn(P, xin, xout, G):
    c = alloc_common(P)
    ensure_ident(P, c, G, "l1f_id16")
    wf = P.sbuf("l1f_wf", [128, 8192], F32)
    wq16 = P.sbuf("l1f_wq", [128, 8192], BF16)
    gbc = P.sbuf("l1f_gbc_s", [128, 1024], F32)
    bbc = P.sbuf("l1f_bbc_s", [128, 1024], F32)
    xm8 = P.sbuf("l1f_xm8_s", [1, 8], F32)
    xn32 = P.sbuf("l1f_xn", [128, 1024], F32)
    xq16 = P.sbuf("l1f_xq", [128, 1024], BF16)
    xqT = P.sbuf("l1f_xqT", [128, 1024], BF16)
    wi = P.sbuf("l1f_wi", [128, 1024], I32)
    wsum = P.sbuf("l1f_wsum", [128, 4], F32)
    onesr16 = P.sbuf("l1f_onr", [1, 128], BF16)
    sc11 = P.sbuf("l1f_sc", [1, 8], F32)
    hi11 = P.sbuf("l1f_hi", [1, 1], BF16)
    lo11 = P.sbuf("l1f_lo", [1, 1], BF16)
    hi128 = P.sbuf("l1f_h128", [128, 4], F32)
    mh16 = P.sbuf("l1f_mh", [128, 1], BF16)
    ml16 = P.sbuf("l1f_ml", [128, 1], BF16)
    P.dma("sp", gbc[:], G("l1f_gbc"))
    P.dma("sp", bbc[:], G("l1f_bbc"))
    P.dma("sp", xm8[:], G("l1f_xm8"))
    P.dma("sp", wf[:].rearrange("p (ki o) -> p ki o", o=1024),
          G("l1f_w").rearrange("(ki p) o -> p ki o", p=128), tag="ld_w")
    vop(P, lambda e: e.memset(onesr16[:], 1.0), [], [onesr16[:]])
    wfv = wf[:].rearrange("p (ki o) -> p ki o", o=1024)
    wqv = wq16[:].rearrange("p (ki o) -> p ki o", o=1024)

    for ki in range(NF):
        if ki == 0:
            vop(P, lambda e: e.tensor_reduce(out=wsum[:, 0:1], in_=wfv[:, 0, :], axis=AX.X,
                                             op=ALU.add, apply_absolute_value=True),
                [wfv[:, 0, :]], [wsum[:, 0:1]], tag="bsum0")
        else:
            vop(P, lambda e, ki=ki: e.tensor_reduce(out=wsum[:, 1:2], in_=wfv[:, ki, :],
                                                    axis=AX.X, op=ALU.add,
                                                    apply_absolute_value=True),
                [wfv[:, ki, :]], [wsum[:, 1:2]], tag=f"bsum{ki}")
            vop(P, lambda e: e.tensor_add(out=wsum[:, 0:1], in0=wsum[:, 0:1], in1=wsum[:, 1:2]),
                [wsum[:, 0:1], wsum[:, 1:2]], [wsum[:, 0:1]], tag=f"bacc{ki}")
    vop(P, lambda e: e.tensor_copy(out=mh16[:], in_=wsum[:, 0:1]), [wsum[:, 0:1]], [mh16[:]])
    vop(P, lambda e: e.tensor_sub(out=ml16[:], in0=wsum[:, 0:1], in1=mh16[:]),
        [wsum[:, 0:1], mh16[:]], [ml16[:]])
    onc16 = P.sbuf("l1f_onc", [128, 1], BF16)
    vop(P, lambda e: e.memset(onc16[:], 1.0), [], [onc16[:]])
    mm(P, c["pD"][0:1, 0:1], onc16[:], mh16[:], start=True, stop=False, tag="bsum_h")
    mm(P, c["pD"][0:1, 0:1], onc16[:], ml16[:], start=False, stop=True, tag="bsum_l")
    vop(P, lambda e: e.tensor_scalar(out=sc11[:, 0:1], in0=c["pD"][0:1, 0:1],
                                     scalar1=1.0 / (D * D), scalar2=None, op0=ALU.mult),
        [c["pD"][0:1, 0:1]], [sc11[:, 0:1]], tag="beta")
    vop(P, lambda e: e.tensor_scalar(out=sc11[:, 1:2], in0=sc11[:, 0:1], scalar1=EPS_Q,
                                     scalar2=None, op0=ALU.add), [sc11[:, 0:1]], [sc11[:, 1:2]])
    vop(P, lambda e: e.reciprocal(out=sc11[:, 2:3], in_=sc11[:, 1:2]),
        [sc11[:, 1:2]], [sc11[:, 2:3]])
    _bcast_scalar(P, c, sc11[:, 2:3], hi128[:, 0:1], onesr16[:], hi11[:], lo11[:], "rb")
    for ki in range(NF):
        vop(P, lambda e, ki=ki: e.tensor_scalar(out=c["sq"][:], in0=wfv[:, ki, :],
                scalar1=hi128[:, 0:1], scalar2=None, op0=ALU.mult),
            [wfv[:, ki, :], hi128[:, 0:1]], [c["sq"][:]], tag=f"wr{ki}")
        vop(P, lambda e: e.tensor_copy(out=wi[:], in_=c["sq"][:]), [c["sq"][:]], [wi[:]],
            tag=f"wi{ki}")
        vop(P, lambda e: e.tensor_copy(out=c["sq"][:], in_=wi[:]), [wi[:]], [c["sq"][:]],
            tag=f"wf{ki}")
        dst = wqv[:, ki, :]
        vop(P, lambda e, dst=dst: e.tensor_scalar(out=dst, in0=c["sq"][:],
                scalar1=-1.0, scalar2=1.0, op0=ALU.max, op1=ALU.min),
            [c["sq"][:]], [dst], tag=f"wc{ki}")
    vop(P, lambda e: e.reduce_max(out=sc11[:, 3:4], in_=xm8[:], axis=AX.X),
        [xm8[:]], [sc11[:, 3:4]], tag="gam")
    vop(P, lambda e: e.reciprocal(out=sc11[:, 4:5], in_=sc11[:, 3:4]),
        [sc11[:, 3:4]], [sc11[:, 4:5]])
    vop(P, lambda e: e.tensor_scalar(out=sc11[:, 5:6], in0=sc11[:, 4:5], scalar1=QB,
                                     scalar2=None, op0=ALU.mult), [sc11[:, 4:5]], [sc11[:, 5:6]])
    _bcast_scalar(P, c, sc11[:, 5:6], hi128[:, 1:2], onesr16[:], hi11[:], lo11[:], "ss")
    vop(P, lambda e: e.tensor_tensor(out=sc11[:, 6:7], in0=sc11[:, 3:4], in1=sc11[:, 0:1],
                                     op=ALU.mult), [sc11[:, 3:4], sc11[:, 0:1]], [sc11[:, 6:7]])
    vop(P, lambda e: e.tensor_scalar(out=sc11[:, 7:8], in0=sc11[:, 6:7], scalar1=1.0 / QB,
                                     scalar2=None, op0=ALU.mult), [sc11[:, 6:7]], [sc11[:, 7:8]])
    _bcast_scalar(P, c, sc11[:, 7:8], hi128[:, 2:3], onesr16[:], hi11[:], lo11[:], "s2")

    for tt in range(NT):
        xa = c["xtm"][tt % 2]
        ot = c["otm"][0]
        P.dma("sp", xa[:], xin[tt * 128:(tt + 1) * 128, :], tag=f"q_ldx{tt}")
        emit_ln_affine_tile(P, c, xa, gbc, bbc, xn32[:], tag=f"q_ln{tt}_")
        vop(P, lambda e: e.tensor_scalar(out=xn32[:], in0=xn32[:],
                scalar1=hi128[:, 1:2], scalar2=None, op0=ALU.mult),
            [xn32[:], hi128[:, 1:2]], [xn32[:]], tag=f"q_s{tt}")
        vop(P, lambda e: e.tensor_scalar(out=xq16[:], in0=xn32[:],
                scalar1=-(QB - EPS_Q), scalar2=QB - EPS_Q, op0=ALU.max, op1=ALU.min),
            [xn32[:]], [xq16[:]], tag=f"q_c{tt}")
        for k in range(NF):
            pb = c["pT0"] if k % 2 == 0 else c["pT1"]
            slot = (k // 2) * 128
            tr(P, pb[:, slot:slot + 128], xq16[:, k * 128:(k + 1) * 128], c["ident16"][:],
               tag=f"q_tr{tt}_{k}")
            dst = xqT[:, k * 128:(k + 1) * 128]
            vop(P, lambda e, pb=pb, slot=slot, dst=dst: e.tensor_copy(out=dst, in_=pb[:, slot:slot + 128]),
                [pb[:, slot:slot + 128]], [dst])
        for n in range(2):
            ps = c["pB"][:, n * 512:(n + 1) * 512]
            for k in range(NF):
                mm(P, ps, xqT[:, k * 128:(k + 1) * 128], wqv[:, k, n * 512:(n + 1) * 512],
                   start=(k == 0), stop=(k == NF - 1), tag=f"q_mm{tt}_{n}")
            vop(P, lambda e, ps=ps, n=n: e.tensor_scalar(out=xn32[:, n * 512:(n + 1) * 512],
                    in0=ps, scalar1=hi128[:, 2:3], scalar2=None, op0=ALU.mult),
                [ps, hi128[:, 2:3]], [xn32[:, n * 512:(n + 1) * 512]], tag=f"q_sc{tt}_{n}")
            vop(P, lambda e, xa=xa, n=n, ot=ot: e.tensor_tensor(
                    out=ot[:, n * 512:(n + 1) * 512], in0=xn32[:, n * 512:(n + 1) * 512],
                    in1=xa[:, n * 512:(n + 1) * 512], op=ALU.add),
                [xn32[:, n * 512:(n + 1) * 512], xa[:, n * 512:(n + 1) * 512]],
                [ot[:, n * 512:(n + 1) * 512]], tag=f"q_add{tt}_{n}")
        P.dma("act", xout[tt * 128:(tt + 1) * 128, :], ot[:], tag=f"q_st{tt}")


STAGE_FUNCS["l1_ffn"] = dict(prep=prep_l1_ffn, dram=dram_l1_ffn, emit=emit_l1_ffn, extra=[])



EPS_PERF = 1e-6


def prep_l2_attn(inputs):
    wf = np.asarray(inputs["L2_attn_W_feat"], np.float32)
    waug = np.concatenate([wf, -np.ones((2, 128), np.float32)], axis=0)
    return {
        "l2a_wq": np.ascontiguousarray(inputs["L2_attn_Wq"]),
        "l2a_wk": np.ascontiguousarray(inputs["L2_attn_Wk"]),
        "l2a_wv": np.ascontiguousarray(inputs["L2_attn_Wv"]),
        "l2a_wo": np.ascontiguousarray(inputs["L2_attn_Wo"]),
        "l2a_waug": waug,
        "l2a_id16": _ident16(),
        "l2a_triu": _triu_incl16(),
    }


def dram_l2_attn(P):
    g = {}
    for nm in ("l2a_wq", "l2a_wk", "l2a_wv", "l2a_wo"):
        g[nm] = P.dram_in(nm, (D, D), F32)
    g["l2a_waug"] = P.dram_in("l2a_waug", (66, 128), F32)
    g["l2a_id16"] = P.dram_in("l2a_id16", (128, 128), BF16)
    g["l2a_triu"] = P.dram_in("l2a_triu", (128, 128), BF16)
    g["l2a_qTd"] = P.dram_tmp("l2a_qTd", (16, 64, S), BF16)
    g["l2a_kTd"] = P.dram_tmp("l2a_kTd", (16, 64, S), BF16)
    return g


def emit_l2_attn(P, xin, xout, G):
    c = alloc_common(P)
    ensure_ident(P, c, G, "l2a_id16")
    big = P.sbuf("l2a_big", [128, NF * S], BF16)
    vsb = P.sbuf("l2a_vsb", [128, NT * 1024], BF16)
    w16 = P.sbuf("l2a_w16", [128, 8192], BF16)
    wv_ = w16[:].rearrange("p (ki o) -> p ki o", o=1024)
    triu = P.sbuf("l2a_triu_s", [128, 128], BF16)
    waug16 = P.sbuf("l2a_waug_s", [66, 128], BF16)
    qbuf = P.sbuf("l2a_qbuf", [66, S], BF16)
    kbuf = P.sbuf("l2a_kbuf", [66, S], BF16)
    qp16 = P.sbuf("l2a_qp", [128, S], BF16)
    kp16 = P.sbuf("l2a_kp", [128, S], BF16)
    sq32 = P.sbuf("l2a_sq32", [64, 512], F32)
    q2h = P.sbuf("l2a_q2h", [64, 512], BF16)[:]
    q2l = P.sbuf("l2a_q2l", [64, 512], BF16)[:]
    sqh = w16[0:1, 4096:8192]
    sql = w16[0:1, 0:4096]
    ones64 = P.sbuf("l2a_on64", [64, 1], BF16)
    s16 = P.sbuf("l2a_s16", [128, 65], BF16)
    sf32 = P.sbuf("l2a_sf32", [128, 65], F32)
    vaug = P.sbuf("l2a_vaug", [128, 65], BF16)
    at16 = P.sbuf("l2a_at16", [128, 128], BF16)
    kpt16 = P.sbuf("l2a_kpt", [128, 128], BF16)
    y16 = P.sbuf("l2a_y16", [128, 64], BF16)
    den32 = P.sbuf("l2a_den", [128, 4], F32)
    atT16 = P.sbuf("l2a_atT", [128, 1024], BF16)
    P.dma("sp", triu[:], G("l2a_triu"))
    P.dma("pool", waug16[:], G("l2a_waug"), tag="ld_waug")
    vop(P, lambda e: e.memset(ones64[:], 1.0), [], [ones64[:]])
    vop(P, lambda e: e.memset(vaug[:, 64:65], 1.0), [], [vaug[:, 64:65]])

    emit_ln(P, c, xin, big)

    P.dma("pool", wv_, G("l2a_wv").rearrange("(ki p) o -> p ki o", p=128), tag="ld_wv")
    for tt in range(NT):
        for n in range(2):
            ps = c["pA"][:, n * 512:(n + 1) * 512]
            for ki in range(NF):
                mm(P, ps, big[:, ki * S + tt * 128:ki * S + (tt + 1) * 128],
                   wv_[:, ki, n * 512:(n + 1) * 512],
                   start=(ki == 0), stop=(ki == NF - 1), tag=f"v_mm{tt}_{n}")
            dst = vsb[:, tt * 1024 + n * 512: tt * 1024 + (n + 1) * 512]
            vop(P, lambda e, ps=ps, dst=dst: e.tensor_copy(out=dst, in_=ps),
                [ps], [dst], tag=f"v_ev{tt}_{n}")
    for qk, (wname, dnam) in enumerate([("l2a_wq", "l2a_qTd"), ("l2a_wk", "l2a_kTd")]):
        P.dma("pool", wv_, G(wname).rearrange("(ki p) o -> p ki o", p=128), tag=f"ld_w{qk}")
        dstd = G(dnam)
        for oc in range(NF):
            for tb in range(8):
                r = (oc * 8 + tb) % 2
                ps = c["pB"][:, r * 512:(r + 1) * 512]
                for ki in range(NF):
                    mm(P, ps, wv_[:, ki, oc * 128:(oc + 1) * 128],
                       big[:, ki * S + tb * 512:ki * S + (tb + 1) * 512],
                       start=(ki == 0), stop=(ki == NF - 1), tag=f"qk{qk}_mm{oc}_{tb}")
                rp = at16 if r == 0 else kpt16
                stg = qp16[:, 0:512] if r == 0 else qp16[:, 512:1024]
                vop(P, lambda e, ps=ps, stg=stg: e.tensor_copy(out=stg, in_=ps),
                    [ps], [stg], tag=f"qk{qk}_ev{oc}_{tb}")
                P.dma("act", dstd[2 * oc, :, tb * 512:(tb + 1) * 512], stg[0:64, :],
                      tag=f"qk{qk}_st0_{oc}_{tb}")
                P.dma("act", dstd[2 * oc + 1, :, tb * 512:(tb + 1) * 512], stg[64:128, :],
                      tag=f"qk{qk}_st1_{oc}_{tb}")

    lnc = P.constant(float(-np.log(np.sqrt(128.0))))
    epsc = P.constant(EPS_PERF)
    for h in range(16):
        P.dma("sp", qbuf[0:64, :], G("l2a_qTd")[h], tag=f"hd_ldq{h}")
        P.dma("sp", kbuf[0:64, :], G("l2a_kTd")[h], tag=f"hd_ldk{h}")
        for which, buf, phi in ((0, qbuf, qp16), (1, kbuf, kp16)):
            for bl in range(8):
                sl = slice(bl * 512, (bl + 1) * 512)
                sop(P, lambda e, buf=buf, sl=sl: e.activation(out=sq32[:], in_=buf[0:64, sl],
                                                              func=AF.Square),
                    [buf[0:64, sl]], [sq32[:]], tag=f"ph_sq{h}_{which}_{bl}")
                vop(P, lambda e: e.tensor_copy(out=q2h, in_=sq32[:]), [sq32[:]], [q2h])
                vop(P, lambda e: e.tensor_sub(out=q2l, in0=sq32[:], in1=q2h),
                    [sq32[:], q2h], [q2l])
                mm(P, c["pC"][0:1, 0:512], ones64[:], q2h, start=True, stop=False,
                   tag=f"ph_mh{h}_{which}_{bl}")
                mm(P, c["pC"][0:1, 0:512], ones64[:], q2l, start=False, stop=True,
                   tag=f"ph_ml{h}_{which}_{bl}")
                vop(P, lambda e, sl=sl: e.tensor_scalar(out=sqh[:, sl], in0=c["pC"][0:1, 0:512],
                        scalar1=0.5, scalar2=None, op0=ALU.mult),
                    [c["pC"][0:1, 0:512]], [sqh[:, sl]], tag=f"ph_hh{h}_{which}_{bl}")
                vop(P, lambda e, sl=sl: e.scalar_tensor_tensor(out=sql[:, sl],
                        in0=c["pC"][0:1, 0:512], scalar=0.5, in1=sqh[:, sl],
                        op0=ALU.mult, op1=ALU.subtract),
                    [c["pC"][0:1, 0:512], sqh[:, sl]], [sql[:, sl]], tag=f"ph_ll{h}_{which}_{bl}")
            P.dma("pool", buf[64:65, :], sqh, tag=f"ph_sth{h}_{which}")
            P.dma("pool", buf[65:66, :], sql, tag=f"ph_stl{h}_{which}")
            for bl in range(8):
                sl = slice(bl * 512, (bl + 1) * 512)
                ps = c["pB"][:, (bl % 2) * 512:(bl % 2) * 512 + 512]
                mm(P, ps, waug16[:], buf[:, sl], tag=f"ph_mm{h}_{which}_{bl}")
                sop(P, lambda e, ps=ps, phi=phi, sl=sl, lnc=lnc: e.activation(
                        out=phi[:, sl], in_=ps, func=AF.Exp, bias=lnc),
                    [ps, lnc], [phi[:, sl]], tag=f"ph_exp{h}_{which}_{bl}")
        for ct in range(NT):
            csl = slice(ct * 128, (ct + 1) * 128)
            if ct == 0:
                vop(P, lambda e: e.memset(s16[:], 0.0), [], [s16[:]], tag=f"sc_z{h}")
                vop(P, lambda e: e.memset(sf32[:], 0.0), [], [sf32[:]], tag=f"sc_zf{h}")
            else:
                vop(P, lambda e: e.tensor_copy(out=s16[:], in_=sf32[:]),
                    [sf32[:]], [s16[:]], tag=f"sc_s{h}_{ct}")
            vsrc = vsb[:, ct * 1024 + h * 64: ct * 1024 + h * 64 + 64]
            vop(P, lambda e, vsrc=vsrc: e.tensor_copy(out=vaug[:, 0:64], in_=vsrc),
                [vsrc], [vaug[:, 0:64]], tag=f"sc_v{h}_{ct}")
            mm(P, c["pA"][:, 0:65], qp16[:, csl], s16[:], start=True, stop=False,
               tag=f"sc_n1{h}_{ct}")
            mm(P, c["pA"][:, 512:640], kp16[:, csl], qp16[:, csl], tag=f"sc_at{h}_{ct}")
            vop(P, lambda e: e.tensor_tensor(out=at16[:], in0=c["pA"][:, 512:640],
                    in1=triu[:], op=ALU.mult),
                [c["pA"][:, 512:640], triu[:]], [at16[:]], tag=f"sc_m{h}_{ct}")
            mm(P, c["pA"][:, 0:65], at16[:], vaug[:], start=False, stop=True,
               tag=f"sc_n2{h}_{ct}")
            vop(P, lambda e, epsc=epsc: e.tensor_scalar(out=den32[:, 0:1],
                    in0=c["pA"][:, 64:65], scalar1=epsc, scalar2=None, op0=ALU.add),
                [c["pA"][:, 64:65], epsc], [den32[:, 0:1]], tag=f"sc_d{h}_{ct}")
            vop(P, lambda e: e.reciprocal(out=den32[:, 1:2], in_=den32[:, 0:1]),
                [den32[:, 0:1]], [den32[:, 1:2]], tag=f"sc_r{h}_{ct}")
            vop(P, lambda e: e.tensor_scalar(out=y16[:], in0=c["pA"][:, 0:64],
                    scalar1=den32[:, 1:2], scalar2=None, op0=ALU.mult),
                [c["pA"][:, 0:64], den32[:, 1:2]], [y16[:]], tag=f"sc_y{h}_{ct}")
            dst = big[:, ct * 1024 + h * 64: ct * 1024 + h * 64 + 64]
            vop(P, lambda e, dst=dst: e.tensor_copy(out=dst, in_=y16[:]),
                [y16[:]], [dst], tag=f"sc_yw{h}_{ct}")
            import os as _os2
            if _os2.environ.get("L2A_DBG") == "1" and h == 15 and ct == 0:
                dat = P.dram_out("dbg_at", (128, 128), BF16)
                dva = P.dram_out("dbg_va", (128, 65), BF16)
                dvs = P.dram_out("dbg_vs", (128, 64), BF16)
                dnm = P.dram_out("dbg_nm", (128, 65), BF16)
                dnm16 = P.sbuf("l2a_dnm16", [128, 65], BF16)
                dva16 = P.sbuf("l2a_dva16", [128, 65], BF16)
                dvs16 = P.sbuf("l2a_dvs16", [128, 64], BF16)
                dat16 = P.sbuf("l2a_dat16", [128, 128], BF16)
                vop(P, lambda e: e.tensor_copy(out=dat16[:], in_=at16[:]),
                    [at16[:]], [dat16[:]], tag="dbg_atev")
                vop(P, lambda e: e.tensor_copy(out=dva16[:], in_=vaug[:]),
                    [vaug[:]], [dva16[:]], tag="dbg_vaev")
                vop(P, lambda e: e.tensor_copy(out=dvs16[:], in_=vsb[:, 960:1024]),
                    [vsb[:, 960:1024]], [dvs16[:]], tag="dbg_vsev")
                vop(P, lambda e: e.tensor_copy(out=dnm16[:], in_=c["pA"][:, 0:65]),
                    [c["pA"][:, 0:65]], [dnm16[:]], tag="dbg_nmev")
                P.dma("act", dat, dat16[:])
                P.dma("act", dva, dva16[:])
                P.dma("act", dvs, dvs16[:])
                P.dma("act", dnm, dnm16[:])
            tr(P, c["pT0"][:, 0:128], kp16[:, csl], c["ident16"][:], tag=f"sc_tr{h}_{ct}")
            vop(P, lambda e: e.tensor_copy(out=kpt16[:], in_=c["pT0"][:, 0:128]),
                [c["pT0"][:, 0:128]], [kpt16[:]], tag=f"sc_te{h}_{ct}")
            mm(P, c["pC"][:, 0:65], kpt16[:], vaug[:], tag=f"sc_su{h}_{ct}")
            vop(P, lambda e: e.tensor_tensor(out=sf32[:], in0=sf32[:], in1=c["pC"][:, 0:65],
                                             op=ALU.add),
                [sf32[:], c["pC"][:, 0:65]], [sf32[:]], tag=f"sc_sa{h}_{ct}")

    import os as _os
    if _os.environ.get("L2A_DBG") == "1":
        dq = P.dram_out("dbg_qp", (128, S), BF16)
        dk = P.dram_out("dbg_kp", (128, S), BF16)
        dy = P.dram_out("dbg_y", (128, 1024), BF16)
        dqb = P.dram_out("dbg_qbuf", (66, S), BF16)
        P.dma("act", dq, qp16[:])
        P.dma("act", dk, kp16[:])
        P.dma("act", dy, big[:, 0:1024])
        P.dma("act", dqb, qbuf[:])

    P.dma("pool", wv_, G("l2a_wo").rearrange("(ki p) o -> p ki o", p=128), tag="ld_wo")
    for tt in range(NT):
        xa = c["xtm"][tt % 2]
        ot = c["otm"][tt % 2]
        for k in range(NF):
            pb = c["pT0"] if k % 2 == 0 else c["pT1"]
            slot = (k // 2) * 128
            tr(P, pb[:, slot:slot + 128], big[:, tt * 1024 + k * 128: tt * 1024 + (k + 1) * 128],
               c["ident16"][:], tag=f"wo_tr{tt}_{k}")
            dst = atT16[:, k * 128:(k + 1) * 128]
            vop(P, lambda e, pb=pb, slot=slot, dst=dst: e.tensor_copy(out=dst, in_=pb[:, slot:slot + 128]),
                [pb[:, slot:slot + 128]], [dst])
        P.dma("sp", xa[:], xin[tt * 128:(tt + 1) * 128, :], tag=f"wo_ldx{tt}")
        for n in range(2):
            ps = c["pB"][:, n * 512:(n + 1) * 512]
            for k in range(NF):
                mm(P, ps, atT16[:, k * 128:(k + 1) * 128], wv_[:, k, n * 512:(n + 1) * 512],
                   start=(k == 0), stop=(k == NF - 1), tag=f"wo_mm{tt}_{n}")
            vop(P, lambda e, ps=ps, xa=xa, n=n, ot=ot: e.tensor_tensor(
                    out=ot[:, n * 512:(n + 1) * 512], in0=ps,
                    in1=xa[:, n * 512:(n + 1) * 512], op=ALU.add),
                [ps, xa[:, n * 512:(n + 1) * 512]], [ot[:, n * 512:(n + 1) * 512]],
                tag=f"wo_add{tt}_{n}")
        P.dma("act", xout[tt * 128:(tt + 1) * 128, :], ot[:], tag=f"wo_st{tt}")


STAGE_FUNCS["l2_attn"] = dict(prep=lambda inputs, carry: prep_l2_attn(inputs),
                              dram=dram_l2_attn, emit=emit_l2_attn, extra=[])



L2F_DEPTH = 6
WIN = 256
SP1 = S + 8


def _l2f_rotary_host():
    half = np.arange(0, 64, 2, dtype=np.float64)
    inv_freq = 1.0 / 10000.0 ** (half / 64.0)
    t = np.arange(S, dtype=np.float64)
    fr = t[:, None] * inv_freq[None, :]
    freqs = np.concatenate([fr, fr], axis=1)
    base_scale = (half + 0.4 * 64) / (1.4 * 64)
    power = (t - S // 2) / (WIN // 2)
    sc = base_scale[None, :] ** power[:, None]
    scale = np.concatenate([sc, sc], axis=1)
    cos = np.cos(freqs); sin = np.sin(freqs)
    return (np.ascontiguousarray((cos * scale).astype(np.float32)),
            np.ascontiguousarray((sin * scale).astype(np.float32)),
            np.ascontiguousarray((cos / scale).astype(np.float32)),
            np.ascontiguousarray((sin / scale).astype(np.float32)))


def prep_l2_ffn(inputs, carry):
    qc, qs, kc, ks = _l2f_rotary_host()
    W1 = np.asarray(inputs["L2_ffn_W1"], np.float32)
    W2 = np.asarray(inputs["L2_ffn_W2"], np.float32)
    b1 = np.asarray(inputs["L2_ffn_b1"], np.float32)
    HP = 2816
    w1a = np.zeros((6, D, HP), np.float32); w1a[:, :, :2730] = W1[:, :, :2730]
    w1g = np.zeros((6, D, HP), np.float32); w1g[:, :, :2730] = W1[:, :, 2730:]
    w2p = np.zeros((6, HP, D), np.float32); w2p[:, :2730, :] = W2
    b1a = np.zeros((6, HP), np.float32); b1a[:, :2730] = b1[:, :2730]
    b1g = np.zeros((6, HP), np.float32); b1g[:, :2730] = b1[:, 2730:]
    return {
        "l2f_wqkv": np.ascontiguousarray(inputs["L2_ffn_Wqkv"]),
        "l2f_wo": np.ascontiguousarray(inputs["L2_ffn_Wo"]),
        "l2f_w1a": np.ascontiguousarray(w1a),
        "l2f_w1g": np.ascontiguousarray(w1g),
        "l2f_w2": np.ascontiguousarray(w2p),
        "l2f_b1a": np.ascontiguousarray(b1a.reshape(6, 22, 128).transpose(0, 2, 1)),
        "l2f_b1g": np.ascontiguousarray(b1g.reshape(6, 22, 128).transpose(0, 2, 1)),
        "l2f_b2bc": np.ascontiguousarray(np.broadcast_to(
            np.asarray(inputs["L2_ffn_b2"], np.float32)[:, None, :], (6, 128, 1024))),
        "l2f_lng": np.ascontiguousarray(np.broadcast_to(
            np.asarray(inputs["L2_ffn_ln_g"], np.float32)[:, None, :], (6, 128, 1024))),
        "l2f_lnb": np.ascontiguousarray(np.broadcast_to(
            np.asarray(inputs["L2_ffn_ln_b"], np.float32)[:, None, :], (6, 128, 1024))),
        "l2f_qg": np.ascontiguousarray(np.broadcast_to(
            np.asarray(inputs["L2_ffn_q_scale"], np.float32)[:, None, :], (6, 128, 64))),
        "l2f_kg": np.ascontiguousarray(np.broadcast_to(
            np.asarray(inputs["L2_ffn_k_scale"], np.float32)[:, None, :], (6, 128, 64))),
        "l2f_qc": qc, "l2f_qs": qs, "l2f_kc": kc, "l2f_ks": ks,
        "l2f_id16": _ident16(), "l2f_tri": _tri16(), "l2f_triu": _triu_incl16(),
    }


def dram_l2_ffn(P):
    g = {}
    g["l2f_wqkv"] = P.dram_in("l2f_wqkv", (6, D, 3 * D), F32)
    g["l2f_wo"] = P.dram_in("l2f_wo", (6, D, D), F32)
    g["l2f_w1a"] = P.dram_in("l2f_w1a", (6, D, 2816), F32)
    g["l2f_w1g"] = P.dram_in("l2f_w1g", (6, D, 2816), F32)
    g["l2f_w2"] = P.dram_in("l2f_w2", (6, 2816, D), F32)
    g["l2f_b1a"] = P.dram_in("l2f_b1a", (6, 128, 22), F32)
    g["l2f_b1g"] = P.dram_in("l2f_b1g", (6, 128, 22), F32)
    g["l2f_b2bc"] = P.dram_in("l2f_b2bc", (6, 128, 1024), F32)
    g["l2f_lng"] = P.dram_in("l2f_lng", (6, 128, 1024), F32)
    g["l2f_lnb"] = P.dram_in("l2f_lnb", (6, 128, 1024), F32)
    g["l2f_qg"] = P.dram_in("l2f_qg", (6, 128, 64), F32)
    g["l2f_kg"] = P.dram_in("l2f_kg", (6, 128, 64), F32)
    for nm in ("l2f_qc", "l2f_qs", "l2f_kc", "l2f_ks"):
        g[nm] = P.dram_in(nm, (S, 64), F32)
    g["l2f_id16"] = P.dram_in("l2f_id16", (128, 128), BF16)
    g["l2f_tri"] = P.dram_in("l2f_tri", (128, 128), BF16)
    g["l2f_triu"] = P.dram_in("l2f_triu", (128, 128), BF16)
    g["l2f_xi"] = P.dram_tmp("l2f_xi", (S, D), F32)
    return g


def emit_l2_ffn(P, xin, xout, G):
    c = alloc_common(P, psums=False)
    ensure_ident(P, c, G, "l2f_id16")
    pS = P.psum("l2f_pS", [128, 1536], F32)
    pV = P.psum("l2f_pV", [128, 512], F32)
    pQ = P.psum("l2f_pQ", [128, 1024], F32)
    pT0 = P.psum("l2f_pT0", [128, 1024], BF16)
    pT1 = P.psum("l2f_pT1", [128, 1024], BF16)
    c["pT0"] = pT0; c["pT1"] = pT1
    big2 = P.sbuf("l2f_big2", [128, NF * SP1], BF16)
    pool = P.sbuf("l2f_pool", [128, 49152], BF16)
    wqkv = pool[:, 0:24576].rearrange("p (ki o) -> p ki o", o=3072)
    wo16 = pool[:, 24576:32768].rearrange("p (ki o) -> p ki o", o=1024)
    qrT = pool[:, 32768:34816]
    krT = [pool[:, 34816:36864], pool[:, 36864:38912]]
    vw = [pool[:, 38912:40960], pool[:, 40960:43008]]
    yw = pool[:, 43008:45056]
    w2v = pool[:, 0:22528].rearrange("p (j o) -> p j o", o=1024)
    hgb = pool[:, 22528:45056].rearrange("p (j t) -> p j t", t=1024)
    w1b = [P.sbuf(f"l2f_w1b{i}", [128, 2048], BF16) for i in range(2)]
    tri16 = P.sbuf("l2f_tri_s", [128, 128], BF16)
    triu16 = P.sbuf("l2f_triu_s", [128, 128], BF16)
    qg = P.sbuf("l2f_qg_s", [128, 64], F32)
    kg = P.sbuf("l2f_kg_s", [128, 64], F32)
    rotq = [P.sbuf(f"l2f_rq{i}", [128, 64], F32) for i in range(2)]
    rotk = [P.sbuf(f"l2f_rk{i}", [128, 64], F32) for i in range(2)]
    lng = P.sbuf("l2f_lng_s", [128, 1024], BF16)
    lnb = P.sbuf("l2f_lnb_s", [128, 1024], BF16)
    b2bc = P.sbuf("l2f_b2_s", [128, 1024], BF16)
    b1a = P.sbuf("l2f_b1a_s", [128, 22], F32)
    b1g = P.sbuf("l2f_b1g_s", [128, 22], F32)
    sqb = c["sq"]
    qn16 = P.sbuf("l2f_qn", [128, 1024], BF16)
    rot16 = P.sbuf("l2f_rot", [128, 1024], BF16)
    qr16 = P.sbuf("l2f_qr", [128, 1024], BF16)
    st2 = P.sbuf("l2f_st2", [128, 32], F32)
    p16 = P.sbuf("l2f_p16", [128, 768], BF16)
    pt16 = P.sbuf("l2f_pt16", [128, 768], BF16)
    ge16 = P.sbuf("l2f_ge16", [128, 512], BF16)
    P.dma("sp", tri16[:], G("l2f_tri"))
    P.dma("sp", triu16[:], G("l2f_triu"))

    st = c["stats"]

    def ln_tile(P, xa, out16, eps, affine=None, tag=""):
        sop(P, lambda e, xa=xa: e.activation(out=c["sq"][:], in_=xa[:], func=AF.Square,
                                             accum_out=st[:, 1:2]), [xa[:]],
            [c["sq"][:], st[:, 1:2]], tag=tag + "sq")
        vop(P, lambda e, xa=xa: e.reduce_sum(out=st[:, 0:1], in_=xa[:], axis=AX.X),
            [xa[:]], [st[:, 0:1]], tag=tag + "su")
        vop(P, lambda e: e.tensor_scalar(out=st[:, 2:3], in0=st[:, 0:1], scalar1=1.0 / D,
                                         scalar2=None, op0=ALU.mult), [st[:, 0:1]], [st[:, 2:3]])
        vop(P, lambda e: e.tensor_scalar(out=st[:, 3:4], in0=st[:, 1:2], scalar1=1.0 / D,
                                         scalar2=None, op0=ALU.mult), [st[:, 1:2]], [st[:, 3:4]])
        vop(P, lambda e: e.tensor_tensor(out=st[:, 4:5], in0=st[:, 2:3], in1=st[:, 2:3],
                                         op=ALU.mult), [st[:, 2:3]], [st[:, 4:5]])
        vop(P, lambda e: e.tensor_sub(out=st[:, 5:6], in0=st[:, 3:4], in1=st[:, 4:5]),
            [st[:, 3:4], st[:, 4:5]], [st[:, 5:6]])
        epsap = P.constant(eps)
        sop(P, lambda e, epsap=epsap: e.activation(out=st[:, 6:7], in_=st[:, 5:6],
                                                   func=AF.Sqrt, bias=epsap),
            [st[:, 5:6], epsap], [st[:, 6:7]])
        vop(P, lambda e: e.reciprocal(out=st[:, 7:8], in_=st[:, 6:7]), [st[:, 6:7]], [st[:, 7:8]])
        vop(P, lambda e: e.tensor_tensor(out=st[:, 8:9], in0=st[:, 2:3], in1=st[:, 7:8],
                                         op=ALU.mult), [st[:, 2:3], st[:, 7:8]], [st[:, 8:9]])
        vop(P, lambda e: e.tensor_scalar(out=st[:, 9:10], in0=st[:, 8:9], scalar1=-1.0,
                                         scalar2=None, op0=ALU.mult), [st[:, 8:9]], [st[:, 9:10]])
        if affine is None:
            vop(P, lambda e, xa=xa, out16=out16: e.tensor_scalar(out=out16, in0=xa[:],
                    scalar1=st[:, 7:8], scalar2=st[:, 9:10], op0=ALU.mult, op1=ALU.add),
                [xa[:], st[:, 7:8], st[:, 9:10]], [out16], tag=tag + "n")
        else:
            gg, bb = affine
            vop(P, lambda e, xa=xa: e.tensor_scalar(out=c["sq"][:], in0=xa[:],
                    scalar1=st[:, 7:8], scalar2=st[:, 9:10], op0=ALU.mult, op1=ALU.add),
                [xa[:], st[:, 7:8], st[:, 9:10]], [c["sq"][:]], tag=tag + "n")
            vop(P, lambda e, gg=gg: e.tensor_mul(out=c["sq"][:], in0=c["sq"][:], in1=gg),
                [c["sq"][:], gg], [c["sq"][:]], tag=tag + "g")
            vop(P, lambda e, bb=bb, out16=out16: e.tensor_tensor(out=out16, in0=c["sq"][:],
                    in1=bb, op=ALU.add),
                [c["sq"][:], bb], [out16], tag=tag + "b")

    def to_big2(P, src16, tt, tag=""):
        for k in range(NF):
            pb = pT0 if k % 2 == 0 else pT1
            slot = (k // 2) * 128
            tr(P, pb[:, slot:slot + 128], src16[:, k * 128:(k + 1) * 128], c["ident16"][:],
               tag=f"{tag}tr{tt}_{k}")
            dst = big2[:, k * SP1 + 1 + tt * 128: k * SP1 + 1 + (tt + 1) * 128]
            vop(P, lambda e, pb=pb, slot=slot, dst=dst: e.tensor_copy(out=dst, in_=pb[:, slot:slot + 128]),
                [pb[:, slot:slot + 128]], [dst], tag=f"{tag}te{tt}_{k}")

    xi = G("l2f_xi")

    for tt in range(NT):
        xa = c["xtm"][tt % 2]
        P.dma("sp", xa[:], xin[tt * 128:(tt + 1) * 128, :], tag=f"in_ld{tt}")
        ln_tile(P, xa, c["lntile"][tt % 2][:], 1e-5, tag=f"in_ln{tt}_")
        ot = c["otm"][tt % 2]
        vop(P, lambda e, xa=xa, ot=ot: e.tensor_scalar(out=ot[:], in0=xa[:],
                scalar1=st[:, 7:8], scalar2=st[:, 9:10], op0=ALU.mult, op1=ALU.add),
            [xa[:], st[:, 7:8], st[:, 9:10]], [ot[:]], tag=f"in_f{tt}")
        P.dma("act", xi[tt * 128:(tt + 1) * 128, :], ot[:], tag=f"in_st{tt}")

    for L in range(L2F_DEPTH):
        for k in range(NF):
            vop(P, lambda e, k=k: e.memset(big2[:, k * SP1:k * SP1 + 1], 0.0),
                [], [big2[:, k * SP1:k * SP1 + 1]], tag=f"L{L}a_z{k}")
        P.dma("pool", wqkv, G("l2f_wqkv")[L].rearrange("(ki p) o -> p ki o", p=128),
              tag=f"L{L}_ldqkv")
        P.dma("pool", wo16, G("l2f_wo")[L].rearrange("(ki p) o -> p ki o", p=128),
              tag=f"L{L}_ldwo")
        P.dma("sp", qg[:], G("l2f_qg")[L], tag=f"L{L}_ldqg")
        P.dma("sp", kg[:], G("l2f_kg")[L], tag=f"L{L}_ldkg")
        for tt in range(NT):
            xa = c["xtm"][tt % 2]
            if tt == 0:
                P.dma("sp", xa[:, 0:512], xi[0:128, 0:512], tag=f"L{L}a_ld{tt}a")
                vop(P, lambda e, xa=xa: e.memset(xa[0:1, 512:1024], 0.0),
                    [], [xa[0:1, 512:1024]], tag=f"L{L}a_z0")
                P.dma("sp", xa[1:128, 512:1024], xi[0:127, 512:1024], tag=f"L{L}a_ld{tt}b")
            else:
                P.dma("sp", xa[:, 0:512], xi[tt * 128:(tt + 1) * 128, 0:512],
                      tag=f"L{L}a_ld{tt}a")
                P.dma("sp", xa[:, 512:1024], xi[tt * 128 - 1:(tt + 1) * 128 - 1, 512:1024],
                      tag=f"L{L}a_ld{tt}b")
            lt = c["lntile"][tt % 2]
            vop(P, lambda e, xa=xa, lt=lt: e.tensor_copy(out=lt[:], in_=xa[:]),
                [xa[:]], [lt[:]], tag=f"L{L}a_c{tt}")
            to_big2(P, lt, tt, tag=f"L{L}a_")
        for w in range(16):
            par = w % 2
            for t2 in range(2):
                tt = w * 2 + t2
                for part in range(3):
                    for n in range(2):
                        ps = pQ[:, n * 512:(n + 1) * 512]
                        for ki in range(NF):
                            mm(P, ps, big2[:, ki * SP1 + 1 + tt * 128: ki * SP1 + 1 + (tt + 1) * 128],
                               wqkv[:, ki, part * 1024 + n * 512: part * 1024 + (n + 1) * 512],
                               start=(ki == 0), stop=(ki == NF - 1),
                               tag=f"L{L}w{w}t{t2}p{part}n{n}")
                        if part == 2:
                            dst = vw[par][:, t2 * 1024 + n * 512: t2 * 1024 + (n + 1) * 512]
                            vop(P, lambda e, ps=ps, dst=dst: e.tensor_copy(out=dst, in_=ps),
                                [ps], [dst], tag=f"L{L}w{w}t{t2}vev{n}")
                        else:
                            dst = qn16[:, n * 512:(n + 1) * 512]
                            vop(P, lambda e, ps=ps, dst=dst: e.tensor_copy(out=dst, in_=ps),
                                [ps], [dst], tag=f"L{L}w{w}t{t2}qev{part}_{n}")
                    if part == 2:
                        continue
                    sop(P, lambda e: e.activation(out=sqb[:], in_=qn16[:], func=AF.Square),
                        [qn16[:]], [sqb[:]], tag=f"L{L}w{w}t{t2}sq{part}")
                    vop(P, lambda e: e.tensor_reduce(
                            out=st2[:, 0:16].unsqueeze(2),
                            in_=sqb[:].rearrange("p (h d) -> p h d", d=64), axis=AX.X,
                            op=ALU.add),
                        [sqb[:]], [st2[:, 0:16]], tag=f"L{L}w{w}t{t2}ss{part}")
                    sop(P, lambda e: e.activation(out=st2[:, 16:32], in_=st2[:, 0:16],
                                                  func=AF.Sqrt),
                        [st2[:, 0:16]], [st2[:, 16:32]], tag=f"L{L}w{w}t{t2}sr{part}")
                    vop(P, lambda e: e.tensor_scalar(out=st2[:, 16:32], in0=st2[:, 16:32],
                            scalar1=1e-12, scalar2=None, op0=ALU.max),
                        [st2[:, 16:32]], [st2[:, 16:32]], tag=f"L{L}w{w}t{t2}sm{part}")
                    vop(P, lambda e: e.reciprocal(out=st2[:, 0:16], in_=st2[:, 16:32]),
                        [st2[:, 16:32]], [st2[:, 0:16]], tag=f"L{L}w{w}t{t2}sc{part}")
                    gain = qg if part == 0 else kg
                    vop(P, lambda e: e.tensor_tensor(
                            out=qn16[:].rearrange("p (h d) -> p h d", d=64),
                            in0=qn16[:].rearrange("p (h d) -> p h d", d=64),
                            in1=st2[:, 0:16].unsqueeze(2).broadcast_to((128, 16, 64)),
                            op=ALU.mult),
                        [qn16[:], st2[:, 0:16]], [qn16[:]], tag=f"L{L}w{w}t{t2}nm{part}")
                    vop(P, lambda e, gain=gain: e.tensor_tensor(
                            out=qn16[:].rearrange("p (h d) -> p h d", d=64),
                            in0=qn16[:].rearrange("p (h d) -> p h d", d=64),
                            in1=gain[:].unsqueeze(1).rearrange("p o d -> p o d").broadcast_to((128, 16, 64)),
                            op=ALU.mult),
                        [qn16[:], gain[:]], [qn16[:]], tag=f"L{L}w{w}t{t2}gn{part}")
                    vop(P, lambda e: e.tensor_scalar(
                            out=rot16[:].rearrange("p (h d) -> p h d", d=64)[:, :, 0:32],
                            in0=qn16[:].rearrange("p (h d) -> p h d", d=64)[:, :, 32:64],
                            scalar1=-1.0, scalar2=None, op0=ALU.mult),
                        [qn16[:]], [rot16[:]], tag=f"L{L}w{w}t{t2}r1{part}")
                    vop(P, lambda e: e.tensor_copy(
                            out=rot16[:].rearrange("p (h d) -> p h d", d=64)[:, :, 32:64],
                            in_=qn16[:].rearrange("p (h d) -> p h d", d=64)[:, :, 0:32]),
                        [qn16[:]], [rot16[:]], tag=f"L{L}w{w}t{t2}r2{part}")
                    rbuf = rotq if part == 0 else rotk
                    cn, sn = ("l2f_qc", "l2f_qs") if part == 0 else ("l2f_kc", "l2f_ks")
                    P.dma("sp", rbuf[0][:], G(cn)[tt * 128:(tt + 1) * 128, :],
                          tag=f"L{L}w{w}t{t2}lc{part}")
                    P.dma("sp", rbuf[1][:], G(sn)[tt * 128:(tt + 1) * 128, :],
                          tag=f"L{L}w{w}t{t2}ls{part}")
                    vop(P, lambda e, rbuf=rbuf: e.tensor_tensor(
                            out=qn16[:].rearrange("p (h d) -> p h d", d=64),
                            in0=qn16[:].rearrange("p (h d) -> p h d", d=64),
                            in1=rbuf[0][:].unsqueeze(1).broadcast_to((128, 16, 64)),
                            op=ALU.mult),
                        [qn16[:], rbuf[0][:]], [qn16[:]], tag=f"L{L}w{w}t{t2}qc{part}")
                    vop(P, lambda e, rbuf=rbuf: e.tensor_tensor(
                            out=rot16[:].rearrange("p (h d) -> p h d", d=64),
                            in0=rot16[:].rearrange("p (h d) -> p h d", d=64),
                            in1=rbuf[1][:].unsqueeze(1).broadcast_to((128, 16, 64)),
                            op=ALU.mult),
                        [rot16[:], rbuf[1][:]], [rot16[:]], tag=f"L{L}w{w}t{t2}qs{part}")
                    vop(P, lambda e: e.tensor_add(out=qr16[:], in0=qn16[:], in1=rot16[:]),
                        [qn16[:], rot16[:]], [qr16[:]], tag=f"L{L}w{w}t{t2}qa{part}")
                    dstT = qrT if part == 0 else krT[par]
                    for k in range(NF):
                        pb = pT0 if k % 2 == 0 else pT1
                        slot = (k // 2) * 128
                        tr(P, pb[:, slot:slot + 128], qr16[:, k * 128:(k + 1) * 128],
                           c["ident16"][:], tag=f"L{L}w{w}t{t2}T{part}_{k}")
                        dst = dstT[:, t2 * 1024 + k * 128: t2 * 1024 + (k + 1) * 128]
                        vop(P, lambda e, pb=pb, slot=slot, dst=dst: e.tensor_copy(
                                out=dst, in_=pb[:, slot:slot + 128]),
                            [pb[:, slot:slot + 128]], [dst], tag=f"L{L}w{w}t{t2}E{part}_{k}")
            for hp in range(8):
                for t2 in range(2):
                    if t2 == 0:
                        acts = [(0, "prev", 0, "triu"), (1, "prev", 1, None),
                                (2, "cur", 0, "tri")]
                    else:
                        acts = [(0, "prev", 1, "triu"), (1, "cur", 0, None),
                                (2, "cur", 1, "tri")]
                    if w == 0:
                        acts = [a for a in acts if a[1] != "prev"]
                    nact = len(acts)
                    specs = []
                    for hh in range(2):
                        b0 = hh * 64
                        qh = qrT[b0:b0 + 64, t2 * 1024 + hp * 128: t2 * 1024 + hp * 128 + 128]
                        for ai, (ks, srcw, kt, msk) in enumerate(acts):
                            kbufp = krT[par] if srcw == "cur" else krT[1 - par]
                            kh = kbufp[b0:b0 + 64,
                                       kt * 1024 + hp * 128: kt * 1024 + hp * 128 + 128]
                            col = hh * 384 + ai * 128
                            specs.append((col, qh, kh))
                    banks = {}
                    for (col, qh, kh) in specs:
                        banks.setdefault(col // 512, []).append((col, qh, kh))
                    for bk, lst in banks.items():
                        for si, (col, qh, kh) in enumerate(lst):
                            mm(P, pS[:, col:col + 128], qh, kh,
                               start=(si == 0), stop=(si == len(lst) - 1),
                               tag=f"L{L}w{w}hp{hp}t{t2}s{col}")
                    wid = nact * 128
                    for hh in range(2):
                        sl = slice(hh * 384, hh * 384 + wid)
                        vop(P, lambda e, sl=sl, hh=hh: e.reduce_max(out=st2[:, 20 + hh:21 + hh],
                                in_=pS[:, sl], axis=AX.X),
                            [pS[:, sl]], [st2[:, 20 + hh:21 + hh]], tag=f"L{L}w{w}hp{hp}t{t2}mx{hh}")
                        vop(P, lambda e, hh=hh: e.tensor_scalar(out=st2[:, 22 + hh:23 + hh],
                                in0=st2[:, 20 + hh:21 + hh], scalar1=-8.0, scalar2=None,
                                op0=ALU.mult),
                            [st2[:, 20 + hh:21 + hh]], [st2[:, 22 + hh:23 + hh]],
                            tag=f"L{L}w{w}hp{hp}t{t2}nm{hh}")
                        sop(P, lambda e, sl=sl, hh=hh, wid=wid: e.activation(
                                out=p16[:, hh * 384:hh * 384 + wid], in_=pS[:, sl],
                                func=AF.Exp, scale=8.0, bias=st2[:, 22 + hh:23 + hh]),
                            [pS[:, sl], st2[:, 22 + hh:23 + hh]],
                            [p16[:, hh * 384:hh * 384 + wid]], tag=f"L{L}w{w}hp{hp}t{t2}e{hh}")
                        for ai, (ks, srcw, kt, msk) in enumerate(acts):
                            if msk is None:
                                continue
                            mt = tri16 if msk == "tri" else triu16
                            slm = slice(hh * 384 + ai * 128, hh * 384 + (ai + 1) * 128)
                            vop(P, lambda e, slm=slm, mt=mt: e.tensor_tensor(
                                    out=p16[:, slm], in0=p16[:, slm], in1=mt[:], op=ALU.mult),
                                [p16[:, slm], mt[:]], [p16[:, slm]],
                                tag=f"L{L}w{w}hp{hp}t{t2}m{hh}_{ai}")
                        vop(P, lambda e, hh=hh, wid=wid: e.reduce_sum(out=st2[:, 24 + hh:25 + hh],
                                in_=p16[:, hh * 384:hh * 384 + wid], axis=AX.X),
                            [p16[:, hh * 384:hh * 384 + wid]], [st2[:, 24 + hh:25 + hh]],
                            tag=f"L{L}w{w}hp{hp}t{t2}d{hh}")
                        vop(P, lambda e, hh=hh: e.reciprocal(out=st2[:, 26 + hh:27 + hh],
                                in_=st2[:, 24 + hh:25 + hh]),
                            [st2[:, 24 + hh:25 + hh]], [st2[:, 26 + hh:27 + hh]],
                            tag=f"L{L}w{w}hp{hp}t{t2}r{hh}")
                    nT = 2 * nact
                    for z in range(nT):
                        hh, ai = divmod(z, nact)
                        src = p16[:, hh * 384 + ai * 128: hh * 384 + (ai + 1) * 128]
                        pb = pT0 if z % 2 == 0 else pT1
                        slot = (z // 2 % 4) * 128
                        tr(P, pb[:, slot:slot + 128], src, c["ident16"][:],
                           tag=f"L{L}w{w}hp{hp}t{t2}pt{z}")
                        dst = pt16[:, z * 128:(z + 1) * 128]
                        vop(P, lambda e, pb=pb, slot=slot, dst=dst: e.tensor_copy(
                                out=dst, in_=pb[:, slot:slot + 128]),
                            [pb[:, slot:slot + 128]], [dst], tag=f"L{L}w{w}hp{hp}t{t2}pe{z}")
                    for hh in range(2):
                        h = 2 * hp + hh
                        for ai, (ks, srcw, kt, msk) in enumerate(acts):
                            vbufp = vw[par] if srcw == "cur" else vw[1 - par]
                            vsrc = vbufp[:, kt * 1024 + h * 64: kt * 1024 + h * 64 + 64]
                            mm(P, pV[:, hh * 64:(hh + 1) * 64],
                               pt16[:, (hh * nact + ai) * 128:(hh * nact + ai + 1) * 128],
                               vsrc, start=(ai == 0), stop=(ai == nact - 1),
                               tag=f"L{L}w{w}h{h}t{t2}pv{ai}")
                        dst = yw[:, t2 * 1024 + h * 64: t2 * 1024 + h * 64 + 64]
                        vop(P, lambda e, hh=hh, dst=dst: e.tensor_scalar(out=dst,
                                in0=pV[:, hh * 64:(hh + 1) * 64],
                                scalar1=st2[:, 26 + hh:27 + hh], scalar2=None, op0=ALU.mult),
                            [pV[:, hh * 64:(hh + 1) * 64], st2[:, 26 + hh:27 + hh]], [dst],
                            tag=f"L{L}w{w}h{h}t{t2}y")
            for t2 in range(2):
                tt = w * 2 + t2
                for k in range(NF):
                    pb = pT0 if k % 2 == 0 else pT1
                    slot = (k // 2) * 128
                    tr(P, pb[:, slot:slot + 128],
                       yw[:, t2 * 1024 + k * 128: t2 * 1024 + (k + 1) * 128],
                       c["ident16"][:], tag=f"L{L}w{w}t{t2}yT{k}")
                    dst = qr16[:, k * 128:(k + 1) * 128]
                    vop(P, lambda e, pb=pb, slot=slot, dst=dst: e.tensor_copy(
                            out=dst, in_=pb[:, slot:slot + 128]),
                        [pb[:, slot:slot + 128]], [dst], tag=f"L{L}w{w}t{t2}yE{k}")
                xa = c["xtm"][t2]
                P.dma("sp", xa[:], xi[tt * 128:(tt + 1) * 128, :], tag=f"L{L}w{w}t{t2}rx")
                ot = c["otm"][0]
                for n in range(2):
                    ps = pQ[:, n * 512:(n + 1) * 512]
                    for k in range(NF):
                        mm(P, ps, qr16[:, k * 128:(k + 1) * 128],
                           wo16[:, k, n * 512:(n + 1) * 512],
                           start=(k == 0), stop=(k == NF - 1), tag=f"L{L}w{w}t{t2}wo{n}")
                    vop(P, lambda e, ps=ps, xa=xa, n=n, ot=ot: e.tensor_tensor(
                            out=ot[:, n * 512:(n + 1) * 512], in0=ps,
                            in1=xa[:, n * 512:(n + 1) * 512], op=ALU.add),
                        [ps, xa[:, n * 512:(n + 1) * 512]], [ot[:, n * 512:(n + 1) * 512]],
                        tag=f"L{L}w{w}t{t2}wa{n}")
                P.dma("act", xi[tt * 128:(tt + 1) * 128, :], ot[:], tag=f"L{L}w{w}t{t2}ws")

        P.dma("pool", lng[:], G("l2f_lng")[L], tag=f"L{L}f_lg")
        P.dma("pool", lnb[:], G("l2f_lnb")[L], tag=f"L{L}f_lb")
        P.dma("pool", b2bc[:], G("l2f_b2bc")[L], tag=f"L{L}f_b2")
        P.dma("sp", b1a[:], G("l2f_b1a")[L], tag=f"L{L}f_b1a")
        P.dma("sp", b1g[:], G("l2f_b1g")[L], tag=f"L{L}f_b1g")
        P.dma("pool", w2v, G("l2f_w2")[L].rearrange("(j p) o -> p j o", p=128),
              tag=f"L{L}f_ldw2")
        for tt in range(NT):
            xa = c["xtm"][tt % 2]
            if tt == 0:
                P.dma("sp", xa[:, 0:512], xi[0:128, 0:512], tag=f"L{L}f_ld{tt}a")
                vop(P, lambda e, xa=xa: e.memset(xa[0:1, 512:1024], 0.0),
                    [], [xa[0:1, 512:1024]], tag=f"L{L}f_z0")
                P.dma("sp", xa[1:128, 512:1024], xi[0:127, 512:1024], tag=f"L{L}f_ld{tt}b")
            else:
                P.dma("sp", xa[:, 0:512], xi[tt * 128:(tt + 1) * 128, 0:512],
                      tag=f"L{L}f_ld{tt}a")
                P.dma("sp", xa[:, 512:1024], xi[tt * 128 - 1:(tt + 1) * 128 - 1, 512:1024],
                      tag=f"L{L}f_ld{tt}b")
            lt = c["lntile"][tt % 2]
            ln_tile(P, xa, lt[:], 1e-5, affine=(lng[:], lnb[:]), tag=f"L{L}f_ln{tt}_")
            to_big2(P, lt, tt, tag=f"L{L}f_")
        for tbb in range(4):
            for j in range(22):
                wb = w1b[j % 2][:].rearrange("p (ki o) -> p ki o", o=256)
                P.dma("pool", wb[:, :, 0:128],
                      G("l2f_w1a")[L][:, j * 128:(j + 1) * 128].rearrange(
                          "(ki p) o -> p ki o", p=128), tag=f"L{L}f_w1a{tbb}_{j}")
                P.dma("pool", wb[:, :, 128:256],
                      G("l2f_w1g")[L][:, j * 128:(j + 1) * 128].rearrange(
                          "(ki p) o -> p ki o", p=128), tag=f"L{L}f_w1g{tbb}_{j}")
                for half in range(2):
                    tok0 = tbb * 1024 + half * 512
                    psa = pQ[:, 0:512]
                    psg = pQ[:, 512:1024]
                    for ki in range(NF):
                        mm(P, psa, wb[:, ki, 0:128],
                           big2[:, ki * SP1 + 1 + tok0: ki * SP1 + 1 + tok0 + 512],
                           start=(ki == 0), stop=(ki == NF - 1),
                           tag=f"L{L}f_a{tbb}_{j}_{half}")
                    for ki in range(NF):
                        mm(P, psg, wb[:, ki, 128:256],
                           big2[:, ki * SP1 + 1 + tok0: ki * SP1 + 1 + tok0 + 512],
                           start=(ki == 0), stop=(ki == NF - 1),
                           tag=f"L{L}f_g{tbb}_{j}_{half}")
                    sop(P, lambda e, psg=psg, j=j: e.activation(out=ge16[:], in_=psg,
                            func=AF.Gelu, bias=b1g[:, j:j + 1]),
                        [psg, b1g[:, j:j + 1]], [ge16[:]], tag=f"L{L}f_ge{tbb}_{j}_{half}")
                    dst = hgb[:, j, half * 512:(half + 1) * 512]
                    vop(P, lambda e, psa=psa, j=j, dst=dst: e.scalar_tensor_tensor(
                            out=dst, in0=psa, scalar=b1a[:, j:j + 1], in1=ge16[:],
                            op0=ALU.add, op1=ALU.mult),
                        [psa, b1a[:, j:j + 1], ge16[:]], [dst], tag=f"L{L}f_hg{tbb}_{j}_{half}")
            for ts in range(8):
                tt = tbb * 8 + ts
                xa = c["xtm"][ts % 2]
                P.dma("sp", xa[:], xi[tt * 128:(tt + 1) * 128, :], tag=f"L{L}f_rx{tt}")
                ot = c["otm"][0]
                for n in range(2):
                    ps = pQ[:, n * 512:(n + 1) * 512]
                    for j in range(22):
                        mm(P, ps, hgb[:, j, ts * 128:(ts + 1) * 128],
                           w2v[:, j, n * 512:(n + 1) * 512],
                           start=(j == 0), stop=(j == 21), tag=f"L{L}f_o{tt}_{n}")
                    vop(P, lambda e, ps=ps, n=n: e.tensor_tensor(out=c["sq"][:, n * 512:(n + 1) * 512],
                            in0=ps, in1=b2bc[:, n * 512:(n + 1) * 512], op=ALU.add),
                        [ps, b2bc[:, n * 512:(n + 1) * 512]],
                        [c["sq"][:, n * 512:(n + 1) * 512]], tag=f"L{L}f_ob{tt}_{n}")
                    vop(P, lambda e, xa=xa, n=n, ot=ot: e.tensor_tensor(
                            out=ot[:, n * 512:(n + 1) * 512],
                            in0=c["sq"][:, n * 512:(n + 1) * 512],
                            in1=xa[:, n * 512:(n + 1) * 512], op=ALU.add),
                        [c["sq"][:, n * 512:(n + 1) * 512], xa[:, n * 512:(n + 1) * 512]],
                        [ot[:, n * 512:(n + 1) * 512]], tag=f"L{L}f_oa{tt}_{n}")
                P.dma("act", xi[tt * 128:(tt + 1) * 128, :], ot[:], tag=f"L{L}f_os{tt}")

    for tt in range(NT):
        xa = c["xtm"][tt % 2]
        ya = c["otm"][0]
        P.dma("sp", xa[:], xin[tt * 128:(tt + 1) * 128, :], tag=f"fz_ldx{tt}")
        P.dma("sp", ya[:], xi[tt * 128:(tt + 1) * 128, :], tag=f"fz_ldy{tt}")
        vop(P, lambda e, xa=xa, ya=ya: e.tensor_add(out=ya[:], in0=ya[:], in1=xa[:]),
            [xa[:], ya[:]], [ya[:]], tag=f"fz_add{tt}")
        P.dma("act", xout[tt * 128:(tt + 1) * 128, :], ya[:], tag=f"fz_st{tt}")


STAGE_FUNCS["l2_ffn"] = dict(prep=prep_l2_ffn, dram=dram_l2_ffn, emit=emit_l2_ffn, extra=[])



def prep_l3_attn(inputs):
    ar = (np.arange(1, S + 1, dtype=np.float32).reshape(32, 128).T).copy()
    return {
        "l3a_wqT": np.ascontiguousarray(inputs["L3_attn_Wq"].T),
        "l3a_wkT": np.ascontiguousarray(inputs["L3_attn_Wk"].T),
        "l3a_wvT": np.ascontiguousarray(inputs["L3_attn_Wv"].T),
        "l3a_woT": np.ascontiguousarray(inputs["L3_attn_Wo"].T),
        "l3a_bobc": np.ascontiguousarray(np.broadcast_to(
            np.asarray(inputs["L3_attn_bo"], np.float32)[None, :], (128, 1024))),
        "l3a_arange": ar,
        "l3a_id16": _ident16(), "l3a_triu": _triu_incl16(),
    }


def dram_l3_attn(P):
    g = {}
    for nm in ("l3a_wqT", "l3a_wkT", "l3a_wvT", "l3a_woT"):
        g[nm] = P.dram_in(nm, (D, D), F32)
    g["l3a_bobc"] = P.dram_in("l3a_bobc", (128, 1024), F32)
    g["l3a_arange"] = P.dram_in("l3a_arange", (128, 32), F32)
    g["l3a_id16"] = P.dram_in("l3a_id16", (128, 128), BF16)
    g["l3a_triu"] = P.dram_in("l3a_triu", (128, 128), BF16)
    g["l3a_qTd"] = P.dram_tmp("l3a_qTd", (16, 64, S), BF16)
    g["l3a_kTd"] = P.dram_tmp("l3a_kTd", (16, 64, S), BF16)
    return g


def emit_l3_attn(P, xin, xout, G):
    c = alloc_common(P)
    ensure_ident(P, c, G, "l3a_id16")
    big = P.sbuf("l3a_big", [128, NF * S], BF16)
    vsb = P.sbuf("l3a_vsb", [128, NT * 1024], BF16)
    w16 = P.sbuf("l3a_w16", [128, 8192], BF16)
    wv_ = w16[:].rearrange("p (ki o) -> p ki o", o=1024)
    triu = P.sbuf("l3a_triu_s", [128, 128], BF16)
    qbuf = P.sbuf("l3a_qbuf", [64, S], BF16)
    kbuf = P.sbuf("l3a_kbuf", [64, S], BF16)
    sq32 = P.sbuf("l3a_sq32", [64, 512], F32)
    q2h = P.sbuf("l3a_q2h", [64, 512], BF16)
    q2l = P.sbuf("l3a_q2l", [64, 512], BF16)
    nr = P.sbuf("l3a_nr", [1, S], F32)
    ones64 = P.sbuf("l3a_on64", [64, 1], BF16)
    onesr = P.sbuf("l3a_onr", [1, 128], BF16)
    sc8 = P.sbuf("l3a_sc8", [1, 8], F32)
    hi11 = P.sbuf("l3a_hi11", [1, 1], BF16)
    lo11 = P.sbuf("l3a_lo11", [1, 1], BF16)
    spp = P.sbuf("l3a_spp", [128, 1], F32)
    ar32 = P.sbuf("l3a_ar", [128, 32], F32)
    kvs32 = P.sbuf("l3a_kvs", [64, 65], F32)
    kvs16 = P.sbuf("l3a_kvs16", [64, 65], BF16)
    vs32 = P.sbuf("l3a_vs", [1, 64], F32)
    vsh = P.sbuf("l3a_vsh", [1, 64], BF16)
    vsl = P.sbuf("l3a_vsl", [1, 64], BF16)
    vaug = P.sbuf("l3a_vaug", [128, 65], BF16)
    at16 = P.sbuf("l3a_at16", [128, 128], BF16)
    ktk16 = P.sbuf("l3a_ktk", [128, 64], BF16)
    cum32 = P.sbuf("l3a_cum", [128, 64], F32)
    y16 = P.sbuf("l3a_y16", [128, 64], BF16)
    den32 = P.sbuf("l3a_den", [128, 4], F32)
    atT16 = P.sbuf("l3a_atT", [128, 1024], BF16)
    bobc = P.sbuf("l3a_bo_s", [128, 1024], BF16)
    P.dma("sp", triu[:], G("l3a_triu"))
    P.dma("sp", ar32[:], G("l3a_arange"), tag="ld_ar")
    vop(P, lambda e: e.memset(ones64[:], 1.0), [], [ones64[:]])
    vop(P, lambda e: e.memset(onesr[:], 1.0), [], [onesr[:]])
    vop(P, lambda e: e.memset(vaug[:, 64:65], 1.0), [], [vaug[:, 64:65]])

    emit_ln(P, c, xin, big)

    P.dma("pool", wv_, G("l3a_wvT").rearrange("(ki p) o -> p ki o", p=128), tag="ld_wv")
    for tt in range(NT):
        for n in range(2):
            ps = c["pA"][:, n * 512:(n + 1) * 512]
            for ki in range(NF):
                mm(P, ps, big[:, ki * S + tt * 128:ki * S + (tt + 1) * 128],
                   wv_[:, ki, n * 512:(n + 1) * 512],
                   start=(ki == 0), stop=(ki == NF - 1), tag=f"v_mm{tt}_{n}")
            dst = vsb[:, tt * 1024 + n * 512: tt * 1024 + (n + 1) * 512]
            vop(P, lambda e, ps=ps, dst=dst: e.tensor_copy(out=dst, in_=ps),
                [ps], [dst], tag=f"v_ev{tt}_{n}")
    for qk, (wname, dnam) in enumerate([("l3a_wqT", "l3a_qTd"), ("l3a_wkT", "l3a_kTd")]):
        P.dma("pool", wv_, G(wname).rearrange("(ki p) o -> p ki o", p=128), tag=f"ld_w{qk}")
        dstd = G(dnam)
        for oc in range(NF):
            for tb in range(8):
                r = (oc * 8 + tb) % 2
                ps = c["pB"][:, r * 512:(r + 1) * 512]
                for ki in range(NF):
                    mm(P, ps, wv_[:, ki, oc * 128:(oc + 1) * 128],
                       big[:, ki * S + tb * 512:ki * S + (tb + 1) * 512],
                       start=(ki == 0), stop=(ki == NF - 1), tag=f"qk{qk}_mm{oc}_{tb}")
                stg = atT16[:, 0:512] if r == 0 else atT16[:, 512:1024]
                vop(P, lambda e, ps=ps, stg=stg: e.tensor_copy(out=stg, in_=ps),
                    [ps], [stg], tag=f"qk{qk}_ev{oc}_{tb}")
                P.dma("act", dstd[2 * oc, :, tb * 512:(tb + 1) * 512], stg[0:64, :],
                      tag=f"qk{qk}_st0_{oc}_{tb}")
                P.dma("act", dstd[2 * oc + 1, :, tb * 512:(tb + 1) * 512], stg[64:128, :],
                      tag=f"qk{qk}_st1_{oc}_{tb}")

    for h in range(16):
        P.dma("sp", qbuf[:], G("l3a_qTd")[h], tag=f"hd_ldq{h}")
        P.dma("sp", kbuf[:], G("l3a_kTd")[h], tag=f"hd_ldk{h}")
        for which, buf in ((0, qbuf), (1, kbuf)):
            for bl in range(8):
                sl = slice(bl * 512, (bl + 1) * 512)
                sop(P, lambda e, buf=buf, sl=sl: e.activation(out=sq32[:], in_=buf[:, sl],
                                                              func=AF.Square),
                    [buf[:, sl]], [sq32[:]], tag=f"n_sq{h}_{which}_{bl}")
                vop(P, lambda e: e.tensor_copy(out=q2h[:], in_=sq32[:]), [sq32[:]], [q2h[:]])
                vop(P, lambda e: e.tensor_sub(out=q2l[:], in0=sq32[:], in1=q2h[:]),
                    [sq32[:], q2h[:]], [q2l[:]])
                mm(P, c["pC"][0:1, 0:512], ones64[:], q2h[:], start=True, stop=False,
                   tag=f"n_mh{h}_{which}_{bl}")
                mm(P, c["pC"][0:1, 0:512], ones64[:], q2l[:], start=False, stop=True,
                   tag=f"n_ml{h}_{which}_{bl}")
                vop(P, lambda e, sl=sl: e.tensor_copy(out=nr[:, sl], in_=c["pC"][0:1, 0:512]),
                    [c["pC"][0:1, 0:512]], [nr[:, sl]], tag=f"n_ev{h}_{which}_{bl}")
            vop(P, lambda e, which=which: e.reduce_max(out=sc8[:, which:which + 1],
                    in_=nr[:], axis=AX.X),
                [nr[:]], [sc8[:, which:which + 1]], tag=f"n_mx{h}_{which}")
        vop(P, lambda e: e.tensor_tensor(out=sc8[:, 2:3], in0=sc8[:, 0:1], in1=sc8[:, 1:2],
                                         op=ALU.mult), [sc8[:, 0:1], sc8[:, 1:2]], [sc8[:, 2:3]])
        sop(P, lambda e: e.activation(out=sc8[:, 3:4], in_=sc8[:, 2:3], func=AF.Sqrt),
            [sc8[:, 2:3]], [sc8[:, 3:4]])
        vop(P, lambda e: e.reciprocal(out=sc8[:, 4:5], in_=sc8[:, 3:4]),
            [sc8[:, 3:4]], [sc8[:, 4:5]])
        _bcast_scalar(P, c, sc8[:, 4:5], spp[:], onesr[:], hi11[:], lo11[:], f"sb{h}")
        vop(P, lambda e: e.memset(kvs32[:], 0.0), [], [kvs32[:]], tag=f"z_kv{h}")
        vop(P, lambda e: e.memset(kvs16[:], 0.0), [], [kvs16[:]], tag=f"z_kv16{h}")
        vop(P, lambda e: e.memset(vs32[:], 0.0), [], [vs32[:]], tag=f"z_vs{h}")
        vop(P, lambda e: e.memset(vsh[:], 0.0), [], [vsh[:]], tag=f"z_vsh{h}")
        vop(P, lambda e: e.memset(vsl[:], 0.0), [], [vsl[:]], tag=f"z_vsl{h}")
        for ct in range(NT):
            csl = slice(ct * 128, (ct + 1) * 128)
            arange = ar32[:, ct:ct + 1]
            vsrc = vsb[:, ct * 1024 + h * 64: ct * 1024 + h * 64 + 64]
            vop(P, lambda e, vsrc=vsrc: e.tensor_copy(out=vaug[:, 0:64], in_=vsrc),
                [vsrc], [vaug[:, 0:64]], tag=f"sc_v{h}_{ct}")
            mm(P, c["pB"][:, 0:64], triu[:], vaug[:, 0:64], start=True, stop=False,
               tag=f"sc_cu{h}_{ct}")
            mm(P, c["pB"][:, 0:64], onesr[:], vsh[:], start=False, stop=False,
               tag=f"sc_ch{h}_{ct}")
            mm(P, c["pB"][:, 0:64], onesr[:], vsl[:], start=False, stop=True,
               tag=f"sc_cl{h}_{ct}")
            vop(P, lambda e: e.tensor_copy(out=cum32[:], in_=c["pB"][:, 0:64]),
                [c["pB"][:, 0:64]], [cum32[:]], tag=f"sc_ce{h}_{ct}")
            mm(P, c["pA"][:, 0:65], qbuf[:, csl], kvs16[:], start=True, stop=False,
               tag=f"sc_n1{h}_{ct}")
            mm(P, c["pA"][:, 512:640], kbuf[:, csl], qbuf[:, csl], tag=f"sc_at{h}_{ct}")
            vop(P, lambda e: e.tensor_tensor(out=at16[:], in0=c["pA"][:, 512:640],
                    in1=triu[:], op=ALU.mult),
                [c["pA"][:, 512:640], triu[:]], [at16[:]], tag=f"sc_m{h}_{ct}")
            mm(P, c["pA"][:, 0:65], at16[:], vaug[:], start=False, stop=True,
               tag=f"sc_n2{h}_{ct}")
            vop(P, lambda e, arange=arange: e.scalar_tensor_tensor(out=den32[:, 0:1],
                    in0=c["pA"][:, 64:65], scalar=spp[:, 0:1], in1=arange[:],
                    op0=ALU.mult, op1=ALU.add),
                [c["pA"][:, 64:65], spp[:, 0:1], arange[:]], [den32[:, 0:1]],
                tag=f"sc_d{h}_{ct}")
            vop(P, lambda e: e.reciprocal(out=den32[:, 1:2], in_=den32[:, 0:1]),
                [den32[:, 0:1]], [den32[:, 1:2]], tag=f"sc_r{h}_{ct}")
            vop(P, lambda e: e.scalar_tensor_tensor(out=cum32[:],
                    in0=c["pA"][:, 0:64], scalar=spp[:, 0:1], in1=cum32[:],
                    op0=ALU.mult, op1=ALU.add),
                [c["pA"][:, 0:64], spp[:, 0:1], cum32[:]], [cum32[:]], tag=f"sc_nf{h}_{ct}")
            vop(P, lambda e: e.tensor_scalar(out=y16[:], in0=cum32[:],
                    scalar1=den32[:, 1:2], scalar2=None, op0=ALU.mult),
                [cum32[:], den32[:, 1:2]], [y16[:]], tag=f"sc_y{h}_{ct}")
            dst = big[:, ct * 1024 + h * 64: ct * 1024 + h * 64 + 64]
            vop(P, lambda e, dst=dst: e.tensor_copy(out=dst, in_=y16[:]),
                [y16[:]], [dst], tag=f"sc_yw{h}_{ct}")
            import os as _os4
            if _os4.environ.get("L3A_DBG") == "1" and h == 0 and ct == 0:
                _d1 = P.dram_out("dbg_cum", (128, 64), F32)
                _d2 = P.dram_out("dbg_num", (128, 65), BF16)
                _d3 = P.dram_out("dbg_den", (128, 1), F32)
                _d4 = P.dram_out("dbg_at0", (128, 128), BF16)
                _d5 = P.dram_out("dbg_va0", (128, 65), BF16)
                _d6 = P.dram_out("dbg_cv", (128, 64), BF16)
                _s1 = P.sbuf("l3a_dbg1", [128, 65], BF16)
                _s2 = P.sbuf("l3a_dbg2", [128, 65], BF16)
                _s3 = P.sbuf("l3a_dbg3", [128, 128], BF16)
                _s4 = P.sbuf("l3a_dbg4", [128, 64], BF16)
                vop(P, lambda e: e.tensor_copy(out=_s1[:], in_=c["pA"][:, 0:65]),
                    [c["pA"][:, 0:65]], [_s1[:]], tag="dbg_l3_1")
                vop(P, lambda e: e.tensor_copy(out=_s2[:], in_=vaug[:]),
                    [vaug[:]], [_s2[:]], tag="dbg_l3_2")
                vop(P, lambda e: e.tensor_copy(out=_s3[:], in_=at16[:]),
                    [at16[:]], [_s3[:]], tag="dbg_l3_3")
                vop(P, lambda e: e.tensor_copy(out=_s4[:], in_=c["pB"][:, 0:64]),
                    [c["pB"][:, 0:64]], [_s4[:]], tag="dbg_l3_4")
                P.dma("act", _d1, cum32[:])
                P.dma("act", _d2, _s1[:])
                P.dma("act", _d3, den32[:, 0:1])
                P.dma("act", _d4, _s3[:])
                P.dma("act", _d5, _s2[:])
                P.dma("act", _d6, _s4[:])
            tr(P, c["pT0"][:, 0:64], kbuf[:, csl], c["ident16"][0:64, 0:64],
               tag=f"sc_tr{h}_{ct}")
            vop(P, lambda e: e.tensor_copy(out=ktk16[:], in_=c["pT0"][:, 0:64]),
                [c["pT0"][:, 0:64]], [ktk16[:]], tag=f"sc_te{h}_{ct}")
            mm(P, c["pC"][0:64, 0:65], ktk16[:], vaug[:], tag=f"sc_kv{h}_{ct}")
            vop(P, lambda e: e.tensor_tensor(out=kvs32[:], in0=kvs32[:],
                    in1=c["pC"][0:64, 0:65], op=ALU.add),
                [kvs32[:], c["pC"][0:64, 0:65]], [kvs32[:]], tag=f"sc_ka{h}_{ct}")
            vop(P, lambda e: e.tensor_copy(out=kvs16[:], in_=kvs32[:]),
                [kvs32[:]], [kvs16[:]], tag=f"sc_kc{h}_{ct}")
            mm(P, c["pD"][0:1, 0:64], ones64[:].rearrange("a b -> a b") if False else ones64[:],
               vaug[0:64, 0:64], tag=f"_dummy{h}_{ct}") if False else None
            mm(P, c["pD"][0:1, 0:64], onesr[:].rearrange("a b -> b a") if False else triu[:, 0:1],
               vaug[:, 0:64], tag=f"sc_vs{h}_{ct}") if False else None
            mm(P, c["pD"][0:1, 0:64], vaug[:, 64:65], vaug[:, 0:64], tag=f"sc_vs{h}_{ct}")
            vop(P, lambda e: e.tensor_tensor(out=vs32[:], in0=vs32[:],
                    in1=c["pD"][0:1, 0:64], op=ALU.add),
                [vs32[:], c["pD"][0:1, 0:64]], [vs32[:]], tag=f"sc_va{h}_{ct}")
            vop(P, lambda e: e.tensor_copy(out=vsh[:], in_=vs32[:]), [vs32[:]], [vsh[:]],
                tag=f"sc_vh{h}_{ct}")
            vop(P, lambda e: e.tensor_sub(out=vsl[:], in0=vs32[:], in1=vsh[:]),
                [vs32[:], vsh[:]], [vsl[:]], tag=f"sc_vl{h}_{ct}")

    import os as _os3
    if _os3.environ.get("L3A_DBG") == "1":
        dy = P.dram_out("dbg_y0", (128, 1024), BF16)
        dsp = P.dram_out("dbg_spp", (128, 1), F32)
        dq = P.dram_out("dbg_q15", (64, S), BF16)
        P.dma("act", dy, big[:, 0:1024])
        P.dma("act", dsp, spp[:])
        P.dma("act", dq, qbuf[:])

    P.dma("pool", wv_, G("l3a_woT").rearrange("(ki p) o -> p ki o", p=128), tag="ld_wo")
    P.dma("pool", bobc[:], G("l3a_bobc"), tag="ld_bo")
    for tt in range(NT):
        xa = c["xtm"][tt % 2]
        ot = c["otm"][tt % 2]
        for k in range(NF):
            pb = c["pT0"] if k % 2 == 0 else c["pT1"]
            slot = (k // 2) * 128
            tr(P, pb[:, slot:slot + 128], big[:, tt * 1024 + k * 128: tt * 1024 + (k + 1) * 128],
               c["ident16"][:], tag=f"wo_tr{tt}_{k}")
            dst = atT16[:, k * 128:(k + 1) * 128]
            vop(P, lambda e, pb=pb, slot=slot, dst=dst: e.tensor_copy(out=dst, in_=pb[:, slot:slot + 128]),
                [pb[:, slot:slot + 128]], [dst])
        P.dma("sp", xa[:], xin[tt * 128:(tt + 1) * 128, :], tag=f"wo_ldx{tt}")
        for n in range(2):
            ps = c["pB"][:, n * 512:(n + 1) * 512]
            for k in range(NF):
                mm(P, ps, atT16[:, k * 128:(k + 1) * 128], wv_[:, k, n * 512:(n + 1) * 512],
                   start=(k == 0), stop=(k == NF - 1), tag=f"wo_mm{tt}_{n}")
            vop(P, lambda e, ps=ps, n=n: e.tensor_tensor(out=c["sq"][:, n * 512:(n + 1) * 512],
                    in0=ps, in1=bobc[:, n * 512:(n + 1) * 512], op=ALU.add),
                [ps, bobc[:, n * 512:(n + 1) * 512]], [c["sq"][:, n * 512:(n + 1) * 512]],
                tag=f"wo_b{tt}_{n}")
            vop(P, lambda e, xa=xa, n=n, ot=ot: e.tensor_tensor(
                    out=ot[:, n * 512:(n + 1) * 512], in0=c["sq"][:, n * 512:(n + 1) * 512],
                    in1=xa[:, n * 512:(n + 1) * 512], op=ALU.add),
                [c["sq"][:, n * 512:(n + 1) * 512], xa[:, n * 512:(n + 1) * 512]],
                [ot[:, n * 512:(n + 1) * 512]], tag=f"wo_add{tt}_{n}")
        P.dma("act", xout[tt * 128:(tt + 1) * 128, :], ot[:], tag=f"wo_st{tt}")


STAGE_FUNCS["l3_attn"] = dict(prep=lambda inputs, carry: prep_l3_attn(inputs),
                              dram=dram_l3_attn, emit=emit_l3_attn, extra=[])



CAP = 512
EPS_RMS = 1e-6


def prep_l3_ffn(inputs):
    DFFP = 2816
    w1 = np.asarray(inputs["L3_ffn_W1"], np.float32)
    w2 = np.asarray(inputs["L3_ffn_W2"], np.float32)
    w3 = np.asarray(inputs["L3_ffn_W3"], np.float32)
    w1T = np.zeros((D, DFFP), np.float32); w1T[:, :2730] = w1.T
    w2T = np.zeros((D, DFFP), np.float32); w2T[:, :2730] = w2.T
    w3T = np.zeros((DFFP, D), np.float32); w3T[:2730, :] = w3.T
    cm = np.zeros((4, 128, CAP), np.float32)
    for qt in range(4):
        i = qt * 128 + np.arange(128)[:, None]
        j = np.arange(CAP)[None, :]
        cm[qt] = (j <= i).astype(np.float32)
    iota = np.broadcast_to(np.arange(CAP, dtype=np.float32)[None, :], (128, CAP)).copy()
    ltstrip = (np.arange(1024)[None, :] < np.arange(128)[:, None] + 512).astype(np.float32)
    return {
        "l3f_wr": np.ascontiguousarray(np.asarray(inputs["L3_ffn_w_router"],
                                                  np.float32).reshape(8, 128).T),
        "l3f_wqkvT": np.ascontiguousarray(inputs["L3_ffn_W_qkv"].T),
        "l3f_woutT": np.ascontiguousarray(inputs["L3_ffn_W_out"].T),
        "l3f_w1T": w1T, "l3f_w2T": w2T, "l3f_w3T": w3T,
        "l3f_g1bc": np.ascontiguousarray(np.broadcast_to(
            np.asarray(inputs["L3_ffn_g1"], np.float32)[None, :], (128, 1024))),
        "l3f_g2bc": np.ascontiguousarray(np.broadcast_to(
            np.asarray(inputs["L3_ffn_g2"], np.float32)[None, :], (128, 1024))),
        "l3f_cmask": cm.astype(BF),
        "l3f_iota": iota,
        "l3f_lts": ltstrip.astype(BF),
        "l3f_id16": _ident16(),
    }


def dram_l3_ffn(P):
    g = {}
    g["l3f_wr"] = P.dram_in("l3f_wr", (128, 8), F32)
    g["l3f_wqkvT"] = P.dram_in("l3f_wqkvT", (D, 3 * D), F32)
    g["l3f_woutT"] = P.dram_in("l3f_woutT", (D, D), F32)
    g["l3f_w1T"] = P.dram_in("l3f_w1T", (D, 2816), F32)
    g["l3f_w2T"] = P.dram_in("l3f_w2T", (D, 2816), F32)
    g["l3f_w3T"] = P.dram_in("l3f_w3T", (2816, D), F32)
    g["l3f_g1bc"] = P.dram_in("l3f_g1bc", (128, 1024), F32)
    g["l3f_g2bc"] = P.dram_in("l3f_g2bc", (128, 1024), F32)
    g["l3f_cmask"] = P.dram_in("l3f_cmask", (4, 128, CAP), BF16)
    g["l3f_iota"] = P.dram_in("l3f_iota", (128, CAP), F32)
    g["l3f_lts"] = P.dram_in("l3f_lts", (128, 1024), BF16)
    g["l3f_id16"] = P.dram_in("l3f_id16", (128, 128), BF16)
    g["l3f_xlntm"] = P.dram_tmp("l3f_xlntm", (S, D), BF16)
    return g


def emit_l3_ffn(P, xin, xout, G):
    c = alloc_common(P, notm=1)
    ensure_ident(P, c, G, "l3f_id16")
    st = c["stats"]
    big = P.sbuf("l3f_big", [128, NF * S], BF16)
    PTall = big[:, 0:16384]
    Pall = big[:, 16384:32768]
    shrow = big[0:1, 2 * 16384 // 2:2 * 16384 // 2 + S]
    slrow = big[0:1, 16384 + S:16384 + 2 * S]
    pool = P.sbuf("l3f_pool", [128, 33792], BF16)
    qT_s = pool[:, 0:4096]
    kT_s = pool[:, 4096:8192]
    vtok = pool[:, 8192:12288]
    osel = pool[:, 12288:16384]
    wout16 = pool[:, 16384:24576].rearrange("p (ki o) -> p ki o", o=1024)
    w3v = pool[:, 0:22528].rearrange("p (j o) -> p j o", o=1024)
    hselv = pool[:, 22528:33792].rearrange("p (j t) -> p j t", t=CAP)
    xp16v = pool[:, 16384:20480]
    hT = P.sbuf("l3f_hT", [128, 4 * 1024], BF16)
    xsel32 = P.sbuf("l3f_xsel", [128, 4 * 1024], F32)
    h16 = P.sbuf("l3f_h16", [128, 1024], BF16)
    wrh = P.sbuf("l3f_wrh", [128, 8], BF16)
    wrl = P.sbuf("l3f_wrl", [128, 8], BF16)
    wr32 = P.sbuf("l3f_wr32", [128, 8], F32)
    xlo = pool[:, 1024:2048]
    xloT = pool[:, 0:1024]
    scol = P.sbuf("l3f_scol", [128, 32], F32)
    rank32 = P.sbuf("l3f_rank", [128, 32], F32)
    nscol = P.sbuf("l3f_ns", [128, 32], F32)
    onesr = P.sbuf("l3f_onr", [1, 128], BF16)
    iota32 = P.sbuf("l3f_iota_s", [128, CAP], F32)
    cmask = P.sbuf("l3f_cm", [128, 4 * CAP], BF16)
    g1bc = P.sbuf("l3f_g1_s", [128, 1024], BF16)
    g2bc = P.sbuf("l3f_g2_s", [128, 1024], BF16)
    wbufT = P.sbuf("l3f_wbT", [128, 4096], BF16)
    wbuf = [wbufT[:, 0:2048], wbufT[:, 2048:4096]]
    s16 = P.sbuf("l3f_s16", [128, CAP], BF16)
    p16 = P.sbuf("l3f_p16", [128, CAP], BF16)
    pt16 = P.sbuf("l3f_pt16", [128, CAP], BF16)
    tmp32 = P.sbuf("l3f_tmp", [128, 1024], F32)
    xp16 = xp16v
    P.dma("sp", iota32[:], G("l3f_iota"))
    lts = P.sbuf("l3f_lts_s", [128, 1024], BF16)
    P.dma("sp", lts[:], G("l3f_lts"))
    for _q in range(4):
        P.dma("sp", cmask[:, _q * CAP:(_q + 1) * CAP], G("l3f_cmask")[_q], tag=f"ld_cm{_q}")
    P.dma("pool", g1bc[:], G("l3f_g1bc"))
    P.dma("pool", g2bc[:], G("l3f_g2bc"))
    P.dma("sp", wr32[:], G("l3f_wr"))
    vop(P, lambda e: e.memset(onesr[:], 1.0), [], [onesr[:]])
    vop(P, lambda e: e.tensor_copy(out=wrh[:], in_=wr32[:]), [wr32[:]], [wrh[:]])
    vop(P, lambda e: e.tensor_sub(out=wrl[:], in0=wr32[:], in1=wrh[:]),
        [wr32[:], wrh[:]], [wrl[:]])

    xlntm = G("l3f_xlntm")
    for tt in range(NT):
        xa = c["xtm"][tt % 2]
        lnt = c["lntile"][tt % 2]
        P.dma("sp", xa[:], xin[tt * 128:(tt + 1) * 128, :], tag=f"ln_ld{tt}")
        sop(P, lambda e, xa=xa: e.activation(out=c["sq"][:], in_=xa[:], func=AF.Square,
                                             accum_out=st[:, 1:2]),
            [xa[:]], [c["sq"][:], st[:, 1:2]])
        vop(P, lambda e, xa=xa: e.reduce_sum(out=st[:, 0:1], in_=xa[:], axis=AX.X),
            [xa[:]], [st[:, 0:1]])
        vop(P, lambda e: e.tensor_scalar(out=st[:, 2:3], in0=st[:, 0:1], scalar1=1.0 / D,
                                         scalar2=None, op0=ALU.mult), [st[:, 0:1]], [st[:, 2:3]])
        vop(P, lambda e: e.tensor_scalar(out=st[:, 3:4], in0=st[:, 1:2], scalar1=1.0 / D,
                                         scalar2=None, op0=ALU.mult), [st[:, 1:2]], [st[:, 3:4]])
        vop(P, lambda e: e.tensor_tensor(out=st[:, 4:5], in0=st[:, 2:3], in1=st[:, 2:3],
                                         op=ALU.mult), [st[:, 2:3]], [st[:, 4:5]])
        vop(P, lambda e: e.tensor_sub(out=st[:, 5:6], in0=st[:, 3:4], in1=st[:, 4:5]),
            [st[:, 3:4], st[:, 4:5]], [st[:, 5:6]])
        epsap = P.constant(EPS_LN)
        sop(P, lambda e, epsap=epsap: e.activation(out=st[:, 6:7], in_=st[:, 5:6],
                                                   func=AF.Sqrt, bias=epsap),
            [st[:, 5:6], epsap], [st[:, 6:7]])
        vop(P, lambda e: e.reciprocal(out=st[:, 7:8], in_=st[:, 6:7]), [st[:, 6:7]], [st[:, 7:8]])
        vop(P, lambda e: e.tensor_tensor(out=st[:, 8:9], in0=st[:, 2:3], in1=st[:, 7:8],
                                         op=ALU.mult), [st[:, 2:3], st[:, 7:8]], [st[:, 8:9]])
        vop(P, lambda e: e.tensor_scalar(out=st[:, 9:10], in0=st[:, 8:9], scalar1=-1.0,
                                         scalar2=None, op0=ALU.mult), [st[:, 8:9]], [st[:, 9:10]])
        lnt32 = c["sq"][:]
        vop(P, lambda e, xa=xa: e.tensor_scalar(out=lnt32, in0=xa[:],
                scalar1=st[:, 7:8], scalar2=st[:, 9:10], op0=ALU.mult, op1=ALU.add),
            [xa[:], st[:, 7:8], st[:, 9:10]], [lnt32], tag=f"ln_n{tt}")
        vop(P, lambda e, lnt=lnt: e.tensor_copy(out=lnt[:], in_=lnt32), [lnt32], [lnt[:]])
        vop(P, lambda e, lnt=lnt: e.tensor_sub(out=xlo[:], in0=lnt32, in1=lnt[:]),
            [lnt32, lnt[:]], [xlo[:]])
        P.dma("act", xlntm[tt * 128:(tt + 1) * 128, :], lnt[:], tag=f"ln_tm{tt}")
        for k in range(NF):
            pb = c["pT0"] if k % 2 == 0 else c["pT1"]
            slot = (k // 2) * 128
            tr(P, pb[:, slot:slot + 128], lnt[:, k * 128:(k + 1) * 128], c["ident16"][:])
            dst = big[:, k * S + tt * 128:k * S + (tt + 1) * 128]
            vop(P, lambda e, pb=pb, slot=slot, dst=dst: e.tensor_copy(out=dst, in_=pb[:, slot:slot + 128]),
                [pb[:, slot:slot + 128]], [dst])
        for k in range(NF):
            pb = c["pT0"] if k % 2 == 0 else c["pT1"]
            slot = (k // 2) * 128
            tr(P, pb[:, slot:slot + 128], xlo[:, k * 128:(k + 1) * 128], c["ident16"][:])
            dst = xloT[:, k * 128:(k + 1) * 128]
            vop(P, lambda e, pb=pb, slot=slot, dst=dst: e.tensor_copy(out=dst, in_=pb[:, slot:slot + 128]),
                [pb[:, slot:slot + 128]], [dst])
        ps1 = c["pB"][0:1, 0:128]
        for k in range(NF):
            xhiT = big[:, k * S + tt * 128:k * S + (tt + 1) * 128]
            mm(P, ps1, wrh[:, k:k + 1], xhiT, start=(k == 0), stop=False, tag=f"sc_h{tt}_{k}")
            mm(P, ps1, wrl[:, k:k + 1], xhiT, start=False, stop=False, tag=f"sc_l{tt}_{k}")
            mm(P, ps1, wrh[:, k:k + 1], xloT[:, k * 128:(k + 1) * 128],
               start=False, stop=(k == NF - 1), tag=f"sc_x{tt}_{k}")
        dsth = shrow[:, tt * 128:(tt + 1) * 128]
        dstl = slrow[:, tt * 128:(tt + 1) * 128]
        vop(P, lambda e, ps1=ps1, dsth=dsth: e.tensor_copy(out=dsth, in_=ps1),
            [ps1], [dsth], tag=f"sc_evh{tt}")
        vop(P, lambda e, ps1=ps1, dsth=dsth, dstl=dstl: e.tensor_tensor(
                out=dstl, in0=ps1, in1=dsth, op=ALU.subtract),
            [ps1, dsth], [dstl], tag=f"sc_evl{tt}")

    for tt in range(NT):
        tr(P, c["pT0"][0:128, 0:1], shrow[:, tt * 128:(tt + 1) * 128],
           c["ident16"][0:1, 0:1], tag=f"rk_t1{tt}")
        vop(P, lambda e, tt=tt: e.tensor_copy(out=scol[:, tt:tt + 1], in_=c["pT0"][0:128, 0:1]),
            [c["pT0"][0:128, 0:1]], [scol[:, tt:tt + 1]], tag=f"rk_e1{tt}")
        tr(P, c["pT1"][0:128, 0:1], slrow[:, tt * 128:(tt + 1) * 128],
           c["ident16"][0:1, 0:1], tag=f"rk_t2{tt}")
        vop(P, lambda e, tt=tt: e.tensor_copy(out=tmp32[:, 0:1], in_=c["pT1"][0:128, 0:1]),
            [c["pT1"][0:128, 0:1]], [tmp32[:, 0:1]], tag=f"rk_e2{tt}")
        vop(P, lambda e, tt=tt: e.tensor_tensor(out=scol[:, tt:tt + 1], in0=scol[:, tt:tt + 1],
                in1=tmp32[:, 0:1], op=ALU.add),
            [scol[:, tt:tt + 1], tmp32[:, 0:1]], [scol[:, tt:tt + 1]], tag=f"rk_a{tt}")
    vop(P, lambda e: e.memset(rank32[:], 0.0), [], [rank32[:]], tag="rk_z")
    for bb in range(8):
        psb = c["pA"][:, 0:512]
        mm(P, psb, onesr[:], shrow[:, bb * 512:(bb + 1) * 512], start=True, stop=False,
           tag=f"rk_bh{bb}")
        mm(P, psb, onesr[:], slrow[:, bb * 512:(bb + 1) * 512], start=False, stop=True,
           tag=f"rk_bl{bb}")
        for tt in range(NT):
            vop(P, lambda e, psb=psb, tt=tt: e.tensor_scalar(out=c["sq"][:, 0:512], in0=psb,
                    scalar1=scol[:, tt:tt + 1], scalar2=None, op0=ALU.is_gt, op1=ALU.add,
                    accum_out=st[:, 12:13]),
                [psb, scol[:, tt:tt + 1]], [c["sq"][:, 0:512], st[:, 12:13]],
                tag=f"rk_c{bb}_{tt}")
            vop(P, lambda e, tt=tt: e.tensor_tensor(out=rank32[:, tt:tt + 1],
                    in0=rank32[:, tt:tt + 1], in1=st[:, 12:13], op=ALU.add),
                [rank32[:, tt:tt + 1], st[:, 12:13]], [rank32[:, tt:tt + 1]],
                tag=f"rk_u{bb}_{tt}")
            if bb < tt // 4:
                vop(P, lambda e, psb=psb, tt=tt: e.tensor_scalar(out=c["sq"][:, 0:512],
                        in0=psb, scalar1=scol[:, tt:tt + 1], scalar2=None, op0=ALU.is_equal,
                        op1=ALU.add, accum_out=st[:, 12:13]),
                    [psb, scol[:, tt:tt + 1]], [c["sq"][:, 0:512], st[:, 12:13]],
                    tag=f"rk_q{bb}_{tt}")
                vop(P, lambda e, tt=tt: e.tensor_tensor(out=rank32[:, tt:tt + 1],
                        in0=rank32[:, tt:tt + 1], in1=st[:, 12:13], op=ALU.add),
                    [rank32[:, tt:tt + 1], st[:, 12:13]], [rank32[:, tt:tt + 1]],
                    tag=f"rk_qa{bb}_{tt}")
            elif bb == tt // 4:
                j0 = tt % 4
                off = 512 - 128 * j0
                vop(P, lambda e, psb=psb, tt=tt: e.tensor_scalar(out=c["sq"][:, 0:512],
                        in0=psb, scalar1=scol[:, tt:tt + 1], scalar2=None, op0=ALU.is_equal),
                    [psb, scol[:, tt:tt + 1]], [c["sq"][:, 0:512]], tag=f"rk_q{bb}_{tt}")
                vop(P, lambda e, off=off: e.tensor_tensor(out=c["sq"][:, 0:512],
                        in0=c["sq"][:, 0:512], in1=lts[:, off:off + 512], op=ALU.mult),
                    [c["sq"][:, 0:512], lts[:, off:off + 512]], [c["sq"][:, 0:512]],
                    tag=f"rk_qm{bb}_{tt}")
                vop(P, lambda e: e.reduce_sum(out=st[:, 12:13], in_=c["sq"][:, 0:512],
                                              axis=AX.X),
                    [c["sq"][:, 0:512]], [st[:, 12:13]], tag=f"rk_qs{bb}_{tt}")
                vop(P, lambda e, tt=tt: e.tensor_tensor(out=rank32[:, tt:tt + 1],
                        in0=rank32[:, tt:tt + 1], in1=st[:, 12:13], op=ALU.add),
                    [rank32[:, tt:tt + 1], st[:, 12:13]], [rank32[:, tt:tt + 1]],
                    tag=f"rk_qa{bb}_{tt}")
    for tt in range(NT):
        dst = PTall[:, tt * 512:(tt + 1) * 512]
        vop(P, lambda e, dst=dst, tt=tt: e.tensor_single_scalar(out=dst, in_=iota32[:],
                scalar=rank32[:, tt:tt + 1], op=ALU.is_equal),
            [iota32[:], rank32[:, tt:tt + 1]], [dst], tag=f"pt_{tt}")
        vop(P, lambda e, tt=tt: e.tensor_single_scalar(out=nscol[:, tt:tt + 1],
                in_=rank32[:, tt:tt + 1], scalar=float(CAP), op=ALU.is_ge),
            [rank32[:, tt:tt + 1]], [nscol[:, tt:tt + 1]], tag=f"ns_{tt}")
    for tt in range(NT):
        for pc in range(4):
            pb = c["pT0"] if pc % 2 == 0 else c["pT1"]
            slot = (pc // 2) * 128
            tr(P, pb[:, slot:slot + 128], PTall[:, tt * 512 + pc * 128: tt * 512 + (pc + 1) * 128],
               c["ident16"][:], tag=f"pp_{tt}_{pc}")
            dst = Pall[:, pc * S + tt * 128:pc * S + (tt + 1) * 128]
            vop(P, lambda e, pb=pb, slot=slot, dst=dst: e.tensor_copy(out=dst, in_=pb[:, slot:slot + 128]),
                [pb[:, slot:slot + 128]], [dst], tag=f"ppe_{tt}_{pc}")

    for half in range(2):
        groups = [(ps_, n_) for ps_ in range(2) for n_ in range(2)]
        for tt in range(NT):
            lnt = c["lntile"][tt % 2]
            P.dma("sp", lnt[:], xlntm[tt * 128:(tt + 1) * 128, :], tag=f"ga_ld{half}_{tt}")
            for gi, (ps_, n_) in enumerate(groups):
                psel = half * 2 + ps_
                psum = (c["pA"] if gi < 2 else c["pB"])[:, (gi % 2) * 512:(gi % 2) * 512 + 512]
                mm(P, psum, PTall[:, tt * 512 + psel * 128: tt * 512 + (psel + 1) * 128],
                   lnt[:, n_ * 512:(n_ + 1) * 512],
                   start=(tt == 0), stop=(tt == NT - 1), tag=f"ga_{half}_{tt}_{gi}")
        for gi, (ps_, n_) in enumerate(groups):
            psel = half * 2 + ps_
            psum = (c["pA"] if gi < 2 else c["pB"])[:, (gi % 2) * 512:(gi % 2) * 512 + 512]
            dst = xsel32[:, psel * 1024 + n_ * 512: psel * 1024 + (n_ + 1) * 512]
            vop(P, lambda e, psum=psum, dst=dst: e.tensor_copy(out=dst, in_=psum),
                [psum], [dst], tag=f"ga_ev{half}_{gi}")

    def rmsnorm_to_hT(P, src32, gbc, tag):
        for sl4 in range(4):
            xa = src32[:, sl4 * 1024:(sl4 + 1) * 1024]
            sop(P, lambda e, xa=xa: e.activation(out=tmp32[:], in_=xa, func=AF.Square,
                                                 accum_out=st[:, 1:2]),
                [xa], [tmp32[:], st[:, 1:2]], tag=f"{tag}sq{sl4}")
            vop(P, lambda e: e.tensor_scalar(out=st[:, 2:3], in0=st[:, 1:2], scalar1=1.0 / D,
                                             scalar2=None, op0=ALU.mult),
                [st[:, 1:2]], [st[:, 2:3]])
            epsap = P.constant(EPS_RMS)
            sop(P, lambda e, epsap=epsap: e.activation(out=st[:, 3:4], in_=st[:, 2:3],
                                                       func=AF.Sqrt, bias=epsap),
                [st[:, 2:3], epsap], [st[:, 3:4]])
            vop(P, lambda e: e.reciprocal(out=st[:, 4:5], in_=st[:, 3:4]),
                [st[:, 3:4]], [st[:, 4:5]])
            vop(P, lambda e, xa=xa: e.tensor_scalar(out=tmp32[:], in0=xa,
                    scalar1=st[:, 4:5], scalar2=None, op0=ALU.mult),
                [xa, st[:, 4:5]], [tmp32[:]], tag=f"{tag}n{sl4}")
            vop(P, lambda e, gbc=gbc: e.tensor_tensor(out=h16[:], in0=tmp32[:], in1=gbc[:],
                                                      op=ALU.mult),
                [tmp32[:], gbc[:]], [h16[:]], tag=f"{tag}g{sl4}")
            for k in range(NF):
                pb = c["pT0"] if k % 2 == 0 else c["pT1"]
                slot = (k // 2) * 128
                tr(P, pb[:, slot:slot + 128], h16[:, k * 128:(k + 1) * 128], c["ident16"][:],
                   tag=f"{tag}tr{sl4}_{k}")
                dst = hT[:, k * 512 + sl4 * 128: k * 512 + (sl4 + 1) * 128]
                vop(P, lambda e, pb=pb, slot=slot, dst=dst: e.tensor_copy(out=dst, in_=pb[:, slot:slot + 128]),
                    [pb[:, slot:slot + 128]], [dst], tag=f"{tag}te{sl4}_{k}")

    rmsnorm_to_hT(P, xsel32[:], g1bc, "r1_")
    for qk, dstT in ((0, qT_s), (1, kT_s)):
        for oc in range(NF):
            wb = wbuf[oc % 2].rearrange("p (ki o) -> p ki o", o=256)
            P.dma("pool", wb[:, :, 0:128],
                  G("l3f_wqkvT")[:, qk * 1024 + oc * 128: qk * 1024 + (oc + 1) * 128].rearrange(
                      "(ki p) o -> p ki o", p=128), tag=f"at_w{qk}_{oc}")
            ps = c["pC"][:, 0:512]
            for ki in range(NF):
                mm(P, ps, wb[:, ki, 0:128], hT[:, ki * 512:(ki + 1) * 512],
                   start=(ki == 0), stop=(ki == NF - 1), tag=f"at_qk{qk}_{oc}")
            dst = dstT[:, oc * 512:(oc + 1) * 512]
            vop(P, lambda e, ps=ps, dst=dst: e.tensor_copy(out=dst, in_=ps),
                [ps], [dst], tag=f"at_ev{qk}_{oc}")
    wv512 = wbufT
    for n in range(2):
        P.dma("pool", wv512[:].rearrange("p (ki o) -> p ki o", o=512) if hasattr(wv512, "ap") and False else wv512[:].rearrange("p (ki o) -> p ki o", o=512),
              G("l3f_wqkvT")[:, 2048 + n * 512: 2048 + (n + 1) * 512].rearrange(
                  "(ki p) o -> p ki o", p=128), tag=f"at_wv{n}")
        wvv = wv512[:].rearrange("p (ki o) -> p ki o", o=512)
        for st4 in range(4):
            ps = c["pC"][:, 0:512]
            for ki in range(NF):
                mm(P, ps, hT[:, ki * 512 + st4 * 128: ki * 512 + (st4 + 1) * 128],
                   wvv[:, ki, :], start=(ki == 0), stop=(ki == NF - 1),
                   tag=f"at_v{n}_{st4}")
            dst = vtok[:, st4 * 1024 + n * 512: st4 * 1024 + (n + 1) * 512]
            vop(P, lambda e, ps=ps, dst=dst: e.tensor_copy(out=dst, in_=ps),
                [ps], [dst], tag=f"at_vev{n}_{st4}")
    for h in range(16):
        b0 = (h % 2) * 64
        ck = (h // 2) * 512
        for qt in range(4):
            qh = qT_s[b0:b0 + 64, ck + qt * 128: ck + (qt + 1) * 128]
            kh = kT_s[b0:b0 + 64, ck: ck + 512]
            ps = c["pD"][:, 0:512]
            mm(P, ps, qh, kh, tag=f"la_s{h}_{qt}")
            vop(P, lambda e, ps=ps: e.reduce_max(out=st[:, 10:11], in_=ps, axis=AX.X),
                [ps], [st[:, 10:11]], tag=f"la_mx{h}_{qt}")
            vop(P, lambda e: e.tensor_scalar(out=st[:, 11:12], in0=st[:, 10:11],
                    scalar1=-0.125, scalar2=None, op0=ALU.mult),
                [st[:, 10:11]], [st[:, 11:12]], tag=f"la_nm{h}_{qt}")
            sop(P, lambda e, ps=ps: e.activation(out=p16[:], in_=ps, func=AF.Exp,
                    scale=0.125, bias=st[:, 11:12]),
                [ps, st[:, 11:12]], [p16[:]], tag=f"la_e{h}_{qt}")
            vop(P, lambda e, qt=qt: e.tensor_tensor(out=p16[:], in0=p16[:],
                    in1=cmask[:, qt * CAP:(qt + 1) * CAP], op=ALU.mult),
                [p16[:], cmask[:, qt * CAP:(qt + 1) * CAP]], [p16[:]], tag=f"la_m{h}_{qt}")
            vop(P, lambda e: e.reduce_sum(out=st[:, 12:13], in_=p16[:], axis=AX.X),
                [p16[:]], [st[:, 12:13]], tag=f"la_d{h}_{qt}")
            vop(P, lambda e: e.reciprocal(out=st[:, 13:14], in_=st[:, 12:13]),
                [st[:, 12:13]], [st[:, 13:14]], tag=f"la_r{h}_{qt}")
            for ks in range(qt + 1):
                pb = c["pT0"] if ks % 2 == 0 else c["pT1"]
                slot = 0
                tr(P, pb[:, slot:slot + 128], p16[:, ks * 128:(ks + 1) * 128],
                   c["ident16"][:], tag=f"la_t{h}_{qt}_{ks}")
                dst = pt16[:, ks * 128:(ks + 1) * 128]
                vop(P, lambda e, pb=pb, slot=slot, dst=dst: e.tensor_copy(out=dst, in_=pb[:, slot:slot + 128]),
                    [pb[:, slot:slot + 128]], [dst], tag=f"la_pe{h}_{qt}_{ks}")
            for ks in range(qt + 1):
                mm(P, c["pC"][:, 0:64], pt16[:, ks * 128:(ks + 1) * 128],
                   vtok[:, ks * 1024 + h * 64: ks * 1024 + h * 64 + 64],
                   start=(ks == 0), stop=(ks == qt), tag=f"la_pv{h}_{qt}_{ks}")
            dst = osel[:, qt * 1024 + h * 64: qt * 1024 + h * 64 + 64]
            vop(P, lambda e, dst=dst: e.tensor_scalar(out=dst, in0=c["pC"][:, 0:64],
                    scalar1=st[:, 13:14], scalar2=None, op0=ALU.mult),
                [c["pC"][:, 0:64], st[:, 13:14]], [dst], tag=f"la_y{h}_{qt}")
    P.dma("pool", wout16, G("l3f_woutT").rearrange("(ki p) o -> p ki o", p=128),
          tag="wo_ld")
    for st4 in range(4):
        for k in range(NF):
            pb = c["pT0"] if k % 2 == 0 else c["pT1"]
            slot = (k // 2) * 128
            tr(P, pb[:, slot:slot + 128],
               osel[:, st4 * 1024 + k * 128: st4 * 1024 + (k + 1) * 128],
               c["ident16"][:], tag=f"wo_tr{st4}_{k}")
            dst = hT[:, k * 512 + st4 * 128: k * 512 + (st4 + 1) * 128]
            vop(P, lambda e, pb=pb, slot=slot, dst=dst: e.tensor_copy(out=dst, in_=pb[:, slot:slot + 128]),
                [pb[:, slot:slot + 128]], [dst], tag=f"wo_te{st4}_{k}")
    for st4 in range(4):
        for n in range(2):
            ps = c["pC"][:, 0:512]
            for k in range(NF):
                mm(P, ps, hT[:, k * 512 + st4 * 128: k * 512 + (st4 + 1) * 128],
                   wout16[:, k, n * 512:(n + 1) * 512],
                   start=(k == 0), stop=(k == NF - 1), tag=f"wo_mm{st4}_{n}")
            dst = xsel32[:, st4 * 1024 + n * 512: st4 * 1024 + (n + 1) * 512]
            vop(P, lambda e, ps=ps, dst=dst: e.tensor_tensor(out=dst, in0=ps, in1=dst,
                                                             op=ALU.add),
                [ps, dst], [dst], tag=f"wo_a{st4}_{n}")
    rmsnorm_to_hT(P, xsel32[:], g2bc, "r2_")
    for j in range(22):
        wb = wbuf[j % 2].rearrange("p (ki o) -> p ki o", o=256)
        P.dma("pool", wb[:, :, 0:128],
              G("l3f_w1T")[:, j * 128:(j + 1) * 128].rearrange("(ki p) o -> p ki o", p=128),
              tag=f"ff_w1{j}")
        P.dma("pool", wb[:, :, 128:256],
              G("l3f_w2T")[:, j * 128:(j + 1) * 128].rearrange("(ki p) o -> p ki o", p=128),
              tag=f"ff_w2{j}")
        ps1 = c["pC"][:, 0:512]
        ps2 = c["pD"][:, 0:512]
        for ki in range(NF):
            mm(P, ps1, wb[:, ki, 0:128], hT[:, ki * 512:(ki + 1) * 512],
               start=(ki == 0), stop=(ki == NF - 1), tag=f"ff_u1{j}")
        for ki in range(NF):
            mm(P, ps2, wb[:, ki, 128:256], hT[:, ki * 512:(ki + 1) * 512],
               start=(ki == 0), stop=(ki == NF - 1), tag=f"ff_u2{j}")
        sop(P, lambda e, ps1=ps1: e.activation(out=s16[:], in_=ps1, func=AF.Silu),
            [ps1], [s16[:]], tag=f"ff_si{j}")
        dst = hselv[:, j, :]
        vop(P, lambda e, ps2=ps2, dst=dst: e.tensor_tensor(out=dst, in0=ps2, in1=s16[:],
                                                           op=ALU.mult),
            [ps2, s16[:]], [dst], tag=f"ff_hh{j}")
    P.dma("pool", w3v, G("l3f_w3T").rearrange("(j p) o -> p j o", p=128), tag="ff_w3")
    for st4 in range(4):
        for n in range(2):
            ps = c["pC"][:, 0:512]
            for j in range(22):
                mm(P, ps, hselv[:, j, st4 * 128:(st4 + 1) * 128],
                   w3v[:, j, n * 512:(n + 1) * 512],
                   start=(j == 0), stop=(j == 21), tag=f"ff_o{st4}_{n}")
            dst = xsel32[:, st4 * 1024 + n * 512: st4 * 1024 + (n + 1) * 512]
            vop(P, lambda e, ps=ps, dst=dst: e.tensor_tensor(out=dst, in0=ps, in1=dst,
                                                             op=ALU.add),
                [ps, dst], [dst], tag=f"ff_a{st4}_{n}")
    for st4 in range(4):
        src = xsel32[:, st4 * 1024:(st4 + 1) * 1024]
        dst = xp16[:, st4 * 1024:(st4 + 1) * 1024]
        vop(P, lambda e, src=src, dst=dst: e.tensor_copy(out=dst, in_=src),
            [src], [dst], tag=f"xp_{st4}")

    for tt in range(NT):
        xa = c["xtm"][tt % 2]
        lnt = c["lntile"][tt % 2]
        P.dma("sp", xa[:], xin[tt * 128:(tt + 1) * 128, :], tag=f"fo_ldx{tt}")
        P.dma("sp", lnt[:], xlntm[tt * 128:(tt + 1) * 128, :], tag=f"fo_ldl{tt}")
        ot = c["otm"][0]
        for n in range(2):
            ps = c["pB"][:, n * 512:(n + 1) * 512]
            for pc in range(4):
                mm(P, ps, Pall[:, pc * S + tt * 128:pc * S + (tt + 1) * 128],
                   xp16[:, pc * 1024 + n * 512: pc * 1024 + (n + 1) * 512],
                   start=(pc == 0), stop=(pc == 3), tag=f"fo_mm{tt}_{n}")
            vop(P, lambda e, lnt=lnt, n=n, tt=tt: e.tensor_scalar(
                    out=tmp32[:, n * 512:(n + 1) * 512], in0=lnt[:, n * 512:(n + 1) * 512],
                    scalar1=nscol[:, tt:tt + 1], scalar2=None, op0=ALU.mult),
                [lnt[:, n * 512:(n + 1) * 512], nscol[:, tt:tt + 1]],
                [tmp32[:, n * 512:(n + 1) * 512]], tag=f"fo_ns{tt}_{n}")
            vop(P, lambda e, xa=xa, n=n: e.tensor_tensor(out=tmp32[:, n * 512:(n + 1) * 512],
                    in0=tmp32[:, n * 512:(n + 1) * 512], in1=xa[:, n * 512:(n + 1) * 512],
                    op=ALU.add),
                [tmp32[:, n * 512:(n + 1) * 512], xa[:, n * 512:(n + 1) * 512]],
                [tmp32[:, n * 512:(n + 1) * 512]], tag=f"fo_x{tt}_{n}")
            vop(P, lambda e, ps=ps, n=n, ot=ot: e.tensor_tensor(
                    out=ot[:, n * 512:(n + 1) * 512], in0=ps,
                    in1=tmp32[:, n * 512:(n + 1) * 512], op=ALU.add),
                [ps, tmp32[:, n * 512:(n + 1) * 512]], [ot[:, n * 512:(n + 1) * 512]],
                tag=f"fo_s{tt}_{n}")
        P.dma("act", xout[tt * 128:(tt + 1) * 128, :], ot[:], tag=f"fo_st{tt}")


STAGE_FUNCS["l3_ffn"] = dict(prep=lambda inputs, carry: prep_l3_ffn(inputs),
                             dram=dram_l3_ffn, emit=emit_l3_ffn, extra=[])
